# version 16
# baseline (speedup 1.0000x reference)
import os
import numpy as np
import ml_dtypes
import concourse.bass as bass
import concourse.mybir as mybir
from concourse.bass_utils import run_bass_kernel_spmd
from contextlib import ExitStack

F32 = mybir.dt.float32
BF16 = mybir.dt.bfloat16
I32 = mybir.dt.int32
AF = mybir.ActivationFunctionType
ALU = mybir.AluOpType
AX = mybir.AxisListType

NCORE = 8
SEQ = 2048
NSEQ = 2
NTOK = SEQ * NSEQ
D = 1024
DH = 4096
NH, HD = 16, 64
TOK = 256
NT = NTOK // TOK
TPS = SEQ // TOK
C_SHIFT = 12.0
RMAGIC = 12582912.0
EPS = 1e-5

ENGINES = ("sync", "vector", "scalar", "tensor", "gpsimd")
DEV_PHASES = os.environ.get("ATH_PHASES", "full")
CUT = int(os.environ.get("ATH_CUT", "9"))


class Sem:
    __slots__ = ("h", "v", "name")
    def __init__(self, h, name):
        self.h, self.v, self.name = h, 0, name


class Prog:
    def __init__(self):
        self.q = {e: [] for e in ENGINES}

    def op(self, eng, fn, waits=(), inc=None, incv=1):
        ws = tuple((s, v) for (s, v) in waits if v > 0)
        self.q[eng].append((fn, ws, inc, incv))
        if inc is not None:
            inc.v += incv
            return inc.v
        return None


def build_nc():
    nc = bass.Bass("TRN2", target_bir_lowering=False)

    x0 = nc.dram_tensor("x", [NTOK, D], F32, kind="ExternalInput").ap()
    w1t = nc.dram_tensor("w1t", [D, DH], F32, kind="ExternalInput").ap()
    w2t = nc.dram_tensor("w2t", [DH, D], F32, kind="ExternalInput").ap()
    cwd = nc.dram_tensor("cwdrm", [128, 128], F32, kind="ExternalInput").ap()
    wqd = nc.dram_tensor("wq", [D, D], F32, kind="ExternalInput").ap()
    wkd = nc.dram_tensor("wk", [D, D], F32, kind="ExternalInput").ap()
    wvd = nc.dram_tensor("wv", [D, D], F32, kind="ExternalInput").ap()
    wod = nc.dram_tensor("wo", [D, D], F32, kind="ExternalInput").ap()
    bqd = nc.dram_tensor("bq", [1, D], F32, kind="ExternalInput").ap()
    bvd = nc.dram_tensor("bv", [1, D], F32, kind="ExternalInput").ap()
    bod = nc.dram_tensor("bo", [1, D], F32, kind="ExternalInput").ap()
    cosd = nc.dram_tensor("cosd", [128, 1024], F32, kind="ExternalInput").ap()
    sind = nc.dram_tensor("sind", [128, 1024], F32, kind="ExternalInput").ap()
    
    identf_d = nc.dram_tensor("identfd", [128, 128], F32, kind="ExternalInput").ap()
    
    onesf_d = nc.dram_tensor("onesfd", [128, 128], F32, kind="ExternalInput").ap()
    e16_d = nc.dram_tensor("e16d", [1, 256], F32, kind="ExternalInput").ap()
    sel_d = nc.dram_tensor("seld", [16, 1024], F32, kind="ExternalInput").ap()
    consts_d = nc.dram_tensor("constsd", [128, 8], F32, kind="ExternalInput").ap()
    out_d = nc.dram_tensor("out", [NTOK, D], F32, kind="ExternalOutput").ap()
    dbg1 = dbg2 = dbg3 = dbg4 = None
    if os.environ.get("ATH_DEBUG"):
        dbg1 = nc.dram_tensor("dbg1", [128, 160], F32, kind="ExternalOutput").ap()
        dbg2 = nc.dram_tensor("dbg2", [128, 8256], F32, kind="ExternalOutput").ap()
        dbg3 = nc.dram_tensor("dbg3", [128, 6192], BF16, kind="ExternalOutput").ap()
        dbg4 = nc.dram_tensor("dbg4", [128, 360], F32, kind="ExternalOutput").ap()
    x1d = nc.dram_tensor("x1buf", [NTOK, D], F32, kind="Internal").ap()
    x2d = nc.dram_tensor("x2buf", [NTOK, D], F32, kind="Internal").ap()
    w1qs = nc.dram_tensor("w1qs", [D, DH], BF16, kind="Internal").ap()
    w2qs = nc.dram_tensor("w2qs", [DH, D], BF16, kind="Internal").ap()

    stack = ExitStack()
    sb = lambda n, s, d: stack.enter_context(nc.sbuf_tensor(n, s, d))

    AR = sb("arena", [128, 45120], F32)
    arb = AR[:].bitcast(BF16)

    def arf(off, n):
        return AR[:, off:off + n]
    def arh(off_f32, n_bf16):
        return arb[:, 2 * off_f32:2 * off_f32 + n_bf16]

    w1q = arh(0, 32768)
    w2q = arh(16384, 32768)
    hbuf = arf(32768, 8256)
    hbuf_i = hbuf.bitcast(I32)
    xio = [arf(41024, 2048), arf(43072, 2048)]

    stg = sb("stg", [128, 1024], F32)
    stg_iv = stg[:].bitcast(I32)
    blk = sb("blk", [128, 256], BF16)
    blkint = sb("blkint", [128, 256], I32)
    sqint = sb("sqint", [128, 256], I32)
    yqT = sb("yqT", [128, 6192], BF16)
    r1row = sb("r1row", [1, 774], F32)
    cvb_t = sb("cvb", [128, 512], F32)
    sqbf = cvb_t[:].bitcast(BF16)
    r1bc = sb("r1bc", [128, 258], F32)
    r2bc = r1bc[:, 0:256]
    r2row = sb("r2row", [1, 256], F32)
    accd = sb("accd", [128, 256], F32)
    accp = sb("accp", [128, 256], F32)
    ssum = sb("ssum", [128, 32], F32)
    ssq = sb("ssq", [128, 32], F32)
    statm = sb("statm", [128, 32], F32)
    guard1 = sb("guard1", [128, 64], F32)
    statr = sb("statr", [128, 32], F32)
    guard2 = sb("guard2", [128, 64], F32)
    tmp = sb("tmp", [128, 16], F32)
    wsc = sb("wsc", [128, 4], F32)
    sc11 = sb("sc11", [1, 4], F32)
    consts = sb("consts", [128, 8], F32)
    cw = sb("cwsb", [128, 128], F32)
    identbf = sb("identbf", [128, 128], BF16)
    identf = sb("identf", [128, 128], F32)
    onesf = sb("onesf", [128, 128], F32)

    CL_CLIP, CL_R127, CL_NSH, CL_EPS, CL_HALF = 0, 1, 2, 3, 4

    P = stack.enter_context(nc.psum_tensor("P", [128, 4096], F32))
    ph = [P[:, 0:258], P[:, 512:770]]
    po = [P[:, 1024:2048], P[:, 2048:3072]]
    ptrb = P[:, 3072:3584].bitcast(BF16)
    pbcf = P[:, 3584:4096]

    def mksem(name):
        return Sem(stack.enter_context(nc.semaphore(name)), name)
    dx2 = [mksem("dxa"), mksem("dxb")]
    dx3 = [mksem("dxc"), mksem("dxd")]
    dsq = mksem("dsq")
    dg3 = [mksem("dge"), mksem("dgf")]
    dw = mksem("dw")
    dw2 = [mksem("dwa"), mksem("dwb")]
    dg2 = [mksem("dga"), mksem("dgb")]
    dsp = mksem("dsp")
    dgd = mksem("dgd")
    sv = mksem("sv")
    sa = mksem("sa")
    sp = mksem("sp")
    st_ = mksem("st")

    pr = Prog()

    def V(fn, waits=(), ms=False):
        return pr.op("vector", fn, tuple(waits) + ((sv, sv.v),), sv)
    def A_(fn, waits=(), ms=False):
        return pr.op("scalar", fn, tuple(waits) + ((sa, sa.v),), sa)
    def T(fn, waits=(), ms=False):
        return pr.op("tensor", fn, waits, st_ if ms else None)
    def G(fn, waits=(), ms=False):
        return pr.op("gpsimd", fn, tuple(waits) + ((sp, sp.v),), sp)
    def LD(fn, waits=(), sem=None):
        sem = sem or dw
        return pr.op("sync", fn, waits, sem, incv=16)
    def GD(fn, waits=(), sem=None):
        return pr.op("gpsimd", fn, waits, sem or dgd, incv=16)

    LD(lambda e: e.dma_start(out=consts[:], in_=consts_d))
    LD(lambda e: e.dma_start(out=cw[:], in_=cwd))
    idl = LD(lambda e: e.dma_start(out=identf[:], in_=identf_d))
    cl = LD(lambda e: e.dma_start(out=onesf[:], in_=onesf_d))
    V(lambda e: e.tensor_copy(out=identbf[:], in_=identf[:]), waits=[(dw, cl)])
    V(lambda e: e.memset(guard1[:], 0.0))
    V(lambda e: e.memset(guard2[:], 0.0))

    x0v = x0.rearrange("(s p) d -> p s d", p=128)
    stat_v = {}
    stat_a = {}
    ld_ms = {}
    for s_ in range(32):
        b = s_ % 2
        w_ = []
        if s_ >= 2:
            w_ = [(sv, stat_v[s_ - 2]), (sa, stat_a[s_ - 2])]
        ld_ms[s_] = LD(lambda e, s_=s_, b=b: e.dma_start(out=xio[b][:, 0:1024], in_=x0v[:, s_, :]),
                       waits=w_, sem=dx2[b])
        stat_v[s_] = V(lambda e, s_=s_, b=b: e.reduce_sum(out=ssum[:, s_:s_ + 1], in_=xio[b][:, 0:1024], axis=AX.X),
                       waits=[(dx2[b], ld_ms[s_])], ms=True)
        wsq = [(dx2[b], ld_ms[s_])]
        if s_ >= 1:
            wsq.append((sa, stat_a[s_ - 1]))
        stat_a[s_] = A_(lambda e, s_=s_, b=b: e.activation(out=stg[:, 0:1024], in_=xio[b][:, 0:1024], func=AF.Square, accum_out=ssq[:, s_:s_ + 1]),
                        waits=wsq, ms=True)
    stats_v, stats_a = stat_v[31], stat_a[31]

    wload = [hbuf[:, 0:4096], hbuf[:, 4096:8192]]
    wload_i = [hbuf_i[:, 0:4096], hbuf_i[:, 4096:8192]]
    w1tv = w1t.rearrange("(c p) h -> c p h", p=128)
    w2tv = w2t.rearrange("(c p) h -> c p h", p=128)

    red_ms = {}
    for j in range(8):
        b = j % 2
        w_ = [] if j < 2 else [(sv, red_ms[j - 2])]
        ld = LD(lambda e, j=j, b=b: e.dma_start(out=wload[b], in_=w1tv[j, :, :]), waits=w_, sem=dw2[b])
        red_ms[j] = V(lambda e, j=j, b=b: e.reduce_sum(out=tmp[:, j:j + 1], in_=wload[b], axis=AX.X, apply_absolute_value=True),
                      waits=[(dw2[b], ld)], ms=True)
    V(lambda e: e.reduce_sum(out=tmp[:, 8:9], in_=tmp[:, 0:8], axis=AX.X), waits=[(sv, red_ms[7])])
    for j in range(32):
        b = j % 2
        w_ = [(sv, red_ms[j - 2])] if j >= 2 else [(sv, red_ms[7] if b else red_ms[6])]
        ld = LD(lambda e, j=j, b=b: e.dma_start(out=wload[b][:, 0:1024], in_=w2tv[j, :, :]), waits=w_, sem=dw2[b])
        acc = 12 + b
        if j < 2:
            red_ms[j] = V(lambda e, b=b, acc=acc: e.reduce_sum(out=tmp[:, acc:acc + 1], in_=wload[b][:, 0:1024], axis=AX.X, apply_absolute_value=True),
                          waits=[(dw2[b], ld)], ms=True)
        else:
            rr = V(lambda e, b=b: e.reduce_sum(out=tmp[:, 10 + b:11 + b], in_=wload[b][:, 0:1024], axis=AX.X, apply_absolute_value=True),
                   waits=[(dw2[b], ld)], ms=True)
            red_ms[j] = V(lambda e, b=b, acc=acc: e.tensor_tensor(out=tmp[:, acc:acc + 1], in0=tmp[:, acc:acc + 1], in1=tmp[:, 10 + b:11 + b], op=ALU.add),
                          waits=[(sv, rr)], ms=True)
    vsum2 = V(lambda e: e.tensor_tensor(out=tmp[:, 9:10], in0=tmp[:, 12:13], in1=tmp[:, 13:14], op=ALU.add),
              waits=[(sv, red_ms[31]), (sv, red_ms[30])], ms=True)

    tps = T(lambda e: e.matmul(out=pbcf[0:1, 0:1], lhsT=tmp[:, 8:9], rhs=onesf[:, 0:1], start=True, stop=True, skip_group_check=True),
            waits=[(sv, vsum2), (dw, cl)])
    tps = T(lambda e: e.matmul(out=pbcf[0:1, 1:2], lhsT=tmp[:, 9:10], rhs=onesf[:, 0:1], start=True, stop=True, skip_group_check=True), ms=True)
    vm = V(lambda e: e.tensor_scalar(out=sc11[0:1, 0:2], in0=pbcf[0:1, 0:2], scalar1=1.0 / (DH * D), scalar2=1e-5, op0=ALU.mult, op1=ALU.max),
           waits=[(st_, tps)], ms=True)
    tbc = T(lambda e: e.matmul(out=pbcf[:, 2:4], lhsT=onesf[0:1, :], rhs=sc11[0:1, 0:2], start=True, stop=True, skip_group_check=True),
            waits=[(sv, vm)], ms=True)
    V(lambda e: e.tensor_copy(out=wsc[:, 0:1], in_=pbcf[:, 2:3]), waits=[(st_, tbc)])
    vx = V(lambda e: e.tensor_copy(out=wsc[:, 2:3], in_=pbcf[:, 3:4]), ms=True)
    vx = V(lambda e: e.reciprocal(out=wsc[:, 1:2], in_=wsc[:, 0:1]), waits=[(sv, vx)], ms=True)
    wsc_rdy = V(lambda e: e.reciprocal(out=wsc[:, 3:4], in_=wsc[:, 2:3]), waits=[(sv, vx)], ms=True)
    wsc_rdy = V(lambda e: e.tensor_copy(out=tmp[:, 15:16], in_=wsc[:, 3:4]), waits=[(sv, wsc_rdy)], ms=True)
    pbc_free = wsc_rdy

    w1qs_v = w1qs.rearrange("(c p) h -> c p h", p=128)
    w2qs_v = w2qs.rearrange("(c p) h -> c p h", p=128)
    q_ms = {}
    for j in range(8):
        b = j % 2
        w_ = [(sv, wsc_rdy)] if j < 2 else [(sv, q_ms[j - 2])]
        ld = LD(lambda e, j=j, b=b: e.dma_start(out=wload[b], in_=w1tv[j, :, :]), waits=w_, sem=dw2[b])
        V(lambda e, j=j, b=b: e.tensor_scalar(out=wload[b], in0=wload[b], scalar1=wsc[:, 1:2], scalar2=None, op0=ALU.mult),
          waits=[(dw2[b], ld)])
        V(lambda e, j=j, b=b: e.tensor_scalar(out=wload[b], in0=wload[b], scalar1=RMAGIC, scalar2=RMAGIC, op0=ALU.add, op1=ALU.subtract))
        q_ms[j] = V(lambda e, j=j, b=b: e.tensor_scalar(out=w1q[:, j * 4096:(j + 1) * 4096], in0=wload[b], scalar1=-1.0, scalar2=1.0, op0=ALU.max, op1=ALU.min), ms=True)
        GD(lambda e, j=j: e.dma_start(out=w1qs_v[j, :, :], in_=w1q[:, j * 4096:(j + 1) * 4096]),
           waits=[(sv, q_ms[j])], sem=dsp)
    for j in range(32):
        b = j % 2
        w_ = [(sv, q_ms[6 + b])] if j < 2 else [(sv, q_ms[8 + j - 2])]
        ld = LD(lambda e, j=j, b=b: e.dma_start(out=wload[b][:, 0:1024], in_=w2tv[j, :, :]), waits=w_, sem=dw2[b])
        V(lambda e, j=j, b=b: e.tensor_scalar(out=wload[b][:, 0:1024], in0=wload[b][:, 0:1024], scalar1=wsc[:, 3:4], scalar2=None, op0=ALU.mult),
          waits=[(dw2[b], ld)])
        V(lambda e, j=j, b=b: e.tensor_scalar(out=wload[b][:, 0:1024], in0=wload[b][:, 0:1024], scalar1=RMAGIC, scalar2=RMAGIC, op0=ALU.add, op1=ALU.subtract))
        q_ms[8 + j] = V(lambda e, j=j, b=b: e.tensor_scalar(out=w2q[:, j * 1024:(j + 1) * 1024], in0=wload[b][:, 0:1024], scalar1=-1.0, scalar2=1.0, op0=ALU.max, op1=ALU.min), ms=True)
        GD(lambda e, j=j: e.dma_start(out=w2qs_v[j, :, :], in_=w2q[:, j * 1024:(j + 1) * 1024]),
           waits=[(sv, q_ms[8 + j])], sem=dsp)
    wq_done_v = q_ms[8 + 31]
    wspill = dsp.v

    def finalize_stats(wv_, wa_, nsteps=9):
        a = wa_
        if nsteps >= 1 and os.environ.get("ATH_PRE"):
            a = A_(lambda e: e.preload_act_table(AF.Exp), waits=[(sa, wa_)], ms=True)
        v = wv_
        if os.environ.get("ATH_FSKIPV"):
            nsteps_v = 0
        else:
            nsteps_v = nsteps
        if nsteps_v >= 2:
            v = V(lambda e: e.tensor_scalar(out=statm[:], in0=ssum[:], scalar1=1.0 / D, scalar2=None, op0=ALU.mult),
                  waits=[(sv, wv_), (sa, wa_)], ms=True)
        if nsteps_v >= 3:
            v = V(lambda e: e.tensor_tensor(out=statr[:], in0=statm[:], in1=statm[:], op=ALU.mult), waits=[(sv, v)], ms=True)
        if nsteps_v >= 4:
            v = V(lambda e: e.tensor_scalar(out=ssq[:], in0=ssq[:], scalar1=1.0 / D, scalar2=None, op0=ALU.mult), waits=[(sv, v)], ms=True)
        if nsteps_v >= 5:
            v = V(lambda e: e.tensor_tensor(out=statr[:], in0=ssq[:], in1=statr[:], op=ALU.subtract), waits=[(sv, v)], ms=True)
        if nsteps >= 6:
            lnf = {"sq": AF.Square, "ln": AF.Ln}[os.environ.get("ATH_LNF", "ln")]
            lnb = 0.0 if os.environ.get("ATH_LNB") else consts[:, CL_EPS:CL_EPS + 1]
            lno = tmp[:, 0:1] if os.environ.get("ATH_LNO") else statr[:]
            lni = statr[:, 0:1] if os.environ.get("ATH_LNO") else statr[:]
            a = A_(lambda e: e.activation(out=lno, in_=lni, func=lnf, bias=lnb, scale=1.0),
               waits=[(sv, v)], ms=True)
        if nsteps >= 7:
            a = A_(lambda e: e.activation(out=statr[:], in_=statr[:], func=AF.Exp, bias=0.0, scale=-0.5), waits=[(sa, a)], ms=True)
        return sv.v, a

    fin_v, fin_a = finalize_stats(max(stats_v, wsc_rdy), stats_a)
    if dbg1 is not None and DEV_PHASES == "ffn1":
        GD(lambda e: e.dma_start(out=dbg1[:, 92:124], in_=statr[:]), waits=[(sa, fin_a)])

    def emit_ffn(src, dst, collect_stats, phase_wait_v, phase_wait_a, extra_first=(), pbc0=None, mm0_wait=()):
        srcv = src.rearrange("(t g p) d -> t p g d", p=128, g=2)
        dstv = dst.rearrange("(t g p) d -> t p g d", p=128, g=2)

        if os.environ.get("ATH_PRE"):
            A_(lambda e: e.preload_act_table(AF.Silu), waits=[(sa, phase_wait_a)])

        x_ld = {}
        x_ld2 = {}
        a_v = {}
        a_a = {}
        a_t = {}
        b_store = {}
        b_stats_a = {}
        b_mm1_done = {}
        evac_ms = {}
        po_free = None
        nonloc = dict(pbc=pbc0 if pbc0 is not None else pbc_free, first_mm=True)

        def slot_view(slot):
            return yqT[:, slot * 2064:(slot + 1) * 2064].rearrange("p (c n) -> p c n", n=258)

        def a_part(i):
            slot = i % 3
            b = i % 2
            sfirst = (i % TPS) == 0
            slast = (i % TPS) == TPS - 1
            w0 = [(dx2[i % 2], x_ld[i])]
            if i >= 1:
                w0.append((st_, a_t[i - 1]))
            tg = {}
            cp_ms = {}
            qs_rdy_g = {}
            for g in range(2):
                stc = 2 * i + g
                wv_ = w0 if g == 0 else [(dx3[i % 2], x_ld2[i]), (sp, cp_ms[(0, 7)])]
                V(lambda e, b=b, g=g, stc=stc: e.tensor_scalar(
                    out=stg[:], in0=xio[b][:, g * 1024:(g + 1) * 1024],
                    scalar1=statm[:, stc:stc + 1], scalar2=None, op0=ALU.subtract, op1=ALU.max,
                    accum_out=tmp[:, g:g + 1]),
                  waits=wv_)
                V(lambda e, b=b, g=g, stc=stc: e.tensor_scalar(
                    out=stg[:], in0=xio[b][:, g * 1024:(g + 1) * 1024],
                    scalar1=statm[:, stc:stc + 1], scalar2=None, op0=ALU.subtract, op1=ALU.min,
                    accum_out=tmp[:, 2 + g:3 + g]))
                V(lambda e, g=g: e.tensor_scalar(out=tmp[:, 2 + g:3 + g], in0=tmp[:, 2 + g:3 + g], scalar1=-1.0, scalar2=None, op0=ALU.mult))
                V(lambda e, g=g: e.tensor_tensor(out=tmp[:, g:g + 1], in0=tmp[:, g:g + 1], in1=tmp[:, 2 + g:3 + g], op=ALU.max))
                V(lambda e, g=g: e.tensor_scalar(out=tmp[:, 2 + g:3 + g], in0=tmp[:, g:g + 1], scalar1=1e-5, scalar2=None, op0=ALU.max))
                vv = V(lambda e, g=g: e.reciprocal(out=tmp[:, 4 + g:5 + g], in_=tmp[:, 2 + g:3 + g]), ms=True)
                vv = V(lambda e, g=g: e.tensor_scalar(out=tmp[:, 4 + g:5 + g], in0=tmp[:, 4 + g:5 + g], scalar1=127.0, scalar2=None, op0=ALU.mult), waits=[(sv, vv)], ms=True)
                V(lambda e, g=g, i=i, stc=stc: e.tensor_tensor(out=tmp[:, 6 + g:7 + g], in0=statr[:, stc:stc + 1], in1=tmp[:, g:g + 1], op=ALU.mult))
                V(lambda e, g=g: e.tensor_scalar(out=tmp[:, 6 + g:7 + g], in0=tmp[:, 6 + g:7 + g], scalar1=1e-5, scalar2=None, op0=ALU.max))
                qs_rdy_g[g] = V(lambda e, g=g: e.tensor_scalar(out=tmp[:, 6 + g:7 + g], in0=tmp[:, 6 + g:7 + g], scalar1=wsc[:, 0:1], scalar2=1.0 / 127.0, op0=ALU.mult, op1=ALU.mult), ms=True)
                w_q = [(sv, qs_rdy_g[g])]
                if g == 1:
                    w_q.append((sp, cp_ms[(0, 7)]))
                qv = V(lambda e, g=g, b=b, stc=stc: e.tensor_scalar(
                    out=stg[:], in0=xio[b][:, g * 1024:(g + 1) * 1024],
                    scalar1=statm[:, stc:stc + 1], scalar2=tmp[:, 4 + g:5 + g],
                    op0=ALU.subtract, op1=ALU.mult),
                    waits=w_q, ms=True)
                qv = V(lambda e: e.tensor_scalar(out=stg[:], in0=stg[:], scalar1=RMAGIC, scalar2=RMAGIC, op0=ALU.add, op1=ALU.subtract), ms=True)
                prev_ev = a_a.get((i, g - 1)) or a_a.get((i - 1, 1))
                for j in range(8):
                    w_c = [(sv, qv)]
                    if (g, j) not in ((0, 0), (0, 1)):
                        pj = (g, j - 2) if j >= 2 else (g - 1, 6 + j)
                        w_c.append((st_, tg.get(pj, 0)))
                    G(lambda e, j=j: e.tensor_copy(out=blkint[:, 128 * (j % 2):128 * (j % 2) + 128], in_=stg[:, 128 * j:128 * (j + 1)]),
                      waits=w_c, ms=True)
                    cp_ms[(g, j)] = G(lambda e, j=j: e.tensor_copy(out=blk[:, 128 * (j % 2):128 * (j % 2) + 128], in_=blkint[:, 128 * (j % 2):128 * (j % 2) + 128]),
                                      ms=True)
                    w_t = [(sp, cp_ms[(g, j)])]
                    if j == 0 and prev_ev:
                        w_t.append((sa, prev_ev))
                    tg[(g, j)] = T(lambda e, g=g, j=j: e.transpose(
                        out=ptrb[:, 128 * j:128 * (j + 1)],
                        in_=blk[:, 128 * (j % 2):128 * (j % 2) + 128],
                        identity=identbf[:]),
                        waits=w_t, ms=True)
                dest = slot_view(slot)[:, :, 1 + 128 * g:1 + 128 * g + 128]
                a_a[(i, g)] = A_(lambda e, dest=dest: e.copy(out=dest, in_=ptrb[:].rearrange("p (c n) -> p c n", n=128)),
                                 waits=[(st_, tg[(g, 7)])], ms=True)
            qs_rdy = qs_rdy_g[1]
            w_ = [(sv, qs_rdy), (sv, nonloc["pbc"])]
            T(lambda e: e.transpose(out=pbcf[0:1, 0:128], in_=tmp[:, 6:7], identity=identf[:]), waits=w_)
            tr2 = T(lambda e: e.transpose(out=pbcf[0:1, 128:256], in_=tmp[:, 7:8], identity=identf[:]), ms=True)
            ev = V(lambda e, slot=slot: e.tensor_copy(out=r1row[0:1, slot * 258 + 1:slot * 258 + 257], in_=pbcf[0:1, 0:256]),
                   waits=[(st_, tr2)], ms=True)
            nonloc["pbc"] = ev
            hv = [(sa, a_a[(i, 1)]), (st_, b_mm1_done.get(i - 2, 0))]
            if sfirst:
                V(lambda e, slot=slot: e.memset(slot_view(slot)[:, :, 0:1], 0.0), waits=hv, ms=True)
                V(lambda e, slot=slot: e.memset(r1row[0:1, slot * 258:slot * 258 + 1], 1.0), ms=True)
            else:
                prev = (i - 1) % 3
                V(lambda e, slot=slot, prev=prev: e.tensor_copy(
                    out=slot_view(prev)[:, :, 257:258], in_=slot_view(slot)[:, :, 1:2]), waits=hv, ms=True)
                V(lambda e, slot=slot, prev=prev: e.tensor_copy(out=r1row[0:1, prev * 258 + 257:prev * 258 + 258], in_=r1row[0:1, slot * 258 + 1:slot * 258 + 2]), ms=True)
            if slast:
                V(lambda e, slot=slot: e.memset(slot_view(slot)[:, :, 257:258], 0.0), ms=True)
                V(lambda e, slot=slot: e.memset(r1row[0:1, slot * 258 + 257:slot * 258 + 258], 1.0), ms=True)
            else:
                nxt = (i + 1) % 3
                V(lambda e, slot=slot, nxt=nxt: e.tensor_copy(
                    out=slot_view(nxt)[:, :, 0:1], in_=slot_view(slot)[:, :, 256:257]), ms=True)
                V(lambda e, slot=slot, nxt=nxt: e.tensor_copy(out=r1row[0:1, nxt * 258:nxt * 258 + 1], in_=r1row[0:1, slot * 258 + 256:slot * 258 + 257]), ms=True)
            a_v[i] = sv.v
            a_t[i] = tg[(1, 7)]

        def b_part(i, halo_v):
            slot = i % 3
            b = i % 2
            w_ = [(sv, halo_v), (sv, nonloc["pbc"]), (sa, a_a[(i, 1)])]
            tb = T(lambda e, slot=slot: e.matmul(out=pbcf[:, 0:258], lhsT=onesf[0:1, :], rhs=r1row[0:1, slot * 258:slot * 258 + 258], start=True, stop=True, skip_group_check=True),
                   waits=w_, ms=True)
            r1bc_ms = V(lambda e: e.tensor_copy(out=r1bc[:], in_=pbcf[:, 0:258]), waits=[(st_, tb)], ms=True)
            nonloc["pbc"] = r1bc_ms

            silu_ms = {}
            tree_ms = {}
            mm1_ms = {}
            if CUT >= 1:
              for c in range(32):
                bank = c % 2
                w_mm = [(sv, evac_ms.get(bank, 0))]
                if c == 0:
                    w_mm.append((sv, r1bc_ms))
                    if b_stats_a.get(i - 1):
                        w_mm.append((sa, b_stats_a[i - 1]))
                    if nonloc.get("first_mm"):
                        w_mm += list(mm0_wait)
                        nonloc["first_mm"] = False
                for d_ in range(8):
                    mm1_ms[c] = T(lambda e, c=c, d_=d_, bank=bank, slot=slot: e.matmul(
                        out=ph[bank], lhsT=w1q[:, d_ * 4096 + c * 128:d_ * 4096 + (c + 1) * 128],
                        rhs=yqT[:, slot * 2064 + d_ * 258:slot * 2064 + (d_ + 1) * 258],
                        start=(d_ == 0), stop=(d_ == 7), skip_group_check=True),
                        waits=w_mm if d_ == 0 else (), ms=(d_ == 7))
                if c == 31:
                    b_mm1_done[i] = mm1_ms[c]
                hc = hbuf[:, c * 258:(c + 1) * 258]
                ev = V(lambda e, hc=hc, bank=bank: e.tensor_tensor(out=hc, in0=ph[bank], in1=r1bc[:], op=ALU.mult),
                       waits=[(st_, mm1_ms[c])], ms=True)
                evac_ms[bank] = ev
                if CUT >= 2:
                    cvb = cvb_t[:, 256 * (c % 2):256 * (c % 2) + 256]
                    w_t1 = [(sv, ev)]
                    if c >= 2:
                        w_t1.append((sv, tree_ms[c - 2]))
                    t1 = A_(lambda e, hc=hc, cvb=cvb, c=c: e.activation(
                        out=cvb, in_=hc[:, 1:257], func=AF.Identity,
                        bias=cw[:, 96 + (c % 32):97 + (c % 32)], scale=cw[:, 32 + (c % 32):33 + (c % 32)]),
                        waits=w_t1, ms=True)
                    t2 = V(lambda e, hc=hc, cvb=cvb, c=c: e.scalar_tensor_tensor(
                        out=cvb, in0=hc[:, 0:256], scalar=cw[:, (c % 32):(c % 32) + 1], in1=cvb, op0=ALU.mult, op1=ALU.add),
                        waits=[(sa, t1)], ms=True)
                    if not os.environ.get("ATH_POOLTAP"):
                        t3 = V(lambda e, hc=hc, cvb=cvb, c=c: e.scalar_tensor_tensor(
                            out=cvb, in0=hc[:, 2:258], scalar=cw[:, 64 + (c % 32):65 + (c % 32)], in1=cvb, op0=ALU.mult, op1=ALU.add),
                            waits=[(sv, t2)], ms=True)
                        A_(lambda e, hc=hc, cvb=cvb: e.activation(out=hc[:, 1:257], in_=cvb, func=AF.Silu),
                           waits=[(sv, t3)])
                    else:
                        G(lambda e, hc=hc, c=c: e.tensor_scalar(
                            out=accp[:], in0=hc[:, 2:258], scalar1=cw[:, 64 + (c % 32):65 + (c % 32)], scalar2=None, op0=ALU.mult),
                          waits=[(sv, ev)])
                        t3 = G(lambda e, cvb=cvb: e.tensor_tensor(out=cvb, in0=cvb, in1=accp[:], op=ALU.add),
                               waits=[(sv, t2)], ms=True)
                        A_(lambda e, hc=hc, cvb=cvb: e.activation(out=hc[:, 1:257], in_=cvb, func=AF.Silu),
                           waits=[(sp, t3)])
                    silu_ms[c] = A_(lambda e, hc=hc, cvb=cvb: e.activation(out=cvb, in_=hc[:, 1:257], func=AF.Abs), ms=True)
                    if c == 0:
                        tree_ms[c] = V(lambda e, cvb=cvb: e.tensor_copy(out=accd[:], in_=cvb),
                                       waits=[(sa, silu_ms[c])], ms=True)
                    else:
                        tree_ms[c] = V(lambda e, cvb=cvb: e.tensor_tensor(out=accd[:], in0=accd[:], in1=cvb, op=ALU.max),
                                       waits=[(sa, silu_ms[c])], ms=True)
            outsc_rdy = 0
            if CUT >= 3:
                comb = tree_ms[31]
                w_ = [(sv, comb), (sv, nonloc["pbc"])]
                T(lambda e: e.transpose(out=pbcf[:, 0:128], in_=accd[:, 0:128], identity=identf[:]), waits=w_)
                tt = T(lambda e: e.transpose(out=pbcf[:, 128:256], in_=accd[:, 128:256], identity=identf[:]), ms=True)
                rd1 = V(lambda e: e.reduce_max(out=tmp[:, 8:9], in_=pbcf[:, 0:128], axis=AX.X), waits=[(st_, tt)], ms=True)
                rd2 = V(lambda e: e.reduce_max(out=tmp[:, 9:10], in_=pbcf[:, 128:256], axis=AX.X), waits=[(sv, rd1)], ms=True)
                vv = V(lambda e: e.tensor_scalar(out=tmp[:, 10:12], in0=tmp[:, 8:10], scalar1=1e-5, scalar2=None, op0=ALU.max), waits=[(sv, rd2)], ms=True)
                vv = V(lambda e: e.reciprocal(out=tmp[:, 12:14], in_=tmp[:, 10:12]), waits=[(sv, vv)], ms=True)
                qs2_rdy = V(lambda e: e.tensor_scalar(out=tmp[:, 12:14], in0=tmp[:, 12:14], scalar1=127.0, scalar2=None, op0=ALU.mult), waits=[(sv, vv)], ms=True)
                outsc_rdy = V(lambda e: e.tensor_scalar(out=tmp[:, 14:16], in0=tmp[:, 10:12], scalar1=wsc[:, 2:3], scalar2=0.5 / 127.0, op0=ALU.mult, op1=ALU.mult), waits=[(sv, qs2_rdy)], ms=True)
                tq = T(lambda e: e.transpose(out=pbcf[0:1, 256:384], in_=tmp[:, 12:13], identity=identf[:]), waits=[(sv, qs2_rdy)])
                tq = T(lambda e: e.transpose(out=pbcf[0:1, 384:512], in_=tmp[:, 13:14], identity=identf[:]), ms=True)
                evq = V(lambda e: e.tensor_copy(out=r2row[0:1, 0:256], in_=pbcf[0:1, 256:512]), waits=[(st_, tq)], ms=True)
                tbc2 = T(lambda e: e.matmul(out=pbcf[:, 0:256], lhsT=onesf[0:1, :], rhs=r2row[0:1, 0:256], start=True, stop=True, skip_group_check=True),
                         waits=[(sv, evq)], ms=True)
                r2bc_ms = V(lambda e: e.tensor_copy(out=r2bc, in_=pbcf[:, 0:256]), waits=[(st_, tbc2)], ms=True)
                nonloc["pbc"] = r2bc_ms
            ev = r1bc_ms
            if CUT >= 4:
                accf = [accd[:], accp[:]]
                mm2_ms = {}
                qt_ms = {}
                i2_ms = {}
                for c in range(32):
                    sq = sqbf[:, 256 * (c % 2):256 * (c % 2) + 256]
                    acc = accf[c % 2]
                    w_ = [(sv, r2bc_ms)] if c < 2 else []
                    V(lambda e, c=c, acc=acc: e.tensor_tensor(
                        out=acc, in0=hbuf[:, c * 258 + 1:c * 258 + 257], in1=r2bc, op=ALU.mult),
                        waits=w_, ms=True)
                    w_ = [] if c < 1 else [(sp, i2_ms[c - 1])]
                    qt_ms[c] = V(lambda e, acc=acc: e.tensor_scalar(
                        out=sqint[:], in0=acc, scalar1=RMAGIC, scalar2=RMAGIC, op0=ALU.add, op1=ALU.subtract),
                        waits=w_, ms=True)
                    w_ = [(sv, qt_ms[c])]
                    if c >= 2:
                        w_.append((st_, mm2_ms[c - 2]))
                    i2_ms[c] = G(lambda e, c=c, sq=sq: e.tensor_copy(out=sq, in_=sqint[:]),
                                 waits=w_, ms=True)
                    for g in range(2):
                        for n2 in range(2):
                            w_m = [(sp, i2_ms[c])]
                            if c == 0 and g == 0 and n2 == 0 and po_free is not None:
                                w_m.append((sv, po_free))
                            mm2_ms[c] = T(lambda e, c=c, g=g, n2=n2, sq=sq: e.matmul(
                                out=po[g][:, n2 * 512:(n2 + 1) * 512],
                                lhsT=sq[:, g * 128:(g + 1) * 128],
                                rhs=w2q[:, c * 1024 + n2 * 512:c * 1024 + n2 * 512 + 512],
                                start=(c == 0), stop=(c == 31), skip_group_check=True),
                                waits=w_m if (g == 0 and n2 == 0) else (), ms=(g == 1 and n2 == 1))
                ev = None
            stat_aa = 0
            for g in range(2):
                if CUT >= 4:
                    for n2 in range(2):
                        ev = V(lambda e, g=g, n2=n2, b=b: e.scalar_tensor_tensor(
                            out=xio[b][:, g * 1024 + n2 * 512:g * 1024 + n2 * 512 + 512],
                            in0=po[g][:, n2 * 512:(n2 + 1) * 512],
                            scalar=tmp[:, 14 + g:15 + g],
                            in1=xio[b][:, g * 1024 + n2 * 512:g * 1024 + n2 * 512 + 512],
                            op0=ALU.mult, op1=ALU.add),
                            waits=[(st_, mm2_ms[31]), (sv, outsc_rdy)] if (g == 0 and n2 == 0) else (), ms=True)
                stat_vv = 0
                if collect_stats:
                    stc = 2 * i + g
                    stat_vv = V(lambda e, g=g, b=b, stc=stc: e.reduce_sum(out=ssum[:, stc:stc + 1], in_=xio[b][:, g * 1024:(g + 1) * 1024], axis=AX.X), ms=True)
                    stat_aa = A_(lambda e, g=g, b=b, stc=stc: e.activation(out=hbuf[:, 0:1024], in_=xio[b][:, g * 1024:(g + 1) * 1024], func=AF.Square, accum_out=ssq[:, stc:stc + 1]),
                                 waits=[(sv, stat_vv)], ms=True)
                b_store[(i, g)] = GD(lambda e, i=i, b=b, g=g: e.dma_start(
                    out=dstv[i, :, g, :], in_=xio[b][:, g * 1024:(g + 1) * 1024]),
                    waits=[(sv, stat_vv or ev), (sa, stat_aa)], sem=(dg2 if g == 0 else dg3)[b])
            po_free_local = ev
            if collect_stats:
                b_stats_a[i] = stat_aa
            return po_free_local

        for i in range(NT):
            if i < 2:
                w_ = [(sv, phase_wait_v), (sa, phase_wait_a)] + list(extra_first)
                w1_ = list(w_)
            else:
                w_ = [(dg2[i % 2], b_store[(i - 2, 0)])]
                w1_ = [(dg3[i % 2], b_store[(i - 2, 1)])]
            x_ld[i] = LD(lambda e, i=i: e.dma_start(out=xio[i % 2][:, 0:1024], in_=srcv[i, :, 0, :]),
                         waits=w_, sem=dx2[i % 2])
            x_ld2[i] = LD(lambda e, i=i: e.dma_start(out=xio[i % 2][:, 1024:2048], in_=srcv[i, :, 1, :]),
                          waits=w1_, sem=dx3[i % 2])
            a_part(i)
            if dbg4 is not None and 2 < i < 6 and os.environ.get("ATH_SNAP"):
                GD(lambda e, i=i: e.dma_start(out=dbg4[:, 260 + 32 * (i - 3):292 + 32 * (i - 3)], in_=statr[:]),
                   waits=[(sv, a_v[i])])
            if i >= 1:
                po_free = b_part(i - 1, a_v[i])
        po_free = b_part(NT - 1, a_v[NT - 1])
        emit_ffn.b_store = dict(b_store)
        emit_ffn.last_pbc = nonloc["pbc"]
        emit_ffn.store_tot = [(dg2[0], dg2[0].v), (dg2[1], dg2[1].v), (dg3[0], dg3[0].v), (dg3[1], dg3[1].v)]
        return sv.v, sa.v


    def emit_attn(src, dst, phase_wait_v, phase_wait_a, store_wait, pbc0):
        srcv = src.rearrange("(g p) d -> p g d", p=128)
        dstv = dst.rearrange("(g p) d -> p g d", p=128)

        wq_b = arh(0, 8192).rearrange("p (c n) -> p c n", n=1024)
        wk_b = arh(4096, 8192).rearrange("p (c n) -> p c n", n=1024)
        wv_b = arh(8192, 8192).rearrange("p (c n) -> p c n", n=1024)
        yT = arh(12288, 16384).rearrange("p (c n) -> p c n", n=2048)
        Kb = arh(20480, 16384).rearrange("p (c n) -> p c n", n=2048)
        Vb = arh(28672, 16640).rearrange("p (c n) -> p c n", n=1040)
        Vb_hk = arh(28672, 16640).rearrange("p (c h k) -> p c h k", h=16, k=65)
        xa = [arf(36992, 1024), arf(38016, 1024)]
        cosb = arh(39040, 1024)
        sinb = arh(39552, 1024)
        bqr = arh(40064, 1024)[0:1, :]
        bvr = arh(40576, 1024)[0:1, :]
        bor = arh(41088, 1024)[0:1, :]
        e16f = arf(41600, 256)[0:1, :]
        selb = arf(41856, 1024)[0:16, :]
        ones128 = arh(42880, 128)[0:1, :]
        ones512 = arh(42944, 512)[0:1, :]
        wo_b = arh(0, 8192).rearrange("p (c n) -> p c n", n=1024)
        Qf = arh(4096, 16384).rearrange("p (c n) -> p c n", n=2048)
        A_sb = arf(12288, 4096)
        Abf = arh(16384, 4096)
        eS = arh(18432, 2048)
        rd_sb = stg[0:16, 0:512]
        selbf = arh(43456, 1024)[0:16, :]
        e16bf = arh(44000, 256)[0:1, :]
        dnbf = arh(44200, 1024)[0:1, :]
        rdbf = arh(44800, 512)[0:16, :]
        stgbf = stg[:].bitcast(BF16)[:, 0:1024]
        cvbbf = cvb_t[:].bitcast(BF16)[:, 0:1024]

        psA = [P[:, 0:512], P[:, 512:1024]]
        psB = [P[:, 1024:1536], P[:, 1536:2048]]
        pv2 = P[:, 1024:2048]
        pav_e = P[:, 2048:2560]
        pav_o = P[:, 2560:3072]
        plv = P[0:16, 3072:3584]
        pR = P[:, 3584:4096]

        pr.op("gpsimd", lambda e: e.nop(),
              waits=[(sv, phase_wait_v), (sa, phase_wait_a), (st_, st_.v), (sp, sp.v),
                     (dsp, dsp.v)])
        wqv = wqd.rearrange("(c p) o -> c p o", p=128)
        wkv = wkd.rearrange("(c p) o -> c p o", p=128)
        wvv = wvd.rearrange("(c p) o -> c p o", p=128)

        def load_qkv(wkv_extra=(), wq_extra=()):
            first = list(wkv_extra)
            for wsrc, wdst in ((wkv, wk_b), (wvv, wv_b)):
                for c in range(8):
                    GD(lambda e, wsrc=wsrc, wdst=wdst, c=c: e.dma_start(
                        out=wdst[:, c, :], in_=wsrc[c, :, :]), waits=first, sem=dsp)
                    first = []
            wkv_done_ = dsp.v
            first = list(wq_extra)
            for c in range(8):
                GD(lambda e, c=c: e.dma_start(
                    out=wq_b[:, c, :], in_=wqv[c, :, :]), waits=first, sem=dsq)
                first = []
            return wkv_done_, dsq.v

        wkv_done, wq_done = load_qkv()
        GD(lambda e: e.dma_start(out=cosb, in_=cosd), sem=dsp)
        GD(lambda e: e.dma_start(out=sinb, in_=sind), sem=dsp)
        wl_done = dsp.v
        wkv_done = wl_done
        bl = LD(lambda e: e.dma_start(out=stg[0:1, 0:1024], in_=bqd), waits=[(sv, phase_wait_v), (sa, phase_wait_a)] + list(store_wait))
        bc = V(lambda e: e.tensor_copy(out=bqr, in_=stg[0:1, 0:1024]), waits=[(dw, bl)], ms=True)
        bl = LD(lambda e: e.dma_start(out=stg[0:1, 0:1024], in_=bvd), waits=[(sv, bc)])
        bc = V(lambda e: e.tensor_copy(out=bvr, in_=stg[0:1, 0:1024]), waits=[(dw, bl)], ms=True)
        bl = LD(lambda e: e.dma_start(out=stg[0:1, 0:1024], in_=bod), waits=[(sv, bc)])
        bc = V(lambda e: e.tensor_copy(out=bor, in_=stg[0:1, 0:1024]), waits=[(dw, bl)], ms=True)
        bl = LD(lambda e: e.dma_start(out=e16f, in_=e16_d), waits=[(sv, bc)])
        bc = V(lambda e: e.tensor_copy(out=e16bf, in_=e16f), waits=[(dw, bl)], ms=True)
        bl = LD(lambda e: e.dma_start(out=selb, in_=sel_d), waits=[(sv, bc)])
        bc = V(lambda e: e.tensor_copy(out=selbf, in_=selb), waits=[(dw, bl)], ms=True)
        V(lambda e: e.memset(ones128, 1.0))
        ones_ms = V(lambda e: e.memset(ones512, 1.0), ms=True)

        pbc = pbc0
        pl_rd_a = 0
        xa_user = {0: ones_ms, 1: ones_ms}
        xa_ld = {0: 0, 1: 0}
        b_st2 = {0: 0, 1: 0}
        eS_rd = {}
        psA_rd = {0: 0, 1: 0}
        psB_rd = {0: 0, 1: 0}
        psB_rd_v = {0: 0, 1: 0}
        pk_free = {}
        stats_ms_a = 0
        kb_dump = {}

        aseq = int(os.environ.get("ATH_ASEQ", "2"))
        aqt = int(os.environ.get("ATH_AQT", "3"))
        for s_ in range(aseq):
            seq0_T_done = st_.v
            if s_ == 1:
                wkv_done, wq_done = load_qkv(
                    wkv_extra=[(st_, kb_dump.get("qf_dead", st_.v))],
                    wq_extra=[(st_, st_.v)])
                wl_done = dsp.v
            st_t = {}
            ev_a = {}
            for st in range(16):
                g = s_ * 16 + st
                par = st % 2
                w_ = [(sv, xa_user[par])]
                if s_ == 0 and st < 2:
                    w_ += list(store_wait)
                if b_st2[par]:
                    w_.append((dg2[par], b_st2[par]))
                xa_ld[par] = LD(lambda e, g=g, par=par: e.dma_start(out=xa[par], in_=srcv[:, g, :]),
                                waits=w_, sem=dx2[par])
                w_ = [(dx2[par], xa_ld[par])]
                if s_ == 0:
                    w_.append((dsp, wl_done))
                elif st == 0:
                    w_.append((sa, pl_rd_a))
                if st >= 1:
                    w_.append((st_, st_t[st - 1]))
                if s_ == 1 and st == 0 and 0 in kb_dump:
                    w_.append((dgd, kb_dump[0]))
                xa_user[par] = V(lambda e, par=par, g=g: e.tensor_scalar(
                    out=stgbf, in0=xa[par], scalar1=statm[:, g:g + 1], scalar2=statr[:, g:g + 1],
                    op0=ALU.subtract, op1=ALU.mult), waits=w_, ms=True)
                for j in range(8):
                    w_ = [(sv, xa_user[par])]
                    if ev_a.get(st - 1):
                        w_.append((sa, ev_a[st - 1]))
                    elif s_ == 1:
                        w_.append((sa, pl_rd_a))
                    st_t[st] = T(lambda e, j=j: e.transpose(
                        out=ptrb[:, 128 * j:128 * (j + 1)],
                        in_=stgbf[:, 128 * j:128 * (j + 1)],
                        identity=identbf[:]), waits=w_ if j == 0 else (), ms=(j == 7))
                dest = yT[:, :, 128 * st:128 * (st + 1)]
                w_ev = [(st_, st_t[st])]
                if s_ == 1 and st == 0:
                    w_ev.append((st_, seq0_T_done))
                ev_a[st] = A_(lambda e, dest=dest: e.copy(out=dest, in_=ptrb[:].rearrange("p (c n) -> p c n", n=128)),
                              waits=w_ev, ms=True)
            yT_a = ev_a[15]
            rope_guard = st_t[15]

            for oc in range(8):
                for tt in range(4):
                    bank = (oc * 4 + tt) % 2
                    w_ = [(sa, yT_a), (sv, pk_free.get(bank, 0)), (sa, psA_rd[bank])]
                    if oc == 0 and tt == 0:
                        w_.append((dsp, wkv_done))
                    for ic in range(8):
                        mmk = T(lambda e, oc=oc, tt=tt, ic=ic, bank=bank: e.matmul(
                            out=psA[bank], lhsT=wk_b[:, ic, 128 * oc:128 * (oc + 1)],
                            rhs=yT[:, ic, 512 * tt:512 * (tt + 1)],
                            start=(ic == 0), stop=(ic == 7), skip_group_check=True),
                            waits=w_ if ic == 0 else (), ms=(ic == 7))
                    pk_free[bank] = V(lambda e, oc=oc, tt=tt, bank=bank: e.tensor_copy(
                        out=Kb[:, oc, 512 * tt:512 * (tt + 1)], in_=psA[bank]),
                        waits=[(st_, mmk)], ms=True)

            for tc in range(16):
                w_ = [(sv, pk_free.get(2, 0)), (sv, psB_rd_v[0]), (sv, psB_rd_v[1]), (sa, psB_rd[0]), (sa, psB_rd[1])]
                for ic in range(8):
                    for n2 in range(2):
                        T(lambda e, tc=tc, ic=ic, n2=n2: e.matmul(
                            out=pv2[:, 512 * n2:512 * (n2 + 1)],
                            lhsT=yT[:, ic, 128 * tc:128 * (tc + 1)],
                            rhs=wv_b[:, ic, 512 * n2:512 * (n2 + 1)],
                            start=(ic == 0), stop=False, skip_group_check=True),
                            waits=w_ if (ic == 0 and n2 == 0) else ())
                for n2 in range(2):
                    mmv = T(lambda e, n2=n2: e.matmul(
                        out=pv2[:, 512 * n2:512 * (n2 + 1)], lhsT=ones128,
                        rhs=bvr[0:1, 512 * n2:512 * (n2 + 1)],
                        start=False, stop=True, skip_group_check=True), ms=(n2 == 1))
                pk_free[2] = V(lambda e, tc=tc: e.tensor_copy(
                    out=Vb_hk[:, tc, :, 0:64], in_=pv2[:].rearrange("p (h k) -> p h k", k=64)),
                    waits=[(st_, mmv)], ms=True)
            vone_ms = V(lambda e: e.memset(Vb_hk[:, :, :, 64:65], 1.0), ms=True)
            psB_rd = {0: 0, 1: 0}
            psB_rd_v = {0: 0, 1: 0}
            qf_guard = 0
            for oc in range(8):
                for tt in range(4):
                    bank = (oc * 4 + tt) % 2
                    w_ = [(sv, pk_free.get(bank, 0)), (sa, psA_rd[bank])]
                    if oc == 0 and tt == 0:
                        w_.append((dsq, wq_done))
                    if oc < 2 and tt == 0:
                        w_.append((sv, vone_ms))
                    for ic in range(8):
                        T(lambda e, oc=oc, tt=tt, ic=ic, bank=bank: e.matmul(
                            out=psA[bank], lhsT=wq_b[:, ic, 128 * oc:128 * (oc + 1)],
                            rhs=yT[:, ic, 512 * tt:512 * (tt + 1)],
                            start=(ic == 0), stop=False, skip_group_check=True),
                            waits=w_ if ic == 0 else ())
                    mmq = T(lambda e, oc=oc, tt=tt, bank=bank: e.matmul(
                        out=psA[bank], lhsT=bqr[0:1, 128 * oc:128 * (oc + 1)], rhs=ones512,
                        start=False, stop=True, skip_group_check=True), ms=True)
                    qf_guard = mmq
                    pk_free[bank] = V(lambda e, oc=oc, tt=tt, bank=bank: e.tensor_copy(
                        out=Qf[:, oc, 512 * tt:512 * (tt + 1)], in_=psA[bank]),
                        waits=[(st_, mmq)], ms=True)
            pr.op("gpsimd", lambda e: e.nop(), waits=[(st_, qf_guard)])
            wov = wod.rearrange("(c p) o -> c p o", p=128)
            for c in range(8):
                GD(lambda e, c=c: e.dma_start(out=wo_b[:, c, :], in_=wov[c, :, :]), sem=dsp)
            wo_done = dsp.v

            def rope(buf_v, oc, col0, width, coff, extra=()):
                sl = lambda a, b: buf_v[a:b, oc, col0:col0 + width]
                V(lambda e: e.tensor_scalar(out=stgbf[0:32, 0:width], in0=sl(32, 64), scalar1=-1.0, scalar2=None, op0=ALU.mult),
                  waits=list(extra))
                V(lambda e: e.tensor_copy(out=stgbf[32:64, 0:width], in_=sl(0, 32)))
                V(lambda e: e.tensor_scalar(out=stgbf[64:96, 0:width], in0=sl(96, 128), scalar1=-1.0, scalar2=None, op0=ALU.mult))
                V(lambda e: e.tensor_copy(out=stgbf[96:128, 0:width], in_=sl(64, 96)))
                V(lambda e: e.tensor_tensor(out=cvbbf[:, 0:width], in0=sl(0, 128), in1=cosb[:, coff:coff + width], op=ALU.mult))
                V(lambda e: e.tensor_tensor(out=stgbf[:, 0:width], in0=stgbf[:, 0:width], in1=sinb[:, coff:coff + width], op=ALU.mult))
                return V(lambda e: e.tensor_tensor(out=sl(0, 128), in0=cvbbf[:, 0:width], in1=stgbf[:, 0:width], op=ALU.add), ms=True)

            rope_ms = 0
            for oc in range(8):
                rope_ms = rope(Kb, oc, 1024, 1024, 0,
                               extra=[(st_, rope_guard)] if oc == 0 else ())
            kbtrace = dbg2 is not None and s_ == 0 and os.environ.get("ATH_KBTRACE")
            if dbg2 is not None and s_ == 0 and not kbtrace:
                kb_dump[0] = GD(lambda e: e.dma_start(out=dbg2[:, 0:8192], in_=arf(20480, 8192)),
                                waits=[(sv, rope_ms)])
            for oc in range(8):
                rope_ms = rope(Qf, oc, 1024, 1024, 0)

            def kbdump(slot, waits):
                kb_dump[0] = GD(lambda e, slot=slot: e.dma_start(
                    out=dbg2[:, 1024 * slot:1024 * (slot + 1)], in_=arf(20480, 1024)), waits=waits)
            if kbtrace:
                kbdump(0, [(sv, rope_ms)])

            A_ev = 0
            dn_ev = 0
            abf_ms = 0
            oproj_t = 0
            for qt in range(aqt + 1):
                q0 = 512 * qt
                qev = rope_ms
                for p in range(8):
                    for cp in range(8):
                        mA = mB = None
                        for cc in range(2):
                            c = 2 * cp + cc
                            w_ = [(sa, psA_rd[0])] if cc == 0 else []
                            if p == 0 and c < 2:
                                w_.append((sv, qev))
                            mA = T(lambda e, p=p, c=c, cc=cc, q0=q0: e.matmul(
                                out=psA[cc], lhsT=Kb[0:64, p, 128 * c:128 * (c + 1)],
                                rhs=Qf[0:64, p, q0:q0 + 512], start=True, stop=True,
                                tile_position=(0, 0), skip_group_check=True),
                                waits=w_)
                            w_ = [(sa, psB_rd[0])] if cc == 0 else []
                            if p == 0 and c < 2:
                                w_.append((sv, psB_rd_v[cc]))
                            mB = T(lambda e, p=p, c=c, cc=cc, q0=q0: e.matmul(
                                out=psB[cc], lhsT=Kb[64:128, p, 128 * c:128 * (c + 1)],
                                rhs=Qf[64:128, p, q0:q0 + 512], start=True, stop=True,
                                tile_position=(64, 0), skip_group_check=True),
                                waits=w_, ms=(cc == 1))
                        w_ = [(st_, mB)]
                        for a_b in (0, 1):
                            for h_ in (0, 1):
                                if eS_rd.get((a_b, h_)):
                                    w_.append((st_, eS_rd[(a_b, h_)]))
                        eA = eB = A_(lambda e: e.activation(
                            out=eS[:, 0:2048], in_=P[:, 0:2048], func=AF.Exp,
                            bias=consts[:, CL_NSH:CL_NSH + 1], scale=0.125),
                            waits=w_, ms=True)
                        psA_rd[0] = psA_rd[1] = eA
                        psB_rd[0] = psB_rd[1] = eB
                        for cc in range(2):
                            c = 2 * cp + cc
                            w_ = [(sa, eA)] if cc == 0 else []
                            if c == 0:
                                w_.append((sv, A_ev))
                                w_.append((sa, stats_ms_a))
                            avA = T(lambda e, p=p, c=c, cc=cc: e.matmul(
                                out=pav_e[0:65, :], lhsT=Vb[:, c, 1040 * 0 + 65 * (2 * p):65 * (2 * p) + 65],
                                rhs=eS[:, 512 * cc:512 * (cc + 1)],
                                start=(c == 0), stop=(c == 15), skip_group_check=True),
                                waits=w_, ms=(c == 15))
                            w_ = [(sa, eB)] if cc == 0 else []
                            if c == 0:
                                w_.append((sv, A_ev))
                            avB = T(lambda e, p=p, c=c, cc=cc: e.matmul(
                                out=pav_o[0:65, :], lhsT=Vb[:, c, 65 * (2 * p + 1):65 * (2 * p + 1) + 65],
                                rhs=eS[:, 1024 + 512 * cc:1024 + 512 * (cc + 1)],
                                start=(c == 0), stop=(c == 15), skip_group_check=True),
                                waits=w_, ms=(c == 15 or cc == 1))
                            if cc == 1:
                                eS_rd[(0, 0)] = eS_rd[(0, 1)] = avB
                                eS_rd[(1, 0)] = eS_rd[(1, 1)] = avB
                    V(lambda e, p=p: e.tensor_copy(out=A_sb[0:64, 512 * p:512 * (p + 1)], in_=pav_e[0:64, :]),
                      waits=[(st_, avA), (st_, avB), (sa, stats_ms_a)])
                    A_ev = V(lambda e, p=p: e.tensor_copy(out=A_sb[64:128, 512 * p:512 * (p + 1)], in_=pav_o[0:64, :]), ms=True)
                    if kbtrace and qt == 1 and p == 7:
                        kbdump(3, [(sv, A_ev)])
                    V(lambda e: e.tensor_copy(out=dnbf[0:1, 0:512], in_=pav_e[64:65, :]))
                    dn_ev = V(lambda e: e.tensor_copy(out=dnbf[0:1, 512:1024], in_=pav_o[64:65, :]), ms=True)
                    w_ = [(sv, dn_ev)]
                    if p == 0:
                        w_.append((sa, pl_rd_a))
                    T(lambda e, p=p: e.matmul(out=plv, lhsT=e16bf[0:1, 16 * (2 * p):16 * (2 * p) + 16],
                                              rhs=dnbf[0:1, 0:512], start=(p == 0), stop=False, skip_group_check=True),
                      waits=w_)
                    pl_mm = T(lambda e, p=p: e.matmul(out=plv, lhsT=e16bf[0:1, 16 * (2 * p + 1):16 * (2 * p + 1) + 16],
                                                      rhs=dnbf[0:1, 512:1024], start=False, stop=(p == 7), skip_group_check=True), ms=True)
                if qt == 3:
                    kb_dump["qf_dead"] = st_.v
                lnm = A_(lambda e: e.activation(out=rd_sb, in_=plv, func=AF.Ln, bias=0.0, scale=1.0),
                         waits=[(st_, pl_mm)], ms=True)
                lnm = A_(lambda e: e.activation(out=rdbf, in_=rd_sb, func=AF.Exp, bias=0.0, scale=-1.0),
                         waits=[(sa, lnm)], ms=True)
                pl_rd_a = lnm
                for g_ in range(8):
                    mr = T(lambda e, g_=g_: e.matmul(out=pR, lhsT=selbf[:, 128 * g_:128 * (g_ + 1)], rhs=rdbf,
                                                     start=True, stop=True, skip_group_check=True),
                           waits=[(sa, lnm), (sv, pbc)], ms=True)
                    w_ = [(st_, mr)]
                    if oproj_t:
                        w_.append((st_, oproj_t))
                    abf_ms = V(lambda e, g_=g_: e.tensor_tensor(
                        out=Abf[:, 512 * g_:512 * (g_ + 1)], in0=pR[:],
                        in1=A_sb[:, 512 * g_:512 * (g_ + 1)], op=ALU.mult), waits=w_, ms=True)
                    pbc = abf_ms
                if kbtrace:
                    kbdump({0: 1, 1: 4, 2: 6, 3: 7}[qt], [(sv, pbc)])
                for tg in range(4):
                    stc = s_ * 16 + qt * 4 + tg
                    par = tg % 2
                    w_ = [(sv, xa_user[par])]
                    if b_st2[par]:
                        w_.append((dg2[par], b_st2[par]))
                    xa_ld[par] = LD(lambda e, stc=stc, par=par: e.dma_start(out=xa[par], in_=srcv[:, stc, :]),
                                    waits=w_, sem=dx2[par])
                    for n2 in range(2):
                        w_m = [(sv, abf_ms), (sv, psB_rd_v[n2]), (sa, psB_rd[n2]), (dsp, wo_done)]
                        for p in range(8):
                            T(lambda e, tg=tg, n2=n2, p=p: e.matmul(
                                out=psB[n2], lhsT=Abf[:, 512 * p + 128 * tg:512 * p + 128 * (tg + 1)],
                                rhs=wo_b[:, p, 512 * n2:512 * (n2 + 1)],
                                start=(p == 0), stop=False, skip_group_check=True),
                                waits=w_m if p == 0 else ())
                        mo = T(lambda e, n2=n2: e.matmul(out=psB[n2], lhsT=ones128,
                                                         rhs=bor[0:1, 512 * n2:512 * (n2 + 1)],
                                                         start=False, stop=True, skip_group_check=True), ms=True)
                        oproj_t = mo
                        psB_rd[n2] = 0
                        psB_rd_v[n2] = V(lambda e, n2=n2, par=par: e.tensor_tensor(
                            out=xa[par][:, 512 * n2:512 * (n2 + 1)], in0=psB[n2],
                            in1=xa[par][:, 512 * n2:512 * (n2 + 1)], op=ALU.add),
                            waits=[(st_, mo), (dx2[par], xa_ld[par])], ms=True)
                    sv_stat = V(lambda e, stc=stc, par=par: e.reduce_sum(out=ssum[:, stc:stc + 1], in_=xa[par], axis=AX.X), ms=True)
                    xa_user[par] = sv_stat
                    stats_ms_a = A_(lambda e, stc=stc, par=par: e.activation(
                        out=A_sb[:, 0:1024], in_=xa[par], func=AF.Square, accum_out=ssq[:, stc:stc + 1]),
                        waits=[(sv, sv_stat)], ms=True)
                    b_st2[par] = GD(lambda e, stc=stc, par=par: e.dma_start(out=dstv[:, stc, :], in_=xa[par]),
                                    waits=[(sv, sv_stat), (sa, stats_ms_a)], sem=dg2[par])
                if kbtrace and qt < 2:
                    kbdump({0: 2, 1: 5}[qt], [(sa, stats_ms_a)])
        emit_attn.qf_dead = kb_dump.get("qf_dead", st_.v)
        return sv.v, sa.v, pbc

    if DEV_PHASES == "ffn1":
        fv, fa = emit_ffn(x0, out_d, collect_stats=False,
                          phase_wait_v=fin_v, phase_wait_a=fin_a,
                          mm0_wait=[(sv, wq_done_v)])
    elif DEV_PHASES == "attn":
        fv, fa = emit_ffn(x0, x1d, collect_stats=True,
                          phase_wait_v=fin_v, phase_wait_a=fin_a,
                          mm0_wait=[(sv, wq_done_v)])
        f1v, f1a = finalize_stats(fv, fa)
        av, aa, apbc = emit_attn(x1d, out_d, f1v, f1a,
                                 store_wait=emit_ffn.store_tot, pbc0=emit_ffn.last_pbc)
    else:
        fv, fa = emit_ffn(x0, x1d, collect_stats=True,
                          phase_wait_v=fin_v, phase_wait_a=fin_a,
                          mm0_wait=[(sv, wq_done_v)])
        f1v, f1a = finalize_stats(fv, fa)
        att_dst = out_d if DEV_PHASES == "attn2" else x2d
        av, aa, apbc = emit_attn(x1d, att_dst, f1v, f1a,
                                 store_wait=emit_ffn.store_tot, pbc0=emit_ffn.last_pbc)
        POST = os.environ.get("ATH_POST", "fin,rl,pre")
        f2v, f2a = (av, aa)
        if "fin" in POST:
            f2v, f2a = finalize_stats(av, aa, nsteps=int(os.environ.get("ATH_FSTEPS", "9")))
        rl0 = dw.v
        if "rl" in POST:
            early = [(st_, emit_attn.qf_dead), (dsp, dsp.v)]
            for j in (2, 3, 4, 5):
                LD(lambda e, j=j: e.dma_start(out=w1q[:, j * 4096:(j + 1) * 4096], in_=w1qs_v[j, :, :]),
                   waits=early)
                early = []
            for j in range(4, 32):
                LD(lambda e, j=j: e.dma_start(out=w2q[:, j * 1024:(j + 1) * 1024], in_=w2qs_v[j, :, :]))
            late = [(sv, av), (sa, aa)]
            for j in (0, 1, 6, 7):
                LD(lambda e, j=j: e.dma_start(out=w1q[:, j * 4096:(j + 1) * 4096], in_=w1qs_v[j, :, :]),
                   waits=late)
                late = []
            for j in range(4):
                LD(lambda e, j=j: e.dma_start(out=w2q[:, j * 1024:(j + 1) * 1024], in_=w2qs_v[j, :, :]))
        rl_done = dw.v
        if "pre" in POST and os.environ.get("ATH_PRE"):
            A_(lambda e: e.preload_act_table(AF.Silu), waits=[(sa, f2a)])
        if DEV_PHASES != "attn2":
            fv, fa = emit_ffn(x2d, out_d, collect_stats=False,
                              phase_wait_v=f2v, phase_wait_a=f2a,
                              extra_first=[(dg2[0], dg2[0].v), (dg2[1], dg2[1].v),
                                           (dg3[0], dg3[0].v), (dg3[1], dg3[1].v)],
                              pbc0=apbc, mm0_wait=[(dw, rl_done)])

    if dbg1 is not None:
        bs = getattr(emit_ffn, "b_store", {})
        for k, col in ([] if os.environ.get("ATH_SNAP") else [(0, 0), (3, 32), (7, 64)]):
            if (k, 1) in bs:
                GD(lambda e, col=col: e.dma_start(out=dbg4[:, 260 + col:260 + col + 32], in_=statr[:]),
                   waits=[(dg3[k % 2], bs[(k, 1)])])
        GD(lambda e: e.dma_start(out=dbg1[:, 0:4], in_=wsc[:]), waits=[(sv, sv.v), (sa, sa.v), (sp, sp.v), (st_, st_.v)])
        GD(lambda e: e.dma_start(out=dbg1[:, 4:36], in_=statm[:]))
        GD(lambda e: e.dma_start(out=dbg1[:, 36:68], in_=statr[:]))
        GD(lambda e: e.dma_start(out=dbg1[:, 68:84], in_=tmp[:]))
        GD(lambda e: e.dma_start(out=dbg1[:, 84:92], in_=consts[:]))
        GD(lambda e: e.dma_start(out=dbg1[:, 124:156], in_=ssq[:]))
        GD(lambda e: e.dma_start(out=dbg4[:, 0:64], in_=guard1[:]))
        GD(lambda e: e.dma_start(out=dbg4[:, 64:128], in_=guard2[:]))
        GD(lambda e: e.dma_start(out=dbg4[:, 128:144], in_=tmp[:]))
        GD(lambda e: e.dma_start(out=dbg4[:, 144:148], in_=wsc[:]))
        GD(lambda e: e.dma_start(out=dbg4[:, 148:156], in_=consts[:]))
        GD(lambda e: e.dma_start(out=dbg4[:, 156:188], in_=statm[:]))
        GD(lambda e: e.dma_start(out=dbg2[:, :], in_=hbuf))

        if not os.environ.get("ATH_KBTRACE"):
            GD(lambda e: e.dma_start(out=dbg3[:, :], in_=yqT[:]))

    pr.op("gpsimd", lambda e: e.nop(),
          waits=[(dgd, dgd.v), (dsp, dsp.v), (dsq, dsq.v), (dg2[0], dg2[0].v), (dg2[1], dg2[1].v),
                 (dg3[0], dg3[0].v), (dg3[1], dg3[1].v),
                 (dx2[0], dx2[0].v), (dx2[1], dx2[1].v),
                 (dx3[0], dx3[0].v), (dx3[1], dx3[1].v), (dw, dw.v),
                 (dw2[0], dw2[0].v), (dw2[1], dw2[1].v)])

    if os.environ.get("ATH_TAGDUMP"):
        with open("_tagdump.txt", "w") as f:
            for eng in ENGINES:
                for i, (fn, ws, inc, incv) in enumerate(pr.q[eng]):
                    wstr = ",".join(f"{s.name}>={v}" for s, v in ws)
                    istr = f"+{inc.name}" if inc is not None else ""
                    f.write(f"{eng} {i} [{wstr}] {istr}\n")

    block = stack.enter_context(nc.Block())

    namelog = open("_instmap.txt", "w") if os.environ.get("ATH_TAGDUMP") else None

    def runner(ops, ename=""):
        def run(eng):
            for i, (fn, waits, inc, incv) in enumerate(ops):
                for s, v in waits:
                    eng.wait_ge(s.h, v)
                instr = fn(eng)
                if inc is not None:
                    instr.then_inc(inc.h, incv)
                if namelog is not None:
                    namelog.write(f"{ename} {i} {getattr(instr, 'name', '?')}\n")
        return run

    block.sync(runner(pr.q["sync"], "sync"))
    block.vector(runner(pr.q["vector"], "vector"))
    block.scalar(runner(pr.q["scalar"], "scalar"))
    block.tensor(runner(pr.q["tensor"], "tensor"))
    block.gpsimd(runner(pr.q["gpsimd"], "gpsimd"))
    if namelog is not None:
        namelog.close()

    stack.close()
    return nc



def kernel(**inputs):
    x = np.asarray(inputs["L0_attn_hidden_states"])
    cos = np.asarray(inputs["L0_attn_cos"])
    sin = np.asarray(inputs["L0_attn_sin"])
    wq = np.asarray(inputs["L0_attn_wq"])
    bq = np.asarray(inputs["L0_attn_bq"])
    wk = np.asarray(inputs["L0_attn_wk"])
    wv = np.asarray(inputs["L0_attn_wv"])
    bv = np.asarray(inputs["L0_attn_bv"])
    wo = np.asarray(inputs["L0_attn_wo"])
    bo = np.asarray(inputs["L0_attn_bo"])
    w1 = np.asarray(inputs["L0_ffn_w1"])
    cwv = np.asarray(inputs["L0_ffn_conv_w"])
    cb = np.asarray(inputs["L0_ffn_conv_b"])
    w2 = np.asarray(inputs["L0_ffn_w2"])

    w1t = np.ascontiguousarray(w1.T)
    w2t = np.ascontiguousarray(w2.T)
    cwm = np.zeros((128, 128), dtype=np.float32)
    taps = cwv[:, 0, :]
    for tap in range(3):
        cwm[:, 32 * tap:32 * tap + 32] = taps[:, tap].reshape(32, 128).T
    cwm[:, 96:128] = cb.reshape(32, 128).T
    cost = np.ascontiguousarray(np.tile(cos.T, (2, 1)).astype(np.float32, copy=False))
    sint = np.ascontiguousarray(np.tile(sin.T, (2, 1)).astype(np.float32, copy=False))

    identf = np.eye(128, dtype=np.float32)
    onesf = np.ones((128, 128), dtype=np.float32)
    e16 = np.zeros((1, 256), dtype=np.float32)
    for h in range(16):
        e16[0, 16 * h + h] = 1
    selm = np.zeros((16, 1024), dtype=np.float32)
    for g in range(8):
        selm[2 * g, 128 * g:128 * g + 64] = 1
        selm[2 * g + 1, 128 * g + 64:128 * g + 128] = 1
    constsv = np.zeros((128, 8), dtype=np.float32)
    constsv[:, 0] = 1e-5
    constsv[:, 1] = 1.0 / 127.0
    constsv[:, 2] = -C_SHIFT
    constsv[:, 3] = EPS
    constsv[:, 4] = 0.5

    nc = build_nc()
    shared = dict(w1t=w1t, w2t=w2t, cwdrm=cwm, wq=wq, wk=wk, wv=wv, wo=wo,
                  bq=bq.reshape(1, -1), bv=bv.reshape(1, -1), bo=bo.reshape(1, -1),
                  cosd=cost, sind=sint, identfd=identf,
                  onesfd=onesf, e16d=e16, seld=selm, constsd=constsv)
    in_maps = []
    for c in range(NCORE):
        m = dict(shared)
        m["x"] = np.ascontiguousarray(x[2 * c:2 * c + 2].reshape(NTOK, D))
        in_maps.append(m)

    res = run_bass_kernel_spmd(nc, in_maps, core_ids=list(range(NCORE)))
    if os.environ.get("ATH_DEBUG"):
        kernel._dbg = res.results
    out = np.stack([np.asarray(r["out"]) for r in res.results])
    return np.ascontiguousarray(out.reshape(16, 2048, 1024).astype(np.float32, copy=False))


# version 17
# speedup vs baseline: 1.0025x; 1.0025x over previous
import os
import numpy as np
import ml_dtypes
import concourse.bass as bass
import concourse.mybir as mybir
from concourse.bass_utils import run_bass_kernel_spmd
from contextlib import ExitStack

F32 = mybir.dt.float32
BF16 = mybir.dt.bfloat16
I32 = mybir.dt.int32
AF = mybir.ActivationFunctionType
ALU = mybir.AluOpType
AX = mybir.AxisListType

NCORE = 8
SEQ = 2048
NSEQ = 2
NTOK = SEQ * NSEQ
D = 1024
DH = 4096
NH, HD = 16, 64
TOK = 256
NT = NTOK // TOK
TPS = SEQ // TOK
C_SHIFT = 12.0
RMAGIC = 12582912.0
EPS = 1e-5

ENGINES = ("sync", "vector", "scalar", "tensor", "gpsimd")
DEV_PHASES = os.environ.get("ATH_PHASES", "full")
CUT = int(os.environ.get("ATH_CUT", "9"))


class Sem:
    __slots__ = ("h", "v", "name")
    def __init__(self, h, name):
        self.h, self.v, self.name = h, 0, name


class Prog:
    def __init__(self):
        self.q = {e: [] for e in ENGINES}

    def op(self, eng, fn, waits=(), inc=None, incv=1):
        ws = tuple((s, v) for (s, v) in waits if v > 0)
        self.q[eng].append((fn, ws, inc, incv))
        if inc is not None:
            inc.v += incv
            return inc.v
        return None


def build_nc():
    nc = bass.Bass("TRN2", target_bir_lowering=False)

    x0 = nc.dram_tensor("x", [NTOK, D], F32, kind="ExternalInput").ap()
    w1t = nc.dram_tensor("w1t", [D, DH], F32, kind="ExternalInput").ap()
    w2t = nc.dram_tensor("w2t", [DH, D], F32, kind="ExternalInput").ap()
    cwd = nc.dram_tensor("cwdrm", [128, 128], F32, kind="ExternalInput").ap()
    wqd = nc.dram_tensor("wq", [D, D], F32, kind="ExternalInput").ap()
    wkd = nc.dram_tensor("wk", [D, D], F32, kind="ExternalInput").ap()
    wvd = nc.dram_tensor("wv", [D, D], F32, kind="ExternalInput").ap()
    wod = nc.dram_tensor("wo", [D, D], F32, kind="ExternalInput").ap()
    bqd = nc.dram_tensor("bq", [1, D], F32, kind="ExternalInput").ap()
    bvd = nc.dram_tensor("bv", [1, D], F32, kind="ExternalInput").ap()
    bod = nc.dram_tensor("bo", [1, D], F32, kind="ExternalInput").ap()
    cosd = nc.dram_tensor("cosd", [128, 1024], F32, kind="ExternalInput").ap()
    sind = nc.dram_tensor("sind", [128, 1024], F32, kind="ExternalInput").ap()
    
    identf_d = nc.dram_tensor("identfd", [128, 128], F32, kind="ExternalInput").ap()
    
    onesf_d = nc.dram_tensor("onesfd", [128, 128], F32, kind="ExternalInput").ap()
    e16_d = nc.dram_tensor("e16d", [1, 256], F32, kind="ExternalInput").ap()
    sel_d = nc.dram_tensor("seld", [16, 1024], F32, kind="ExternalInput").ap()
    consts_d = nc.dram_tensor("constsd", [128, 8], F32, kind="ExternalInput").ap()
    out_d = nc.dram_tensor("out", [NTOK, D], F32, kind="ExternalOutput").ap()
    dbg1 = dbg2 = dbg3 = dbg4 = None
    if os.environ.get("ATH_DEBUG"):
        dbg1 = nc.dram_tensor("dbg1", [128, 160], F32, kind="ExternalOutput").ap()
        dbg2 = nc.dram_tensor("dbg2", [128, 8256], F32, kind="ExternalOutput").ap()
        dbg3 = nc.dram_tensor("dbg3", [128, 6192], BF16, kind="ExternalOutput").ap()
        dbg4 = nc.dram_tensor("dbg4", [128, 360], F32, kind="ExternalOutput").ap()
    x1d = nc.dram_tensor("x1buf", [NTOK, D], F32, kind="Internal").ap()
    x2d = nc.dram_tensor("x2buf", [NTOK, D], F32, kind="Internal").ap()
    w1qs = nc.dram_tensor("w1qs", [D, DH], BF16, kind="Internal").ap()
    w2qs = nc.dram_tensor("w2qs", [DH, D], BF16, kind="Internal").ap()

    stack = ExitStack()
    sb = lambda n, s, d: stack.enter_context(nc.sbuf_tensor(n, s, d))

    AR = sb("arena", [128, 45120], F32)
    arb = AR[:].bitcast(BF16)

    def arf(off, n):
        return AR[:, off:off + n]
    def arh(off_f32, n_bf16):
        return arb[:, 2 * off_f32:2 * off_f32 + n_bf16]

    w1q = arh(0, 32768)
    w2q = arh(16384, 32768)
    hbuf = arf(32768, 8256)
    hbuf_i = hbuf.bitcast(I32)
    xio = [arf(41024, 2048), arf(43072, 2048)]

    stg = sb("stg", [128, 1024], F32)
    stg_iv = stg[:].bitcast(I32)
    blk = sb("blk", [128, 256], BF16)
    blkint = sb("blkint", [128, 256], I32)
    sqint = sb("sqint", [128, 256], I32)
    yqT = sb("yqT", [128, 6192], BF16)
    r1row = sb("r1row", [1, 774], F32)
    cvb_t = sb("cvb", [128, 512], F32)
    sqbf = cvb_t[:].bitcast(BF16)
    r1bc = sb("r1bc", [128, 258], F32)
    r2bc = r1bc[:, 0:256]
    r2row = sb("r2row", [1, 256], F32)
    accd = sb("accd", [128, 256], F32)
    accp = sb("accp", [128, 256], F32)
    ssum = sb("ssum", [128, 32], F32)
    ssq = sb("ssq", [128, 32], F32)
    statm = sb("statm", [128, 32], F32)
    guard1 = sb("guard1", [128, 64], F32)
    statr = sb("statr", [128, 32], F32)
    guard2 = sb("guard2", [128, 64], F32)
    tmp = sb("tmp", [128, 16], F32)
    wsc = sb("wsc", [128, 4], F32)
    sc11 = sb("sc11", [1, 4], F32)
    consts = sb("consts", [128, 8], F32)
    cw = sb("cwsb", [128, 128], F32)
    identbf = sb("identbf", [128, 128], BF16)
    identf = sb("identf", [128, 128], F32)
    onesf = sb("onesf", [128, 128], F32)

    CL_CLIP, CL_R127, CL_NSH, CL_EPS, CL_HALF = 0, 1, 2, 3, 4

    P = stack.enter_context(nc.psum_tensor("P", [128, 4096], F32))
    ph = [P[:, 0:258], P[:, 512:770]]
    po = [P[:, 1024:2048], P[:, 2048:3072]]
    ptrb = P[:, 3072:3584].bitcast(BF16)
    pbcf = P[:, 3584:4096]

    def mksem(name):
        return Sem(stack.enter_context(nc.semaphore(name)), name)
    dx2 = [mksem("dxa"), mksem("dxb")]
    dx3 = [mksem("dxc"), mksem("dxd")]
    dsq = mksem("dsq")
    dg3 = [mksem("dge"), mksem("dgf")]
    dw = mksem("dw")
    dw2 = [mksem("dwa"), mksem("dwb")]
    dg2 = [mksem("dga"), mksem("dgb")]
    dsp = mksem("dsp")
    dgd = mksem("dgd")
    sv = mksem("sv")
    sa = mksem("sa")
    sp = mksem("sp")
    st_ = mksem("st")

    pr = Prog()

    def V(fn, waits=(), ms=False):
        return pr.op("vector", fn, tuple(waits) + ((sv, sv.v),), sv)
    def A_(fn, waits=(), ms=False):
        return pr.op("scalar", fn, tuple(waits) + ((sa, sa.v),), sa)
    def T(fn, waits=(), ms=False):
        return pr.op("tensor", fn, waits, st_ if ms else None)
    def G(fn, waits=(), ms=False):
        return pr.op("gpsimd", fn, tuple(waits) + ((sp, sp.v),), sp)
    def LD(fn, waits=(), sem=None):
        sem = sem or dw
        return pr.op("sync", fn, waits, sem, incv=16)
    def GD(fn, waits=(), sem=None):
        return pr.op("gpsimd", fn, waits, sem or dgd, incv=16)

    LD(lambda e: e.dma_start(out=consts[:], in_=consts_d))
    LD(lambda e: e.dma_start(out=cw[:], in_=cwd))
    idl = LD(lambda e: e.dma_start(out=identf[:], in_=identf_d))
    cl = LD(lambda e: e.dma_start(out=onesf[:], in_=onesf_d))
    V(lambda e: e.tensor_copy(out=identbf[:], in_=identf[:]), waits=[(dw, cl)])
    V(lambda e: e.memset(guard1[:], 0.0))
    V(lambda e: e.memset(guard2[:], 0.0))

    x0v = x0.rearrange("(s p) d -> p s d", p=128)
    stat_v = {}
    stat_a = {}
    ld_ms = {}
    for s_ in range(32):
        b = s_ % 2
        w_ = []
        if s_ >= 2:
            w_ = [(sv, stat_v[s_ - 2]), (sa, stat_a[s_ - 2])]
        ld_ms[s_] = LD(lambda e, s_=s_, b=b: e.dma_start(out=xio[b][:, 0:1024], in_=x0v[:, s_, :]),
                       waits=w_, sem=dx2[b])
        stat_v[s_] = V(lambda e, s_=s_, b=b: e.reduce_sum(out=ssum[:, s_:s_ + 1], in_=xio[b][:, 0:1024], axis=AX.X),
                       waits=[(dx2[b], ld_ms[s_])], ms=True)
        wsq = [(dx2[b], ld_ms[s_])]
        if s_ >= 1:
            wsq.append((sa, stat_a[s_ - 1]))
        stat_a[s_] = A_(lambda e, s_=s_, b=b: e.activation(out=stg[:, 0:1024], in_=xio[b][:, 0:1024], func=AF.Square, accum_out=ssq[:, s_:s_ + 1]),
                        waits=wsq, ms=True)
    stats_v, stats_a = stat_v[31], stat_a[31]

    wload = [hbuf[:, 0:4096], hbuf[:, 4096:8192]]
    wload_i = [hbuf_i[:, 0:4096], hbuf_i[:, 4096:8192]]
    w1tv = w1t.rearrange("(c p) h -> c p h", p=128)
    w2tv = w2t.rearrange("(c p) h -> c p h", p=128)

    red_ms = {}
    for j in range(8):
        b = j % 2
        w_ = [] if j < 2 else [(sv, red_ms[j - 2])]
        ld = LD(lambda e, j=j, b=b: e.dma_start(out=wload[b], in_=w1tv[j, :, :]), waits=w_, sem=dw2[b])
        red_ms[j] = V(lambda e, j=j, b=b: e.reduce_sum(out=tmp[:, j:j + 1], in_=wload[b], axis=AX.X, apply_absolute_value=True),
                      waits=[(dw2[b], ld)], ms=True)
    V(lambda e: e.reduce_sum(out=tmp[:, 8:9], in_=tmp[:, 0:8], axis=AX.X), waits=[(sv, red_ms[7])])
    for j in range(32):
        b = j % 2
        w_ = [(sv, red_ms[j - 2])] if j >= 2 else [(sv, red_ms[7] if b else red_ms[6])]
        ld = LD(lambda e, j=j, b=b: e.dma_start(out=wload[b][:, 0:1024], in_=w2tv[j, :, :]), waits=w_, sem=dw2[b])
        acc = 12 + b
        if j < 2:
            red_ms[j] = V(lambda e, b=b, acc=acc: e.reduce_sum(out=tmp[:, acc:acc + 1], in_=wload[b][:, 0:1024], axis=AX.X, apply_absolute_value=True),
                          waits=[(dw2[b], ld)], ms=True)
        else:
            rr = V(lambda e, b=b: e.reduce_sum(out=tmp[:, 10 + b:11 + b], in_=wload[b][:, 0:1024], axis=AX.X, apply_absolute_value=True),
                   waits=[(dw2[b], ld)], ms=True)
            red_ms[j] = V(lambda e, b=b, acc=acc: e.tensor_tensor(out=tmp[:, acc:acc + 1], in0=tmp[:, acc:acc + 1], in1=tmp[:, 10 + b:11 + b], op=ALU.add),
                          waits=[(sv, rr)], ms=True)
    vsum2 = V(lambda e: e.tensor_tensor(out=tmp[:, 9:10], in0=tmp[:, 12:13], in1=tmp[:, 13:14], op=ALU.add),
              waits=[(sv, red_ms[31]), (sv, red_ms[30])], ms=True)

    tps = T(lambda e: e.matmul(out=pbcf[0:1, 0:1], lhsT=tmp[:, 8:9], rhs=onesf[:, 0:1], start=True, stop=True, skip_group_check=True),
            waits=[(sv, vsum2), (dw, cl)])
    tps = T(lambda e: e.matmul(out=pbcf[0:1, 1:2], lhsT=tmp[:, 9:10], rhs=onesf[:, 0:1], start=True, stop=True, skip_group_check=True), ms=True)
    vm = V(lambda e: e.tensor_scalar(out=sc11[0:1, 0:2], in0=pbcf[0:1, 0:2], scalar1=1.0 / (DH * D), scalar2=1e-5, op0=ALU.mult, op1=ALU.max),
           waits=[(st_, tps)], ms=True)
    tbc = T(lambda e: e.matmul(out=pbcf[:, 2:4], lhsT=onesf[0:1, :], rhs=sc11[0:1, 0:2], start=True, stop=True, skip_group_check=True),
            waits=[(sv, vm)], ms=True)
    V(lambda e: e.tensor_copy(out=wsc[:, 0:1], in_=pbcf[:, 2:3]), waits=[(st_, tbc)])
    vx = V(lambda e: e.tensor_copy(out=wsc[:, 2:3], in_=pbcf[:, 3:4]), ms=True)
    vx = V(lambda e: e.reciprocal(out=wsc[:, 1:2], in_=wsc[:, 0:1]), waits=[(sv, vx)], ms=True)
    wsc_rdy = V(lambda e: e.reciprocal(out=wsc[:, 3:4], in_=wsc[:, 2:3]), waits=[(sv, vx)], ms=True)
    wsc_rdy = V(lambda e: e.tensor_copy(out=tmp[:, 15:16], in_=wsc[:, 3:4]), waits=[(sv, wsc_rdy)], ms=True)
    pbc_free = wsc_rdy

    w1qs_v = w1qs.rearrange("(c p) h -> c p h", p=128)
    w2qs_v = w2qs.rearrange("(c p) h -> c p h", p=128)
    q_ms = {}
    for j in range(8):
        b = j % 2
        w_ = [(sv, wsc_rdy)] if j < 2 else [(sv, q_ms[j - 2])]
        ld = LD(lambda e, j=j, b=b: e.dma_start(out=wload[b], in_=w1tv[j, :, :]), waits=w_, sem=dw2[b])
        V(lambda e, j=j, b=b: e.tensor_scalar(out=wload[b], in0=wload[b], scalar1=wsc[:, 1:2], scalar2=None, op0=ALU.mult),
          waits=[(dw2[b], ld)])
        V(lambda e, j=j, b=b: e.tensor_scalar(out=wload[b], in0=wload[b], scalar1=RMAGIC, scalar2=RMAGIC, op0=ALU.add, op1=ALU.subtract))
        q_ms[j] = V(lambda e, j=j, b=b: e.tensor_scalar(out=w1q[:, j * 4096:(j + 1) * 4096], in0=wload[b], scalar1=-1.0, scalar2=1.0, op0=ALU.max, op1=ALU.min), ms=True)
        GD(lambda e, j=j: e.dma_start(out=w1qs_v[j, :, :], in_=w1q[:, j * 4096:(j + 1) * 4096]),
           waits=[(sv, q_ms[j])], sem=dsp)
    for j in range(32):
        b = j % 2
        w_ = [(sv, q_ms[6 + b])] if j < 2 else [(sv, q_ms[8 + j - 2])]
        ld = LD(lambda e, j=j, b=b: e.dma_start(out=wload[b][:, 0:1024], in_=w2tv[j, :, :]), waits=w_, sem=dw2[b])
        V(lambda e, j=j, b=b: e.tensor_scalar(out=wload[b][:, 0:1024], in0=wload[b][:, 0:1024], scalar1=wsc[:, 3:4], scalar2=None, op0=ALU.mult),
          waits=[(dw2[b], ld)])
        V(lambda e, j=j, b=b: e.tensor_scalar(out=wload[b][:, 0:1024], in0=wload[b][:, 0:1024], scalar1=RMAGIC, scalar2=RMAGIC, op0=ALU.add, op1=ALU.subtract))
        q_ms[8 + j] = V(lambda e, j=j, b=b: e.tensor_scalar(out=w2q[:, j * 1024:(j + 1) * 1024], in0=wload[b][:, 0:1024], scalar1=-1.0, scalar2=1.0, op0=ALU.max, op1=ALU.min), ms=True)
        GD(lambda e, j=j: e.dma_start(out=w2qs_v[j, :, :], in_=w2q[:, j * 1024:(j + 1) * 1024]),
           waits=[(sv, q_ms[8 + j])], sem=dsp)
    wq_done_v = q_ms[8 + 31]
    wspill = dsp.v

    def finalize_stats(wv_, wa_, nsteps=9):
        a = wa_
        if nsteps >= 1 and os.environ.get("ATH_PRE"):
            a = A_(lambda e: e.preload_act_table(AF.Exp), waits=[(sa, wa_)], ms=True)
        v = wv_
        if os.environ.get("ATH_FSKIPV"):
            nsteps_v = 0
        else:
            nsteps_v = nsteps
        if nsteps_v >= 2:
            v = V(lambda e: e.tensor_scalar(out=statm[:], in0=ssum[:], scalar1=1.0 / D, scalar2=None, op0=ALU.mult),
                  waits=[(sv, wv_), (sa, wa_)], ms=True)
        if nsteps_v >= 3:
            v = V(lambda e: e.tensor_tensor(out=statr[:], in0=statm[:], in1=statm[:], op=ALU.mult), waits=[(sv, v)], ms=True)
        if nsteps_v >= 4:
            v = V(lambda e: e.tensor_scalar(out=ssq[:], in0=ssq[:], scalar1=1.0 / D, scalar2=None, op0=ALU.mult), waits=[(sv, v)], ms=True)
        if nsteps_v >= 5:
            v = V(lambda e: e.tensor_tensor(out=statr[:], in0=ssq[:], in1=statr[:], op=ALU.subtract), waits=[(sv, v)], ms=True)
        if nsteps >= 6:
            lnf = {"sq": AF.Square, "ln": AF.Ln}[os.environ.get("ATH_LNF", "ln")]
            lnb = 0.0 if os.environ.get("ATH_LNB") else consts[:, CL_EPS:CL_EPS + 1]
            lno = tmp[:, 0:1] if os.environ.get("ATH_LNO") else statr[:]
            lni = statr[:, 0:1] if os.environ.get("ATH_LNO") else statr[:]
            a = A_(lambda e: e.activation(out=lno, in_=lni, func=lnf, bias=lnb, scale=1.0),
               waits=[(sv, v)], ms=True)
        if nsteps >= 7:
            a = A_(lambda e: e.activation(out=statr[:], in_=statr[:], func=AF.Exp, bias=0.0, scale=-0.5), waits=[(sa, a)], ms=True)
        return sv.v, a

    fin_v, fin_a = finalize_stats(max(stats_v, wsc_rdy), stats_a)
    if dbg1 is not None and DEV_PHASES == "ffn1":
        GD(lambda e: e.dma_start(out=dbg1[:, 92:124], in_=statr[:]), waits=[(sa, fin_a)])

    def emit_ffn(src, dst, collect_stats, phase_wait_v, phase_wait_a, extra_first=(), pbc0=None, mm0_wait=()):
        srcv = src.rearrange("(t g p) d -> t p g d", p=128, g=2)
        dstv = dst.rearrange("(t g p) d -> t p g d", p=128, g=2)

        if os.environ.get("ATH_PRE"):
            A_(lambda e: e.preload_act_table(AF.Silu), waits=[(sa, phase_wait_a)])

        x_ld = {}
        x_ld2 = {}
        a_v = {}
        a_a = {}
        a_t = {}
        b_store = {}
        b_stats_a = {}
        b_mm1_done = {}
        evac_ms = {}
        po_free = None
        nonloc = dict(pbc=pbc0 if pbc0 is not None else pbc_free, first_mm=True)

        def slot_view(slot):
            return yqT[:, slot * 2064:(slot + 1) * 2064].rearrange("p (c n) -> p c n", n=258)

        def a_part(i):
            slot = i % 3
            b = i % 2
            sfirst = (i % TPS) == 0
            slast = (i % TPS) == TPS - 1
            w0 = [(dx2[i % 2], x_ld[i])]
            if i >= 1:
                w0.append((st_, a_t[i - 1]))
            tg = {}
            cp_ms = {}
            qs_rdy_g = {}
            for g in range(2):
                stc = 2 * i + g
                wv_ = w0 if g == 0 else [(dx3[i % 2], x_ld2[i]), (sp, cp_ms[(0, 7)])]
                V(lambda e, b=b, g=g, stc=stc: e.tensor_scalar(
                    out=stg[:], in0=xio[b][:, g * 1024:(g + 1) * 1024],
                    scalar1=statm[:, stc:stc + 1], scalar2=None, op0=ALU.subtract, op1=ALU.max,
                    accum_out=tmp[:, g:g + 1]),
                  waits=wv_)
                V(lambda e, b=b, g=g, stc=stc: e.tensor_scalar(
                    out=stg[:], in0=xio[b][:, g * 1024:(g + 1) * 1024],
                    scalar1=statm[:, stc:stc + 1], scalar2=None, op0=ALU.subtract, op1=ALU.min,
                    accum_out=tmp[:, 2 + g:3 + g]))
                V(lambda e, g=g: e.tensor_scalar(out=tmp[:, 2 + g:3 + g], in0=tmp[:, 2 + g:3 + g], scalar1=-1.0, scalar2=None, op0=ALU.mult))
                V(lambda e, g=g: e.tensor_tensor(out=tmp[:, g:g + 1], in0=tmp[:, g:g + 1], in1=tmp[:, 2 + g:3 + g], op=ALU.max))
                V(lambda e, g=g: e.tensor_scalar(out=tmp[:, 2 + g:3 + g], in0=tmp[:, g:g + 1], scalar1=1e-5, scalar2=None, op0=ALU.max))
                vv = V(lambda e, g=g: e.reciprocal(out=tmp[:, 4 + g:5 + g], in_=tmp[:, 2 + g:3 + g]), ms=True)
                vv = V(lambda e, g=g: e.tensor_scalar(out=tmp[:, 4 + g:5 + g], in0=tmp[:, 4 + g:5 + g], scalar1=127.0, scalar2=None, op0=ALU.mult), waits=[(sv, vv)], ms=True)
                V(lambda e, g=g, i=i, stc=stc: e.tensor_tensor(out=tmp[:, 6 + g:7 + g], in0=statr[:, stc:stc + 1], in1=tmp[:, g:g + 1], op=ALU.mult))
                V(lambda e, g=g: e.tensor_scalar(out=tmp[:, 6 + g:7 + g], in0=tmp[:, 6 + g:7 + g], scalar1=1e-5, scalar2=None, op0=ALU.max))
                qs_rdy_g[g] = V(lambda e, g=g: e.tensor_scalar(out=tmp[:, 6 + g:7 + g], in0=tmp[:, 6 + g:7 + g], scalar1=wsc[:, 0:1], scalar2=1.0 / 127.0, op0=ALU.mult, op1=ALU.mult), ms=True)
                w_q = [(sv, qs_rdy_g[g])]
                if g == 1:
                    w_q.append((sp, cp_ms[(0, 7)]))
                qv = V(lambda e, g=g, b=b, stc=stc: e.tensor_scalar(
                    out=stg[:], in0=xio[b][:, g * 1024:(g + 1) * 1024],
                    scalar1=statm[:, stc:stc + 1], scalar2=tmp[:, 4 + g:5 + g],
                    op0=ALU.subtract, op1=ALU.mult),
                    waits=w_q, ms=True)
                qv = V(lambda e: e.tensor_scalar(out=stg[:], in0=stg[:], scalar1=RMAGIC, scalar2=RMAGIC, op0=ALU.add, op1=ALU.subtract), ms=True)
                prev_ev = a_a.get((i, g - 1)) or a_a.get((i - 1, 1))
                for j in range(8):
                    w_c = [(sv, qv)]
                    if (g, j) not in ((0, 0), (0, 1)):
                        pj = (g, j - 2) if j >= 2 else (g - 1, 6 + j)
                        w_c.append((st_, tg.get(pj, 0)))
                    G(lambda e, j=j: e.tensor_copy(out=blkint[:, 128 * (j % 2):128 * (j % 2) + 128], in_=stg[:, 128 * j:128 * (j + 1)]),
                      waits=w_c, ms=True)
                    cp_ms[(g, j)] = G(lambda e, j=j: e.tensor_copy(out=blk[:, 128 * (j % 2):128 * (j % 2) + 128], in_=blkint[:, 128 * (j % 2):128 * (j % 2) + 128]),
                                      ms=True)
                    w_t = [(sp, cp_ms[(g, j)])]
                    if j == 0 and prev_ev:
                        w_t.append((sa, prev_ev))
                    tg[(g, j)] = T(lambda e, g=g, j=j: e.transpose(
                        out=ptrb[:, 128 * j:128 * (j + 1)],
                        in_=blk[:, 128 * (j % 2):128 * (j % 2) + 128],
                        identity=identbf[:]),
                        waits=w_t, ms=True)
                dest = slot_view(slot)[:, :, 1 + 128 * g:1 + 128 * g + 128]
                a_a[(i, g)] = A_(lambda e, dest=dest: e.copy(out=dest, in_=ptrb[:].rearrange("p (c n) -> p c n", n=128)),
                                 waits=[(st_, tg[(g, 7)])], ms=True)
            qs_rdy = qs_rdy_g[1]
            w_ = [(sv, qs_rdy), (sv, nonloc["pbc"])]
            T(lambda e: e.transpose(out=pbcf[0:1, 0:128], in_=tmp[:, 6:7], identity=identf[:]), waits=w_)
            tr2 = T(lambda e: e.transpose(out=pbcf[0:1, 128:256], in_=tmp[:, 7:8], identity=identf[:]), ms=True)
            ev = V(lambda e, slot=slot: e.tensor_copy(out=r1row[0:1, slot * 258 + 1:slot * 258 + 257], in_=pbcf[0:1, 0:256]),
                   waits=[(st_, tr2)], ms=True)
            nonloc["pbc"] = ev
            hv = [(sa, a_a[(i, 1)]), (st_, b_mm1_done.get(i - 2, 0))]
            if sfirst:
                V(lambda e, slot=slot: e.memset(slot_view(slot)[:, :, 0:1], 0.0), waits=hv, ms=True)
                V(lambda e, slot=slot: e.memset(r1row[0:1, slot * 258:slot * 258 + 1], 1.0), ms=True)
            else:
                prev = (i - 1) % 3
                V(lambda e, slot=slot, prev=prev: e.tensor_copy(
                    out=slot_view(prev)[:, :, 257:258], in_=slot_view(slot)[:, :, 1:2]), waits=hv, ms=True)
                V(lambda e, slot=slot, prev=prev: e.tensor_copy(out=r1row[0:1, prev * 258 + 257:prev * 258 + 258], in_=r1row[0:1, slot * 258 + 1:slot * 258 + 2]), ms=True)
            if slast:
                V(lambda e, slot=slot: e.memset(slot_view(slot)[:, :, 257:258], 0.0), ms=True)
                V(lambda e, slot=slot: e.memset(r1row[0:1, slot * 258 + 257:slot * 258 + 258], 1.0), ms=True)
            else:
                nxt = (i + 1) % 3
                V(lambda e, slot=slot, nxt=nxt: e.tensor_copy(
                    out=slot_view(nxt)[:, :, 0:1], in_=slot_view(slot)[:, :, 256:257]), ms=True)
                V(lambda e, slot=slot, nxt=nxt: e.tensor_copy(out=r1row[0:1, nxt * 258:nxt * 258 + 1], in_=r1row[0:1, slot * 258 + 256:slot * 258 + 257]), ms=True)
            a_v[i] = sv.v
            a_t[i] = tg[(1, 7)]

        def b_part(i, halo_v):
            slot = i % 3
            b = i % 2
            w_ = [(sv, halo_v), (sv, nonloc["pbc"]), (sa, a_a[(i, 1)])]
            tb = T(lambda e, slot=slot: e.matmul(out=pbcf[:, 0:258], lhsT=onesf[0:1, :], rhs=r1row[0:1, slot * 258:slot * 258 + 258], start=True, stop=True, skip_group_check=True),
                   waits=w_, ms=True)
            r1bc_ms = V(lambda e: e.tensor_copy(out=r1bc[:], in_=pbcf[:, 0:258]), waits=[(st_, tb)], ms=True)
            nonloc["pbc"] = r1bc_ms

            silu_ms = {}
            tree_ms = {}
            mm1_ms = {}
            if CUT >= 1:
              for c in range(32):
                bank = c % 2
                w_mm = [(sv, evac_ms.get(bank, 0))]
                if c == 0:
                    w_mm.append((sv, r1bc_ms))
                    if b_stats_a.get(i - 1):
                        w_mm.append((sa, b_stats_a[i - 1]))
                    if nonloc.get("first_mm"):
                        w_mm += list(mm0_wait)
                        nonloc["first_mm"] = False
                for d_ in range(8):
                    mm1_ms[c] = T(lambda e, c=c, d_=d_, bank=bank, slot=slot: e.matmul(
                        out=ph[bank], lhsT=w1q[:, d_ * 4096 + c * 128:d_ * 4096 + (c + 1) * 128],
                        rhs=yqT[:, slot * 2064 + d_ * 258:slot * 2064 + (d_ + 1) * 258],
                        start=(d_ == 0), stop=(d_ == 7), skip_group_check=True),
                        waits=w_mm if d_ == 0 else (), ms=(d_ == 7))
                if c == 31:
                    b_mm1_done[i] = mm1_ms[c]
                hc = hbuf[:, c * 258:(c + 1) * 258]
                ev = V(lambda e, hc=hc, bank=bank: e.tensor_tensor(out=hc, in0=ph[bank], in1=r1bc[:], op=ALU.mult),
                       waits=[(st_, mm1_ms[c])], ms=True)
                evac_ms[bank] = ev
                if CUT >= 2:
                    cvb = cvb_t[:, 256 * (c % 2):256 * (c % 2) + 256]
                    w_t1 = [(sv, ev)]
                    if c >= 2:
                        w_t1.append((sv, tree_ms[c - 2]))
                    t1 = A_(lambda e, hc=hc, cvb=cvb, c=c: e.activation(
                        out=cvb, in_=hc[:, 1:257], func=AF.Identity,
                        bias=cw[:, 96 + (c % 32):97 + (c % 32)], scale=cw[:, 32 + (c % 32):33 + (c % 32)]),
                        waits=w_t1, ms=True)
                    t2 = V(lambda e, hc=hc, cvb=cvb, c=c: e.scalar_tensor_tensor(
                        out=cvb, in0=hc[:, 0:256], scalar=cw[:, (c % 32):(c % 32) + 1], in1=cvb, op0=ALU.mult, op1=ALU.add),
                        waits=[(sa, t1)], ms=True)
                    if not os.environ.get("ATH_POOLTAP"):
                        t3 = V(lambda e, hc=hc, cvb=cvb, c=c: e.scalar_tensor_tensor(
                            out=cvb, in0=hc[:, 2:258], scalar=cw[:, 64 + (c % 32):65 + (c % 32)], in1=cvb, op0=ALU.mult, op1=ALU.add),
                            waits=[(sv, t2)], ms=True)
                        A_(lambda e, hc=hc, cvb=cvb: e.activation(out=hc[:, 1:257], in_=cvb, func=AF.Silu),
                           waits=[(sv, t3)])
                    else:
                        G(lambda e, hc=hc, c=c: e.tensor_scalar(
                            out=accp[:], in0=hc[:, 2:258], scalar1=cw[:, 64 + (c % 32):65 + (c % 32)], scalar2=None, op0=ALU.mult),
                          waits=[(sv, ev)])
                        t3 = G(lambda e, cvb=cvb: e.tensor_tensor(out=cvb, in0=cvb, in1=accp[:], op=ALU.add),
                               waits=[(sv, t2)], ms=True)
                        A_(lambda e, hc=hc, cvb=cvb: e.activation(out=hc[:, 1:257], in_=cvb, func=AF.Silu),
                           waits=[(sp, t3)])
                    silu_ms[c] = A_(lambda e, hc=hc, cvb=cvb: e.activation(out=cvb, in_=hc[:, 1:257], func=AF.Abs), ms=True)
                    if c == 0:
                        tree_ms[c] = V(lambda e, cvb=cvb: e.tensor_copy(out=accd[:], in_=cvb),
                                       waits=[(sa, silu_ms[c])], ms=True)
                    else:
                        tree_ms[c] = V(lambda e, cvb=cvb: e.tensor_tensor(out=accd[:], in0=accd[:], in1=cvb, op=ALU.max),
                                       waits=[(sa, silu_ms[c])], ms=True)
            outsc_rdy = 0
            if CUT >= 3:
                comb = tree_ms[31]
                w_ = [(sv, comb), (sv, nonloc["pbc"])]
                T(lambda e: e.transpose(out=pbcf[:, 0:128], in_=accd[:, 0:128], identity=identf[:]), waits=w_)
                tt = T(lambda e: e.transpose(out=pbcf[:, 128:256], in_=accd[:, 128:256], identity=identf[:]), ms=True)
                rd1 = V(lambda e: e.reduce_max(out=tmp[:, 8:9], in_=pbcf[:, 0:128], axis=AX.X), waits=[(st_, tt)], ms=True)
                rd2 = V(lambda e: e.reduce_max(out=tmp[:, 9:10], in_=pbcf[:, 128:256], axis=AX.X), waits=[(sv, rd1)], ms=True)
                vv = V(lambda e: e.tensor_scalar(out=tmp[:, 10:12], in0=tmp[:, 8:10], scalar1=1e-5, scalar2=None, op0=ALU.max), waits=[(sv, rd2)], ms=True)
                vv = V(lambda e: e.reciprocal(out=tmp[:, 12:14], in_=tmp[:, 10:12]), waits=[(sv, vv)], ms=True)
                qs2_rdy = V(lambda e: e.tensor_scalar(out=tmp[:, 12:14], in0=tmp[:, 12:14], scalar1=127.0, scalar2=None, op0=ALU.mult), waits=[(sv, vv)], ms=True)
                outsc_rdy = V(lambda e: e.tensor_scalar(out=tmp[:, 14:16], in0=tmp[:, 10:12], scalar1=wsc[:, 2:3], scalar2=0.5 / 127.0, op0=ALU.mult, op1=ALU.mult), waits=[(sv, qs2_rdy)], ms=True)
                tq = T(lambda e: e.transpose(out=pbcf[0:1, 256:384], in_=tmp[:, 12:13], identity=identf[:]), waits=[(sv, qs2_rdy)])
                tq = T(lambda e: e.transpose(out=pbcf[0:1, 384:512], in_=tmp[:, 13:14], identity=identf[:]), ms=True)
                evq = V(lambda e: e.tensor_copy(out=r2row[0:1, 0:256], in_=pbcf[0:1, 256:512]), waits=[(st_, tq)], ms=True)
                tbc2 = T(lambda e: e.matmul(out=pbcf[:, 0:256], lhsT=onesf[0:1, :], rhs=r2row[0:1, 0:256], start=True, stop=True, skip_group_check=True),
                         waits=[(sv, evq)], ms=True)
                r2bc_ms = V(lambda e: e.tensor_copy(out=r2bc, in_=pbcf[:, 0:256]), waits=[(st_, tbc2)], ms=True)
                nonloc["pbc"] = r2bc_ms
            ev = r1bc_ms
            if CUT >= 4:
                accf = [accd[:], accp[:]]
                mm2_ms = {}
                qt_ms = {}
                i2_ms = {}
                for c in range(32):
                    sq = sqbf[:, 256 * (c % 2):256 * (c % 2) + 256]
                    acc = accf[c % 2]
                    w_ = [(sv, r2bc_ms)] if c < 2 else []
                    V(lambda e, c=c, acc=acc: e.tensor_tensor(
                        out=acc, in0=hbuf[:, c * 258 + 1:c * 258 + 257], in1=r2bc, op=ALU.mult),
                        waits=w_, ms=True)
                    w_ = [] if c < 1 else [(sp, i2_ms[c - 1])]
                    qt_ms[c] = V(lambda e, acc=acc: e.tensor_scalar(
                        out=sqint[:], in0=acc, scalar1=RMAGIC, scalar2=RMAGIC, op0=ALU.add, op1=ALU.subtract),
                        waits=w_, ms=True)
                    w_ = [(sv, qt_ms[c])]
                    if c >= 2:
                        w_.append((st_, mm2_ms[c - 2]))
                    i2_ms[c] = G(lambda e, c=c, sq=sq: e.tensor_copy(out=sq, in_=sqint[:]),
                                 waits=w_, ms=True)
                    for g in range(2):
                        for n2 in range(2):
                            w_m = [(sp, i2_ms[c])]
                            if c == 0 and g == 0 and n2 == 0 and po_free is not None:
                                w_m.append((sv, po_free))
                            mm2_ms[c] = T(lambda e, c=c, g=g, n2=n2, sq=sq: e.matmul(
                                out=po[g][:, n2 * 512:(n2 + 1) * 512],
                                lhsT=sq[:, g * 128:(g + 1) * 128],
                                rhs=w2q[:, c * 1024 + n2 * 512:c * 1024 + n2 * 512 + 512],
                                start=(c == 0), stop=(c == 31), skip_group_check=True),
                                waits=w_m if (g == 0 and n2 == 0) else (), ms=(g == 1 and n2 == 1))
                ev = None
            stat_aa = 0
            for g in range(2):
                if CUT >= 4:
                    for n2 in range(2):
                        ev = V(lambda e, g=g, n2=n2, b=b: e.scalar_tensor_tensor(
                            out=xio[b][:, g * 1024 + n2 * 512:g * 1024 + n2 * 512 + 512],
                            in0=po[g][:, n2 * 512:(n2 + 1) * 512],
                            scalar=tmp[:, 14 + g:15 + g],
                            in1=xio[b][:, g * 1024 + n2 * 512:g * 1024 + n2 * 512 + 512],
                            op0=ALU.mult, op1=ALU.add),
                            waits=[(st_, mm2_ms[31]), (sv, outsc_rdy)] if (g == 0 and n2 == 0) else (), ms=True)
                stat_vv = 0
                if collect_stats:
                    stc = 2 * i + g
                    stat_vv = V(lambda e, g=g, b=b, stc=stc: e.reduce_sum(out=ssum[:, stc:stc + 1], in_=xio[b][:, g * 1024:(g + 1) * 1024], axis=AX.X), ms=True)
                    stat_aa = A_(lambda e, g=g, b=b, stc=stc: e.activation(out=hbuf[:, 0:1024], in_=xio[b][:, g * 1024:(g + 1) * 1024], func=AF.Square, accum_out=ssq[:, stc:stc + 1]),
                                 waits=[(sv, stat_vv)], ms=True)
                b_store[(i, g)] = GD(lambda e, i=i, b=b, g=g: e.dma_start(
                    out=dstv[i, :, g, :], in_=xio[b][:, g * 1024:(g + 1) * 1024]),
                    waits=[(sv, stat_vv or ev), (sa, stat_aa)], sem=(dg2 if g == 0 else dg3)[b])
            po_free_local = ev
            if collect_stats:
                b_stats_a[i] = stat_aa
            return po_free_local

        for i in range(NT):
            if i < 2:
                w_ = [(sv, phase_wait_v), (sa, phase_wait_a)] + list(extra_first)
                w1_ = list(w_)
            else:
                w_ = [(dg2[i % 2], b_store[(i - 2, 0)])]
                w1_ = [(dg3[i % 2], b_store[(i - 2, 1)])]
            x_ld[i] = LD(lambda e, i=i: e.dma_start(out=xio[i % 2][:, 0:1024], in_=srcv[i, :, 0, :]),
                         waits=w_, sem=dx2[i % 2])
            x_ld2[i] = LD(lambda e, i=i: e.dma_start(out=xio[i % 2][:, 1024:2048], in_=srcv[i, :, 1, :]),
                          waits=w1_, sem=dx3[i % 2])
            a_part(i)
            if dbg4 is not None and 2 < i < 6 and os.environ.get("ATH_SNAP"):
                GD(lambda e, i=i: e.dma_start(out=dbg4[:, 260 + 32 * (i - 3):292 + 32 * (i - 3)], in_=statr[:]),
                   waits=[(sv, a_v[i])])
            if i >= 1:
                po_free = b_part(i - 1, a_v[i])
        po_free = b_part(NT - 1, a_v[NT - 1])
        emit_ffn.b_store = dict(b_store)
        emit_ffn.last_pbc = nonloc["pbc"]
        emit_ffn.store_tot = [(dg2[0], dg2[0].v), (dg2[1], dg2[1].v), (dg3[0], dg3[0].v), (dg3[1], dg3[1].v)]
        return sv.v, sa.v


    def emit_attn(src, dst, phase_wait_v, phase_wait_a, store_wait, pbc0):
        srcv = src.rearrange("(g p) d -> p g d", p=128)
        dstv = dst.rearrange("(g p) d -> p g d", p=128)

        wq_b = arh(0, 8192).rearrange("p (c n) -> p c n", n=1024)
        wk_b = arh(4096, 8192).rearrange("p (c n) -> p c n", n=1024)
        wv_b = arh(8192, 8192).rearrange("p (c n) -> p c n", n=1024)
        yT = arh(12288, 16384).rearrange("p (c n) -> p c n", n=2048)
        Kb = arh(20480, 16384).rearrange("p (c n) -> p c n", n=2048)
        Vb = arh(28672, 16640).rearrange("p (c n) -> p c n", n=1040)
        Vb_hk = arh(28672, 16640).rearrange("p (c h k) -> p c h k", h=16, k=65)
        xa = [arf(36992, 1024), arf(38016, 1024)]
        cosb = arh(39040, 1024)
        sinb = arh(39552, 1024)
        bqr = arh(40064, 1024)[0:1, :]
        bvr = arh(40576, 1024)[0:1, :]
        bor = arh(41088, 1024)[0:1, :]
        e16f = arf(41600, 256)[0:1, :]
        selb = arf(41856, 1024)[0:16, :]
        ones128 = arh(42880, 128)[0:1, :]
        ones512 = arh(42944, 512)[0:1, :]
        wo_b = arh(0, 8192).rearrange("p (c n) -> p c n", n=1024)
        Qf = arh(4096, 16384).rearrange("p (c n) -> p c n", n=2048)
        A_sb = arf(12288, 4096)
        Abf = arh(16384, 4096)
        eS = arh(18432, 2048)
        rd_sb = stg[0:16, 0:512]
        selbf = arh(43456, 1024)[0:16, :]
        e16bf = arh(44000, 256)[0:1, :]
        dnbf = arh(44200, 1024)[0:1, :]
        rdbf = arh(44800, 512)[0:16, :]
        stgbf = stg[:].bitcast(BF16)[:, 0:1024]
        cvbbf = cvb_t[:].bitcast(BF16)[:, 0:1024]

        psA = [P[:, 0:512], P[:, 512:1024]]
        psB = [P[:, 1024:1536], P[:, 1536:2048]]
        pv2 = P[:, 1024:2048]
        pav_e = P[:, 2048:2560]
        pav_o = P[:, 2560:3072]
        plv = P[0:16, 3072:3584]
        pR = P[:, 3584:4096]

        pr.op("gpsimd", lambda e: e.nop(),
              waits=[(sv, phase_wait_v), (sa, phase_wait_a), (st_, st_.v), (sp, sp.v),
                     (dsp, dsp.v)])
        wqv = wqd.rearrange("(c p) o -> c p o", p=128)
        wkv = wkd.rearrange("(c p) o -> c p o", p=128)
        wvv = wvd.rearrange("(c p) o -> c p o", p=128)

        def load_qkv(wkv_extra=(), wq_extra=()):
            first = list(wkv_extra)
            for wsrc, wdst in ((wkv, wk_b), (wvv, wv_b)):
                for c in range(8):
                    GD(lambda e, wsrc=wsrc, wdst=wdst, c=c: e.dma_start(
                        out=wdst[:, c, :], in_=wsrc[c, :, :]), waits=first, sem=dsp)
                    first = []
            wkv_done_ = dsp.v
            first = list(wq_extra)
            for c in range(8):
                GD(lambda e, c=c: e.dma_start(
                    out=wq_b[:, c, :], in_=wqv[c, :, :]), waits=first, sem=dsq)
                first = []
            return wkv_done_, dsq.v

        wkv_done, wq_done = load_qkv()
        GD(lambda e: e.dma_start(out=cosb, in_=cosd), sem=dsp)
        GD(lambda e: e.dma_start(out=sinb, in_=sind), sem=dsp)
        wl_done = dsp.v
        wkv_done = wl_done
        bl = LD(lambda e: e.dma_start(out=stg[0:1, 0:1024], in_=bqd), waits=[(sv, phase_wait_v), (sa, phase_wait_a)] + list(store_wait))
        bc = V(lambda e: e.tensor_copy(out=bqr, in_=stg[0:1, 0:1024]), waits=[(dw, bl)], ms=True)
        bl = LD(lambda e: e.dma_start(out=stg[0:1, 0:1024], in_=bvd), waits=[(sv, bc)])
        bc = V(lambda e: e.tensor_copy(out=bvr, in_=stg[0:1, 0:1024]), waits=[(dw, bl)], ms=True)
        bl = LD(lambda e: e.dma_start(out=stg[0:1, 0:1024], in_=bod), waits=[(sv, bc)])
        bc = V(lambda e: e.tensor_copy(out=bor, in_=stg[0:1, 0:1024]), waits=[(dw, bl)], ms=True)
        bl = LD(lambda e: e.dma_start(out=e16f, in_=e16_d), waits=[(sv, bc)])
        bc = V(lambda e: e.tensor_copy(out=e16bf, in_=e16f), waits=[(dw, bl)], ms=True)
        bl = LD(lambda e: e.dma_start(out=selb, in_=sel_d), waits=[(sv, bc)])
        bc = V(lambda e: e.tensor_copy(out=selbf, in_=selb), waits=[(dw, bl)], ms=True)
        V(lambda e: e.memset(ones128, 1.0))
        ones_ms = V(lambda e: e.memset(ones512, 1.0), ms=True)

        pbc = pbc0
        pl_rd_a = 0
        xa_user = {0: ones_ms, 1: ones_ms}
        xa_ld = {0: 0, 1: 0}
        b_st2 = {0: 0, 1: 0}
        eS_rd = {}
        psA_rd = {0: 0, 1: 0}
        psB_rd = {0: 0, 1: 0}
        psB_rd_v = {0: 0, 1: 0}
        pk_free = {}
        stats_ms_a = 0
        kb_dump = {}

        aseq = int(os.environ.get("ATH_ASEQ", "2"))
        aqt = int(os.environ.get("ATH_AQT", "3"))
        for s_ in range(aseq):
            seq0_T_done = st_.v
            if s_ == 1:
                wkv_done, wq_done = load_qkv(
                    wkv_extra=[(st_, kb_dump.get("qf_dead", st_.v))],
                    wq_extra=[(st_, st_.v)])
                wl_done = dsp.v
            st_t = {}
            ev_a = {}
            for st in range(16):
                g = s_ * 16 + st
                par = st % 2
                w_ = [(sv, xa_user[par])]
                if s_ == 0 and st < 2:
                    w_ += list(store_wait)
                if b_st2[par]:
                    w_.append((dg2[par], b_st2[par]))
                xa_ld[par] = LD(lambda e, g=g, par=par: e.dma_start(out=xa[par], in_=srcv[:, g, :]),
                                waits=w_, sem=dx2[par])
                w_ = [(dx2[par], xa_ld[par])]
                if s_ == 0:
                    w_.append((dsp, wl_done))
                elif st == 0:
                    w_.append((sa, pl_rd_a))
                if st >= 1:
                    w_.append((st_, st_t[st - 1]))
                if s_ == 1 and st == 0 and 0 in kb_dump:
                    w_.append((dgd, kb_dump[0]))
                xa_user[par] = V(lambda e, par=par, g=g: e.tensor_scalar(
                    out=stgbf, in0=xa[par], scalar1=statm[:, g:g + 1], scalar2=statr[:, g:g + 1],
                    op0=ALU.subtract, op1=ALU.mult), waits=w_, ms=True)
                for j in range(8):
                    w_ = [(sv, xa_user[par])]
                    if ev_a.get(st - 1):
                        w_.append((sa, ev_a[st - 1]))
                    elif s_ == 1:
                        w_.append((sa, pl_rd_a))
                    st_t[st] = T(lambda e, j=j: e.transpose(
                        out=ptrb[:, 128 * j:128 * (j + 1)],
                        in_=stgbf[:, 128 * j:128 * (j + 1)],
                        identity=identbf[:]), waits=w_ if j == 0 else (), ms=(j == 7))
                dest = yT[:, :, 128 * st:128 * (st + 1)]
                w_ev = [(st_, st_t[st])]
                if s_ == 1 and st == 0:
                    w_ev.append((st_, seq0_T_done))
                ev_a[st] = A_(lambda e, dest=dest: e.copy(out=dest, in_=ptrb[:].rearrange("p (c n) -> p c n", n=128)),
                              waits=w_ev, ms=True)
            yT_a = ev_a[15]
            rope_guard = st_t[15]

            for oc in range(8):
                for tt in range(4):
                    bank = (oc * 4 + tt) % 2
                    w_ = [(sa, yT_a), (sv, pk_free.get(bank, 0)), (sa, psA_rd[bank])]
                    if oc == 0 and tt == 0:
                        w_.append((dsp, wkv_done))
                    for ic in range(8):
                        mmk = T(lambda e, oc=oc, tt=tt, ic=ic, bank=bank: e.matmul(
                            out=psA[bank], lhsT=wk_b[:, ic, 128 * oc:128 * (oc + 1)],
                            rhs=yT[:, ic, 512 * tt:512 * (tt + 1)],
                            start=(ic == 0), stop=(ic == 7), skip_group_check=True),
                            waits=w_ if ic == 0 else (), ms=(ic == 7))
                    pk_free[bank] = V(lambda e, oc=oc, tt=tt, bank=bank: e.tensor_copy(
                        out=Kb[:, oc, 512 * tt:512 * (tt + 1)], in_=psA[bank]),
                        waits=[(st_, mmk)], ms=True)

            for tc in range(16):
                w_ = [(sv, pk_free.get(2, 0)), (sv, psB_rd_v[0]), (sv, psB_rd_v[1]), (sa, psB_rd[0]), (sa, psB_rd[1])]
                for ic in range(8):
                    for n2 in range(2):
                        T(lambda e, tc=tc, ic=ic, n2=n2: e.matmul(
                            out=pv2[:, 512 * n2:512 * (n2 + 1)],
                            lhsT=yT[:, ic, 128 * tc:128 * (tc + 1)],
                            rhs=wv_b[:, ic, 512 * n2:512 * (n2 + 1)],
                            start=(ic == 0), stop=False, skip_group_check=True),
                            waits=w_ if (ic == 0 and n2 == 0) else ())
                for n2 in range(2):
                    mmv = T(lambda e, n2=n2: e.matmul(
                        out=pv2[:, 512 * n2:512 * (n2 + 1)], lhsT=ones128,
                        rhs=bvr[0:1, 512 * n2:512 * (n2 + 1)],
                        start=False, stop=True, skip_group_check=True), ms=(n2 == 1))
                pk_free[2] = V(lambda e, tc=tc: e.tensor_copy(
                    out=Vb_hk[:, tc, :, 0:64], in_=pv2[:].rearrange("p (h k) -> p h k", k=64)),
                    waits=[(st_, mmv)], ms=True)
            vone_ms = V(lambda e: e.memset(Vb_hk[:, :, :, 64:65], 1.0), ms=True)
            psB_rd = {0: 0, 1: 0}
            psB_rd_v = {0: 0, 1: 0}
            qf_guard = 0
            for oc in range(8):
                for tt in range(4):
                    bank = (oc * 4 + tt) % 2
                    w_ = [(sv, pk_free.get(bank, 0)), (sa, psA_rd[bank])]
                    if oc == 0 and tt == 0:
                        w_.append((dsq, wq_done))
                    if oc < 2 and tt == 0:
                        w_.append((sv, vone_ms))
                    for ic in range(8):
                        T(lambda e, oc=oc, tt=tt, ic=ic, bank=bank: e.matmul(
                            out=psA[bank], lhsT=wq_b[:, ic, 128 * oc:128 * (oc + 1)],
                            rhs=yT[:, ic, 512 * tt:512 * (tt + 1)],
                            start=(ic == 0), stop=False, skip_group_check=True),
                            waits=w_ if ic == 0 else ())
                    mmq = T(lambda e, oc=oc, tt=tt, bank=bank: e.matmul(
                        out=psA[bank], lhsT=bqr[0:1, 128 * oc:128 * (oc + 1)], rhs=ones512,
                        start=False, stop=True, skip_group_check=True), ms=True)
                    qf_guard = mmq
                    pk_free[bank] = V(lambda e, oc=oc, tt=tt, bank=bank: e.tensor_copy(
                        out=Qf[:, oc, 512 * tt:512 * (tt + 1)], in_=psA[bank]),
                        waits=[(st_, mmq)], ms=True)
            pr.op("gpsimd", lambda e: e.nop(), waits=[(st_, qf_guard)])
            wov = wod.rearrange("(c p) o -> c p o", p=128)
            for c in range(8):
                GD(lambda e, c=c: e.dma_start(out=wo_b[:, c, :], in_=wov[c, :, :]), sem=dsp)
            wo_done = dsp.v

            def rope(buf_v, oc, col0, width, coff, extra=()):
                sl = lambda a, b: buf_v[a:b, oc, col0:col0 + width]
                V(lambda e: e.tensor_scalar(out=stgbf[0:32, 0:width], in0=sl(32, 64), scalar1=-1.0, scalar2=None, op0=ALU.mult),
                  waits=list(extra))
                V(lambda e: e.tensor_copy(out=stgbf[32:64, 0:width], in_=sl(0, 32)))
                V(lambda e: e.tensor_scalar(out=stgbf[64:96, 0:width], in0=sl(96, 128), scalar1=-1.0, scalar2=None, op0=ALU.mult))
                V(lambda e: e.tensor_copy(out=stgbf[96:128, 0:width], in_=sl(64, 96)))
                V(lambda e: e.tensor_tensor(out=cvbbf[:, 0:width], in0=sl(0, 128), in1=cosb[:, coff:coff + width], op=ALU.mult))
                V(lambda e: e.tensor_tensor(out=stgbf[:, 0:width], in0=stgbf[:, 0:width], in1=sinb[:, coff:coff + width], op=ALU.mult))
                return V(lambda e: e.tensor_tensor(out=sl(0, 128), in0=cvbbf[:, 0:width], in1=stgbf[:, 0:width], op=ALU.add), ms=True)

            rope_ms = 0
            for oc in range(8):
                rope_ms = rope(Kb, oc, 1024, 1024, 0,
                               extra=[(st_, rope_guard)] if oc == 0 else ())
            kbtrace = dbg2 is not None and s_ == 0 and os.environ.get("ATH_KBTRACE")
            if dbg2 is not None and s_ == 0 and not kbtrace:
                kb_dump[0] = GD(lambda e: e.dma_start(out=dbg2[:, 0:8192], in_=arf(20480, 8192)),
                                waits=[(sv, rope_ms)])
            for oc in range(8):
                rope_ms = rope(Qf, oc, 1024, 1024, 0)

            def kbdump(slot, waits):
                kb_dump[0] = GD(lambda e, slot=slot: e.dma_start(
                    out=dbg2[:, 1024 * slot:1024 * (slot + 1)], in_=arf(20480, 1024)), waits=waits)
            if kbtrace:
                kbdump(0, [(sv, rope_ms)])

            A_ev = 0
            dn_ev = 0
            abf_ms = 0
            oproj_t = 0
            for qt in range(aqt + 1):
                q0 = 512 * qt
                qev = rope_ms
                for p in range(8):
                    for cp in range(8):
                        mA = mB = None
                        for cc in range(2):
                            c = 2 * cp + cc
                            w_ = [(sa, psA_rd[0])] if cc == 0 else []
                            if p == 0 and c < 2:
                                w_.append((sv, qev))
                            mA = T(lambda e, p=p, c=c, cc=cc, q0=q0: e.matmul(
                                out=psA[cc], lhsT=Kb[0:64, p, 128 * c:128 * (c + 1)],
                                rhs=Qf[0:64, p, q0:q0 + 512], start=True, stop=True,
                                tile_position=(0, 0), skip_group_check=True),
                                waits=w_, ms=(cc == 1))
                            w_ = [(sa, psB_rd[0])] if cc == 0 else []
                            if p == 0 and c < 2:
                                w_.append((sv, psB_rd_v[cc]))
                            mB = T(lambda e, p=p, c=c, cc=cc, q0=q0: e.matmul(
                                out=psB[cc], lhsT=Kb[64:128, p, 128 * c:128 * (c + 1)],
                                rhs=Qf[64:128, p, q0:q0 + 512], start=True, stop=True,
                                tile_position=(64, 0), skip_group_check=True),
                                waits=w_, ms=(cc == 1))
                        w_ = [(st_, mA)]
                        for h_ in (0, 1):
                            if eS_rd.get((0, h_)):
                                w_.append((st_, eS_rd[(0, h_)]))
                        eA = A_(lambda e: e.activation(
                            out=eS[:, 0:1024], in_=P[:, 0:1024], func=AF.Exp,
                            bias=consts[:, CL_NSH:CL_NSH + 1], scale=0.125),
                            waits=w_, ms=True)
                        psA_rd[0] = psA_rd[1] = eA
                        w_ = [(st_, mB)]
                        for h_ in (0, 1):
                            if eS_rd.get((1, h_)):
                                w_.append((st_, eS_rd[(1, h_)]))
                        eB = A_(lambda e: e.activation(
                            out=eS[:, 1024:2048], in_=P[:, 1024:2048], func=AF.Exp,
                            bias=consts[:, CL_NSH:CL_NSH + 1], scale=0.125),
                            waits=w_, ms=True)
                        psB_rd[0] = psB_rd[1] = eB
                        for cc in range(2):
                            c = 2 * cp + cc
                            w_ = [(sa, eA)] if cc == 0 else []
                            if c == 0:
                                w_.append((sv, A_ev))
                                w_.append((sa, stats_ms_a))
                            avA = T(lambda e, p=p, c=c, cc=cc: e.matmul(
                                out=pav_e[0:65, :], lhsT=Vb[:, c, 1040 * 0 + 65 * (2 * p):65 * (2 * p) + 65],
                                rhs=eS[:, 512 * cc:512 * (cc + 1)],
                                start=(c == 0), stop=(c == 15), skip_group_check=True),
                                waits=w_, ms=(c == 15))
                            w_ = [(sa, eB)] if cc == 0 else []
                            if c == 0:
                                w_.append((sv, A_ev))
                            avB = T(lambda e, p=p, c=c, cc=cc: e.matmul(
                                out=pav_o[0:65, :], lhsT=Vb[:, c, 65 * (2 * p + 1):65 * (2 * p + 1) + 65],
                                rhs=eS[:, 1024 + 512 * cc:1024 + 512 * (cc + 1)],
                                start=(c == 0), stop=(c == 15), skip_group_check=True),
                                waits=w_, ms=(c == 15 or cc == 1))
                            if cc == 1:
                                eS_rd[(0, 0)] = eS_rd[(0, 1)] = avB
                                eS_rd[(1, 0)] = eS_rd[(1, 1)] = avB
                    V(lambda e, p=p: e.tensor_copy(out=A_sb[0:64, 512 * p:512 * (p + 1)], in_=pav_e[0:64, :]),
                      waits=[(st_, avA), (st_, avB), (sa, stats_ms_a)])
                    A_ev = V(lambda e, p=p: e.tensor_copy(out=A_sb[64:128, 512 * p:512 * (p + 1)], in_=pav_o[0:64, :]), ms=True)
                    if kbtrace and qt == 1 and p == 7:
                        kbdump(3, [(sv, A_ev)])
                    V(lambda e: e.tensor_copy(out=dnbf[0:1, 0:512], in_=pav_e[64:65, :]))
                    dn_ev = V(lambda e: e.tensor_copy(out=dnbf[0:1, 512:1024], in_=pav_o[64:65, :]), ms=True)
                    w_ = [(sv, dn_ev)]
                    if p == 0:
                        w_.append((sa, pl_rd_a))
                    T(lambda e, p=p: e.matmul(out=plv, lhsT=e16bf[0:1, 16 * (2 * p):16 * (2 * p) + 16],
                                              rhs=dnbf[0:1, 0:512], start=(p == 0), stop=False, skip_group_check=True),
                      waits=w_)
                    pl_mm = T(lambda e, p=p: e.matmul(out=plv, lhsT=e16bf[0:1, 16 * (2 * p + 1):16 * (2 * p + 1) + 16],
                                                      rhs=dnbf[0:1, 512:1024], start=False, stop=(p == 7), skip_group_check=True), ms=True)
                if qt == 3:
                    kb_dump["qf_dead"] = st_.v
                lnm = A_(lambda e: e.activation(out=rd_sb, in_=plv, func=AF.Ln, bias=0.0, scale=1.0),
                         waits=[(st_, pl_mm)], ms=True)
                lnm = A_(lambda e: e.activation(out=rdbf, in_=rd_sb, func=AF.Exp, bias=0.0, scale=-1.0),
                         waits=[(sa, lnm)], ms=True)
                pl_rd_a = lnm
                for g_ in range(8):
                    mr = T(lambda e, g_=g_: e.matmul(out=pR, lhsT=selbf[:, 128 * g_:128 * (g_ + 1)], rhs=rdbf,
                                                     start=True, stop=True, skip_group_check=True),
                           waits=[(sa, lnm), (sv, pbc)], ms=True)
                    w_ = [(st_, mr)]
                    if oproj_t:
                        w_.append((st_, oproj_t))
                    abf_ms = V(lambda e, g_=g_: e.tensor_tensor(
                        out=Abf[:, 512 * g_:512 * (g_ + 1)], in0=pR[:],
                        in1=A_sb[:, 512 * g_:512 * (g_ + 1)], op=ALU.mult), waits=w_, ms=True)
                    pbc = abf_ms
                if kbtrace:
                    kbdump({0: 1, 1: 4, 2: 6, 3: 7}[qt], [(sv, pbc)])
                for tg in range(4):
                    stc = s_ * 16 + qt * 4 + tg
                    par = tg % 2
                    w_ = [(sv, xa_user[par])]
                    if b_st2[par]:
                        w_.append((dg2[par], b_st2[par]))
                    xa_ld[par] = LD(lambda e, stc=stc, par=par: e.dma_start(out=xa[par], in_=srcv[:, stc, :]),
                                    waits=w_, sem=dx2[par])
                    for n2 in range(2):
                        w_m = [(sv, abf_ms), (sv, psB_rd_v[n2]), (sa, psB_rd[n2]), (dsp, wo_done)]
                        for p in range(8):
                            T(lambda e, tg=tg, n2=n2, p=p: e.matmul(
                                out=psB[n2], lhsT=Abf[:, 512 * p + 128 * tg:512 * p + 128 * (tg + 1)],
                                rhs=wo_b[:, p, 512 * n2:512 * (n2 + 1)],
                                start=(p == 0), stop=False, skip_group_check=True),
                                waits=w_m if p == 0 else ())
                        mo = T(lambda e, n2=n2: e.matmul(out=psB[n2], lhsT=ones128,
                                                         rhs=bor[0:1, 512 * n2:512 * (n2 + 1)],
                                                         start=False, stop=True, skip_group_check=True), ms=True)
                        oproj_t = mo
                        psB_rd[n2] = 0
                        psB_rd_v[n2] = V(lambda e, n2=n2, par=par: e.tensor_tensor(
                            out=xa[par][:, 512 * n2:512 * (n2 + 1)], in0=psB[n2],
                            in1=xa[par][:, 512 * n2:512 * (n2 + 1)], op=ALU.add),
                            waits=[(st_, mo), (dx2[par], xa_ld[par])], ms=True)
                    sv_stat = V(lambda e, stc=stc, par=par: e.reduce_sum(out=ssum[:, stc:stc + 1], in_=xa[par], axis=AX.X), ms=True)
                    xa_user[par] = sv_stat
                    stats_ms_a = A_(lambda e, stc=stc, par=par: e.activation(
                        out=A_sb[:, 0:1024], in_=xa[par], func=AF.Square, accum_out=ssq[:, stc:stc + 1]),
                        waits=[(sv, sv_stat)], ms=True)
                    b_st2[par] = GD(lambda e, stc=stc, par=par: e.dma_start(out=dstv[:, stc, :], in_=xa[par]),
                                    waits=[(sv, sv_stat), (sa, stats_ms_a)], sem=dg2[par])
                if kbtrace and qt < 2:
                    kbdump({0: 2, 1: 5}[qt], [(sa, stats_ms_a)])
        emit_attn.qf_dead = kb_dump.get("qf_dead", st_.v)
        return sv.v, sa.v, pbc

    if DEV_PHASES == "ffn1":
        fv, fa = emit_ffn(x0, out_d, collect_stats=False,
                          phase_wait_v=fin_v, phase_wait_a=fin_a,
                          mm0_wait=[(sv, wq_done_v)])
    elif DEV_PHASES == "attn":
        fv, fa = emit_ffn(x0, x1d, collect_stats=True,
                          phase_wait_v=fin_v, phase_wait_a=fin_a,
                          mm0_wait=[(sv, wq_done_v)])
        f1v, f1a = finalize_stats(fv, fa)
        av, aa, apbc = emit_attn(x1d, out_d, f1v, f1a,
                                 store_wait=emit_ffn.store_tot, pbc0=emit_ffn.last_pbc)
    else:
        fv, fa = emit_ffn(x0, x1d, collect_stats=True,
                          phase_wait_v=fin_v, phase_wait_a=fin_a,
                          mm0_wait=[(sv, wq_done_v)])
        f1v, f1a = finalize_stats(fv, fa)
        att_dst = out_d if DEV_PHASES == "attn2" else x2d
        av, aa, apbc = emit_attn(x1d, att_dst, f1v, f1a,
                                 store_wait=emit_ffn.store_tot, pbc0=emit_ffn.last_pbc)
        POST = os.environ.get("ATH_POST", "fin,rl,pre")
        f2v, f2a = (av, aa)
        if "fin" in POST:
            f2v, f2a = finalize_stats(av, aa, nsteps=int(os.environ.get("ATH_FSTEPS", "9")))
        rl0 = dw.v
        if "rl" in POST:
            early_s = [(st_, emit_attn.qf_dead), (dsp, dsp.v)]
            early_g = [(st_, emit_attn.qf_dead)]
            for j in (2, 3):
                LD(lambda e, j=j: e.dma_start(out=w1q[:, j * 4096:(j + 1) * 4096], in_=w1qs_v[j, :, :]),
                   waits=early_s)
                early_s = []
            for j in (4, 5):
                GD(lambda e, j=j: e.dma_start(out=w1q[:, j * 4096:(j + 1) * 4096], in_=w1qs_v[j, :, :]),
                   waits=early_g, sem=dsp)
                early_g = []
            for j in range(4, 32):
                if j % 2 == 0:
                    LD(lambda e, j=j: e.dma_start(out=w2q[:, j * 1024:(j + 1) * 1024], in_=w2qs_v[j, :, :]))
                else:
                    GD(lambda e, j=j: e.dma_start(out=w2q[:, j * 1024:(j + 1) * 1024], in_=w2qs_v[j, :, :]), sem=dsp)
            late_s = [(sv, av), (sa, aa)]
            late_g = [(sv, av), (sa, aa)]
            for j in (0, 6):
                LD(lambda e, j=j: e.dma_start(out=w1q[:, j * 4096:(j + 1) * 4096], in_=w1qs_v[j, :, :]),
                   waits=late_s)
                late_s = []
            for j in (1, 7):
                GD(lambda e, j=j: e.dma_start(out=w1q[:, j * 4096:(j + 1) * 4096], in_=w1qs_v[j, :, :]),
                   waits=late_g, sem=dsp)
                late_g = []
            for j in range(4):
                if j % 2 == 0:
                    LD(lambda e, j=j: e.dma_start(out=w2q[:, j * 1024:(j + 1) * 1024], in_=w2qs_v[j, :, :]))
                else:
                    GD(lambda e, j=j: e.dma_start(out=w2q[:, j * 1024:(j + 1) * 1024], in_=w2qs_v[j, :, :]), sem=dsp)
        rl_done = dw.v
        rl_done_g = dsp.v
        if "pre" in POST and os.environ.get("ATH_PRE"):
            A_(lambda e: e.preload_act_table(AF.Silu), waits=[(sa, f2a)])
        if DEV_PHASES != "attn2":
            fv, fa = emit_ffn(x2d, out_d, collect_stats=False,
                              phase_wait_v=f2v, phase_wait_a=f2a,
                              extra_first=[(dg2[0], dg2[0].v), (dg2[1], dg2[1].v),
                                           (dg3[0], dg3[0].v), (dg3[1], dg3[1].v)],
                              pbc0=apbc, mm0_wait=[(dw, rl_done), (dsp, rl_done_g)])

    if dbg1 is not None:
        bs = getattr(emit_ffn, "b_store", {})
        for k, col in ([] if os.environ.get("ATH_SNAP") else [(0, 0), (3, 32), (7, 64)]):
            if (k, 1) in bs:
                GD(lambda e, col=col: e.dma_start(out=dbg4[:, 260 + col:260 + col + 32], in_=statr[:]),
                   waits=[(dg3[k % 2], bs[(k, 1)])])
        GD(lambda e: e.dma_start(out=dbg1[:, 0:4], in_=wsc[:]), waits=[(sv, sv.v), (sa, sa.v), (sp, sp.v), (st_, st_.v)])
        GD(lambda e: e.dma_start(out=dbg1[:, 4:36], in_=statm[:]))
        GD(lambda e: e.dma_start(out=dbg1[:, 36:68], in_=statr[:]))
        GD(lambda e: e.dma_start(out=dbg1[:, 68:84], in_=tmp[:]))
        GD(lambda e: e.dma_start(out=dbg1[:, 84:92], in_=consts[:]))
        GD(lambda e: e.dma_start(out=dbg1[:, 124:156], in_=ssq[:]))
        GD(lambda e: e.dma_start(out=dbg4[:, 0:64], in_=guard1[:]))
        GD(lambda e: e.dma_start(out=dbg4[:, 64:128], in_=guard2[:]))
        GD(lambda e: e.dma_start(out=dbg4[:, 128:144], in_=tmp[:]))
        GD(lambda e: e.dma_start(out=dbg4[:, 144:148], in_=wsc[:]))
        GD(lambda e: e.dma_start(out=dbg4[:, 148:156], in_=consts[:]))
        GD(lambda e: e.dma_start(out=dbg4[:, 156:188], in_=statm[:]))
        GD(lambda e: e.dma_start(out=dbg2[:, :], in_=hbuf))

        if not os.environ.get("ATH_KBTRACE"):
            GD(lambda e: e.dma_start(out=dbg3[:, :], in_=yqT[:]))

    pr.op("gpsimd", lambda e: e.nop(),
          waits=[(dgd, dgd.v), (dsp, dsp.v), (dsq, dsq.v), (dg2[0], dg2[0].v), (dg2[1], dg2[1].v),
                 (dg3[0], dg3[0].v), (dg3[1], dg3[1].v),
                 (dx2[0], dx2[0].v), (dx2[1], dx2[1].v),
                 (dx3[0], dx3[0].v), (dx3[1], dx3[1].v), (dw, dw.v),
                 (dw2[0], dw2[0].v), (dw2[1], dw2[1].v)])

    if os.environ.get("ATH_TAGDUMP"):
        with open("_tagdump.txt", "w") as f:
            for eng in ENGINES:
                for i, (fn, ws, inc, incv) in enumerate(pr.q[eng]):
                    wstr = ",".join(f"{s.name}>={v}" for s, v in ws)
                    istr = f"+{inc.name}" if inc is not None else ""
                    f.write(f"{eng} {i} [{wstr}] {istr}\n")

    block = stack.enter_context(nc.Block())

    namelog = open("_instmap.txt", "w") if os.environ.get("ATH_TAGDUMP") else None

    def runner(ops, ename=""):
        def run(eng):
            for i, (fn, waits, inc, incv) in enumerate(ops):
                for s, v in waits:
                    eng.wait_ge(s.h, v)
                instr = fn(eng)
                if inc is not None:
                    instr.then_inc(inc.h, incv)
                if namelog is not None:
                    namelog.write(f"{ename} {i} {getattr(instr, 'name', '?')}\n")
        return run

    block.sync(runner(pr.q["sync"], "sync"))
    block.vector(runner(pr.q["vector"], "vector"))
    block.scalar(runner(pr.q["scalar"], "scalar"))
    block.tensor(runner(pr.q["tensor"], "tensor"))
    block.gpsimd(runner(pr.q["gpsimd"], "gpsimd"))
    if namelog is not None:
        namelog.close()

    stack.close()
    return nc



def kernel(**inputs):
    x = np.asarray(inputs["L0_attn_hidden_states"])
    cos = np.asarray(inputs["L0_attn_cos"])
    sin = np.asarray(inputs["L0_attn_sin"])
    wq = np.asarray(inputs["L0_attn_wq"])
    bq = np.asarray(inputs["L0_attn_bq"])
    wk = np.asarray(inputs["L0_attn_wk"])
    wv = np.asarray(inputs["L0_attn_wv"])
    bv = np.asarray(inputs["L0_attn_bv"])
    wo = np.asarray(inputs["L0_attn_wo"])
    bo = np.asarray(inputs["L0_attn_bo"])
    w1 = np.asarray(inputs["L0_ffn_w1"])
    cwv = np.asarray(inputs["L0_ffn_conv_w"])
    cb = np.asarray(inputs["L0_ffn_conv_b"])
    w2 = np.asarray(inputs["L0_ffn_w2"])

    w1t = np.ascontiguousarray(w1.T)
    w2t = np.ascontiguousarray(w2.T)
    cwm = np.zeros((128, 128), dtype=np.float32)
    taps = cwv[:, 0, :]
    for tap in range(3):
        cwm[:, 32 * tap:32 * tap + 32] = taps[:, tap].reshape(32, 128).T
    cwm[:, 96:128] = cb.reshape(32, 128).T
    cost = np.ascontiguousarray(np.tile(cos.T, (2, 1)).astype(np.float32, copy=False))
    sint = np.ascontiguousarray(np.tile(sin.T, (2, 1)).astype(np.float32, copy=False))

    identf = np.eye(128, dtype=np.float32)
    onesf = np.ones((128, 128), dtype=np.float32)
    e16 = np.zeros((1, 256), dtype=np.float32)
    for h in range(16):
        e16[0, 16 * h + h] = 1
    selm = np.zeros((16, 1024), dtype=np.float32)
    for g in range(8):
        selm[2 * g, 128 * g:128 * g + 64] = 1
        selm[2 * g + 1, 128 * g + 64:128 * g + 128] = 1
    constsv = np.zeros((128, 8), dtype=np.float32)
    constsv[:, 0] = 1e-5
    constsv[:, 1] = 1.0 / 127.0
    constsv[:, 2] = -C_SHIFT
    constsv[:, 3] = EPS
    constsv[:, 4] = 0.5

    nc = build_nc()
    shared = dict(w1t=w1t, w2t=w2t, cwdrm=cwm, wq=wq, wk=wk, wv=wv, wo=wo,
                  bq=bq.reshape(1, -1), bv=bv.reshape(1, -1), bo=bo.reshape(1, -1),
                  cosd=cost, sind=sint, identfd=identf,
                  onesfd=onesf, e16d=e16, seld=selm, constsd=constsv)
    in_maps = []
    for c in range(NCORE):
        m = dict(shared)
        m["x"] = np.ascontiguousarray(x[2 * c:2 * c + 2].reshape(NTOK, D))
        in_maps.append(m)

    res = run_bass_kernel_spmd(nc, in_maps, core_ids=list(range(NCORE)))
    if os.environ.get("ATH_DEBUG"):
        kernel._dbg = res.results
    out = np.stack([np.asarray(r["out"]) for r in res.results])
    return np.ascontiguousarray(out.reshape(16, 2048, 1024).astype(np.float32, copy=False))


# version 18
# speedup vs baseline: 1.0098x; 1.0073x over previous
import os
import numpy as np
import ml_dtypes
import concourse.bass as bass
import concourse.mybir as mybir
from concourse.bass_utils import run_bass_kernel_spmd
from contextlib import ExitStack

F32 = mybir.dt.float32
BF16 = mybir.dt.bfloat16
I32 = mybir.dt.int32
AF = mybir.ActivationFunctionType
ALU = mybir.AluOpType
AX = mybir.AxisListType

NCORE = 8
SEQ = 2048
NSEQ = 2
NTOK = SEQ * NSEQ
D = 1024
DH = 4096
NH, HD = 16, 64
TOK = 256
NT = NTOK // TOK
TPS = SEQ // TOK
C_SHIFT = 12.0
RMAGIC = 12582912.0
EPS = 1e-5

ENGINES = ("sync", "vector", "scalar", "tensor", "gpsimd")
DEV_PHASES = os.environ.get("ATH_PHASES", "full")
CUT = int(os.environ.get("ATH_CUT", "9"))


class Sem:
    __slots__ = ("h", "v", "name")
    def __init__(self, h, name):
        self.h, self.v, self.name = h, 0, name


class Prog:
    def __init__(self):
        self.q = {e: [] for e in ENGINES}

    def op(self, eng, fn, waits=(), inc=None, incv=1):
        ws = tuple((s, v) for (s, v) in waits if v > 0)
        self.q[eng].append((fn, ws, inc, incv))
        if inc is not None:
            inc.v += incv
            return inc.v
        return None


def build_nc():
    nc = bass.Bass("TRN2", target_bir_lowering=False)

    x0 = nc.dram_tensor("x", [NTOK, D], F32, kind="ExternalInput").ap()
    w1t = nc.dram_tensor("w1t", [D, DH], F32, kind="ExternalInput").ap()
    w2t = nc.dram_tensor("w2t", [DH, D], F32, kind="ExternalInput").ap()
    cwd = nc.dram_tensor("cwdrm", [128, 128], F32, kind="ExternalInput").ap()
    wqd = nc.dram_tensor("wq", [D, D], F32, kind="ExternalInput").ap()
    wkd = nc.dram_tensor("wk", [D, D], F32, kind="ExternalInput").ap()
    wvd = nc.dram_tensor("wv", [D, D], F32, kind="ExternalInput").ap()
    wod = nc.dram_tensor("wo", [D, D], F32, kind="ExternalInput").ap()
    bqd = nc.dram_tensor("bq", [1, D], F32, kind="ExternalInput").ap()
    bvd = nc.dram_tensor("bv", [1, D], F32, kind="ExternalInput").ap()
    bod = nc.dram_tensor("bo", [1, D], F32, kind="ExternalInput").ap()
    cosd = nc.dram_tensor("cosd", [128, 1024], F32, kind="ExternalInput").ap()
    sind = nc.dram_tensor("sind", [128, 1024], F32, kind="ExternalInput").ap()
    
    identf_d = nc.dram_tensor("identfd", [128, 128], F32, kind="ExternalInput").ap()
    
    onesf_d = nc.dram_tensor("onesfd", [128, 128], F32, kind="ExternalInput").ap()
    e16_d = nc.dram_tensor("e16d", [1, 256], F32, kind="ExternalInput").ap()
    sel_d = nc.dram_tensor("seld", [16, 1024], F32, kind="ExternalInput").ap()
    consts_d = nc.dram_tensor("constsd", [128, 8], F32, kind="ExternalInput").ap()
    out_d = nc.dram_tensor("out", [NTOK, D], F32, kind="ExternalOutput").ap()
    dbg1 = dbg2 = dbg3 = dbg4 = None
    if os.environ.get("ATH_DEBUG"):
        dbg1 = nc.dram_tensor("dbg1", [128, 160], F32, kind="ExternalOutput").ap()
        dbg2 = nc.dram_tensor("dbg2", [128, 8256], F32, kind="ExternalOutput").ap()
        dbg3 = nc.dram_tensor("dbg3", [128, 6192], BF16, kind="ExternalOutput").ap()
        dbg4 = nc.dram_tensor("dbg4", [128, 360], F32, kind="ExternalOutput").ap()
    x1d = nc.dram_tensor("x1buf", [NTOK, D], F32, kind="Internal").ap()
    x2d = nc.dram_tensor("x2buf", [NTOK, D], F32, kind="Internal").ap()
    w1qs = nc.dram_tensor("w1qs", [D, DH], BF16, kind="Internal").ap()
    w2qs = nc.dram_tensor("w2qs", [DH, D], BF16, kind="Internal").ap()

    stack = ExitStack()
    sb = lambda n, s, d: stack.enter_context(nc.sbuf_tensor(n, s, d))

    AR = sb("arena", [128, 45120], F32)
    arb = AR[:].bitcast(BF16)

    def arf(off, n):
        return AR[:, off:off + n]
    def arh(off_f32, n_bf16):
        return arb[:, 2 * off_f32:2 * off_f32 + n_bf16]

    w1q = arh(0, 32768)
    w2q = arh(16384, 32768)
    hbuf = arf(32768, 8256)
    hbuf_i = hbuf.bitcast(I32)
    xio = [arf(41024, 2048), arf(43072, 2048)]

    stg = sb("stg", [128, 1024], F32)
    stg_iv = stg[:].bitcast(I32)
    blk = sb("blk", [128, 256], BF16)
    blkint = sb("blkint", [128, 256], I32)
    sqint = sb("sqint", [128, 256], I32)
    yqT = sb("yqT", [128, 6192], BF16)
    r1row = sb("r1row", [1, 774], F32)
    cvb_t = sb("cvb", [128, 512], F32)
    sqbf = cvb_t[:].bitcast(BF16)
    r1bc = sb("r1bc", [128, 258], F32)
    r2bc = r1bc[:, 0:256]
    r2row = sb("r2row", [1, 256], F32)
    accd = sb("accd", [128, 256], F32)
    accp = sb("accp", [128, 256], F32)
    ssum = sb("ssum", [128, 32], F32)
    ssq = sb("ssq", [128, 32], F32)
    statm = sb("statm", [128, 32], F32)
    guard1 = sb("guard1", [128, 64], F32)
    statr = sb("statr", [128, 32], F32)
    guard2 = sb("guard2", [128, 64], F32)
    tmp = sb("tmp", [128, 16], F32)
    wsc = sb("wsc", [128, 4], F32)
    sc11 = sb("sc11", [1, 4], F32)
    consts = sb("consts", [128, 8], F32)
    cw = sb("cwsb", [128, 128], F32)
    identbf = sb("identbf", [128, 128], BF16)
    identf = sb("identf", [128, 128], F32)
    onesf = sb("onesf", [128, 128], F32)

    CL_CLIP, CL_R127, CL_NSH, CL_EPS, CL_HALF = 0, 1, 2, 3, 4

    P = stack.enter_context(nc.psum_tensor("P", [128, 4096], F32))
    ph = [P[:, 0:258], P[:, 512:770]]
    po = [P[:, 1024:2048], P[:, 2048:3072]]
    ptrb = P[:, 3072:3584].bitcast(BF16)
    pbcf = P[:, 3584:4096]

    def mksem(name):
        return Sem(stack.enter_context(nc.semaphore(name)), name)
    dx2 = [mksem("dxa"), mksem("dxb")]
    dx3 = [mksem("dxc"), mksem("dxd")]
    dsq = mksem("dsq")
    dg3 = [mksem("dge"), mksem("dgf")]
    dw = mksem("dw")
    dw2 = [mksem("dwa"), mksem("dwb")]
    dg2 = [mksem("dga"), mksem("dgb")]
    dsp = mksem("dsp")
    dgd = mksem("dgd")
    sv = mksem("sv")
    sa = mksem("sa")
    sp = mksem("sp")
    st_ = mksem("st")

    pr = Prog()

    def _chain(waits, sem):
        w = tuple(waits)
        if any(s_ is sem and v_ >= sem.v for s_, v_ in w):
            return w
        return w + ((sem, sem.v),)

    def V(fn, waits=(), ms=False):
        return pr.op("vector", fn, _chain(waits, sv), sv)
    def A_(fn, waits=(), ms=False):
        return pr.op("scalar", fn, _chain(waits, sa), sa)
    def T(fn, waits=(), ms=False):
        return pr.op("tensor", fn, waits, st_ if ms else None)
    def G(fn, waits=(), ms=False):
        return pr.op("gpsimd", fn, _chain(waits, sp), sp)
    def LD(fn, waits=(), sem=None):
        sem = sem or dw
        return pr.op("sync", fn, waits, sem, incv=16)
    def GD(fn, waits=(), sem=None):
        return pr.op("gpsimd", fn, waits, sem or dgd, incv=16)

    LD(lambda e: e.dma_start(out=consts[:], in_=consts_d))
    LD(lambda e: e.dma_start(out=cw[:], in_=cwd))
    idl = LD(lambda e: e.dma_start(out=identf[:], in_=identf_d))
    cl = LD(lambda e: e.dma_start(out=onesf[:], in_=onesf_d))
    V(lambda e: e.tensor_copy(out=identbf[:], in_=identf[:]), waits=[(dw, cl)])
    V(lambda e: e.memset(guard1[:], 0.0))
    V(lambda e: e.memset(guard2[:], 0.0))

    x0v = x0.rearrange("(s p) d -> p s d", p=128)
    stat_v = {}
    stat_a = {}
    ld_ms = {}
    for s_ in range(32):
        b = s_ % 2
        w_ = []
        if s_ >= 2:
            w_ = [(sv, stat_v[s_ - 2]), (sa, stat_a[s_ - 2])]
        ld_ms[s_] = LD(lambda e, s_=s_, b=b: e.dma_start(out=xio[b][:, 0:1024], in_=x0v[:, s_, :]),
                       waits=w_, sem=dx2[b])
        stat_v[s_] = V(lambda e, s_=s_, b=b: e.reduce_sum(out=ssum[:, s_:s_ + 1], in_=xio[b][:, 0:1024], axis=AX.X),
                       waits=[(dx2[b], ld_ms[s_])], ms=True)
        wsq = [(dx2[b], ld_ms[s_])]
        if s_ >= 1:
            wsq.append((sa, stat_a[s_ - 1]))
        stat_a[s_] = A_(lambda e, s_=s_, b=b: e.activation(out=stg[:, 0:1024], in_=xio[b][:, 0:1024], func=AF.Square, accum_out=ssq[:, s_:s_ + 1]),
                        waits=wsq, ms=True)
    stats_v, stats_a = stat_v[31], stat_a[31]

    wload = [hbuf[:, 0:4096], hbuf[:, 4096:8192]]
    wload_i = [hbuf_i[:, 0:4096], hbuf_i[:, 4096:8192]]
    w1tv = w1t.rearrange("(c p) h -> c p h", p=128)
    w2tv = w2t.rearrange("(c p) h -> c p h", p=128)

    red_ms = {}
    for j in range(8):
        b = j % 2
        w_ = [] if j < 2 else [(sv, red_ms[j - 2])]
        ld = LD(lambda e, j=j, b=b: e.dma_start(out=wload[b], in_=w1tv[j, :, :]), waits=w_, sem=dw2[b])
        red_ms[j] = V(lambda e, j=j, b=b: e.reduce_sum(out=tmp[:, j:j + 1], in_=wload[b], axis=AX.X, apply_absolute_value=True),
                      waits=[(dw2[b], ld)], ms=True)
    V(lambda e: e.reduce_sum(out=tmp[:, 8:9], in_=tmp[:, 0:8], axis=AX.X), waits=[(sv, red_ms[7])])
    for j in range(32):
        b = j % 2
        w_ = [(sv, red_ms[j - 2])] if j >= 2 else [(sv, red_ms[7] if b else red_ms[6])]
        ld = LD(lambda e, j=j, b=b: e.dma_start(out=wload[b][:, 0:1024], in_=w2tv[j, :, :]), waits=w_, sem=dw2[b])
        acc = 12 + b
        if j < 2:
            red_ms[j] = V(lambda e, b=b, acc=acc: e.reduce_sum(out=tmp[:, acc:acc + 1], in_=wload[b][:, 0:1024], axis=AX.X, apply_absolute_value=True),
                          waits=[(dw2[b], ld)], ms=True)
        else:
            rr = V(lambda e, b=b: e.reduce_sum(out=tmp[:, 10 + b:11 + b], in_=wload[b][:, 0:1024], axis=AX.X, apply_absolute_value=True),
                   waits=[(dw2[b], ld)], ms=True)
            red_ms[j] = V(lambda e, b=b, acc=acc: e.tensor_tensor(out=tmp[:, acc:acc + 1], in0=tmp[:, acc:acc + 1], in1=tmp[:, 10 + b:11 + b], op=ALU.add),
                          waits=[(sv, rr)], ms=True)
    vsum2 = V(lambda e: e.tensor_tensor(out=tmp[:, 9:10], in0=tmp[:, 12:13], in1=tmp[:, 13:14], op=ALU.add),
              waits=[(sv, red_ms[31]), (sv, red_ms[30])], ms=True)

    tps = T(lambda e: e.matmul(out=pbcf[0:1, 0:1], lhsT=tmp[:, 8:9], rhs=onesf[:, 0:1], start=True, stop=True, skip_group_check=True),
            waits=[(sv, vsum2), (dw, cl)])
    tps = T(lambda e: e.matmul(out=pbcf[0:1, 1:2], lhsT=tmp[:, 9:10], rhs=onesf[:, 0:1], start=True, stop=True, skip_group_check=True), ms=True)
    vm = V(lambda e: e.tensor_scalar(out=sc11[0:1, 0:2], in0=pbcf[0:1, 0:2], scalar1=1.0 / (DH * D), scalar2=1e-5, op0=ALU.mult, op1=ALU.max),
           waits=[(st_, tps)], ms=True)
    tbc = T(lambda e: e.matmul(out=pbcf[:, 2:4], lhsT=onesf[0:1, :], rhs=sc11[0:1, 0:2], start=True, stop=True, skip_group_check=True),
            waits=[(sv, vm)], ms=True)
    V(lambda e: e.tensor_copy(out=wsc[:, 0:1], in_=pbcf[:, 2:3]), waits=[(st_, tbc)])
    vx = V(lambda e: e.tensor_copy(out=wsc[:, 2:3], in_=pbcf[:, 3:4]), ms=True)
    vx = V(lambda e: e.reciprocal(out=wsc[:, 1:2], in_=wsc[:, 0:1]), waits=[(sv, vx)], ms=True)
    wsc_rdy = V(lambda e: e.reciprocal(out=wsc[:, 3:4], in_=wsc[:, 2:3]), waits=[(sv, vx)], ms=True)
    wsc_rdy = V(lambda e: e.tensor_copy(out=tmp[:, 15:16], in_=wsc[:, 3:4]), waits=[(sv, wsc_rdy)], ms=True)
    pbc_free = wsc_rdy

    w1qs_v = w1qs.rearrange("(c p) h -> c p h", p=128)
    w2qs_v = w2qs.rearrange("(c p) h -> c p h", p=128)
    q_ms = {}
    for j in range(8):
        b = j % 2
        w_ = [(sv, wsc_rdy)] if j < 2 else [(sv, q_ms[j - 2])]
        ld = LD(lambda e, j=j, b=b: e.dma_start(out=wload[b], in_=w1tv[j, :, :]), waits=w_, sem=dw2[b])
        V(lambda e, j=j, b=b: e.tensor_scalar(out=wload[b], in0=wload[b], scalar1=wsc[:, 1:2], scalar2=None, op0=ALU.mult),
          waits=[(dw2[b], ld)])
        V(lambda e, j=j, b=b: e.tensor_scalar(out=wload[b], in0=wload[b], scalar1=RMAGIC, scalar2=RMAGIC, op0=ALU.add, op1=ALU.subtract))
        q_ms[j] = V(lambda e, j=j, b=b: e.tensor_scalar(out=w1q[:, j * 4096:(j + 1) * 4096], in0=wload[b], scalar1=-1.0, scalar2=1.0, op0=ALU.max, op1=ALU.min), ms=True)
        GD(lambda e, j=j: e.dma_start(out=w1qs_v[j, :, :], in_=w1q[:, j * 4096:(j + 1) * 4096]),
           waits=[(sv, q_ms[j])], sem=dsp)
    for j in range(32):
        b = j % 2
        w_ = [(sv, q_ms[6 + b])] if j < 2 else [(sv, q_ms[8 + j - 2])]
        ld = LD(lambda e, j=j, b=b: e.dma_start(out=wload[b][:, 0:1024], in_=w2tv[j, :, :]), waits=w_, sem=dw2[b])
        V(lambda e, j=j, b=b: e.tensor_scalar(out=wload[b][:, 0:1024], in0=wload[b][:, 0:1024], scalar1=wsc[:, 3:4], scalar2=None, op0=ALU.mult),
          waits=[(dw2[b], ld)])
        V(lambda e, j=j, b=b: e.tensor_scalar(out=wload[b][:, 0:1024], in0=wload[b][:, 0:1024], scalar1=RMAGIC, scalar2=RMAGIC, op0=ALU.add, op1=ALU.subtract))
        q_ms[8 + j] = V(lambda e, j=j, b=b: e.tensor_scalar(out=w2q[:, j * 1024:(j + 1) * 1024], in0=wload[b][:, 0:1024], scalar1=-1.0, scalar2=1.0, op0=ALU.max, op1=ALU.min), ms=True)
        GD(lambda e, j=j: e.dma_start(out=w2qs_v[j, :, :], in_=w2q[:, j * 1024:(j + 1) * 1024]),
           waits=[(sv, q_ms[8 + j])], sem=dsp)
    wq_done_v = q_ms[8 + 31]
    wspill = dsp.v

    def finalize_stats(wv_, wa_, nsteps=9):
        a = wa_
        if nsteps >= 1 and os.environ.get("ATH_PRE"):
            a = A_(lambda e: e.preload_act_table(AF.Exp), waits=[(sa, wa_)], ms=True)
        v = wv_
        if os.environ.get("ATH_FSKIPV"):
            nsteps_v = 0
        else:
            nsteps_v = nsteps
        if nsteps_v >= 2:
            v = V(lambda e: e.tensor_scalar(out=statm[:], in0=ssum[:], scalar1=1.0 / D, scalar2=None, op0=ALU.mult),
                  waits=[(sv, wv_), (sa, wa_)], ms=True)
        if nsteps_v >= 3:
            v = V(lambda e: e.tensor_tensor(out=statr[:], in0=statm[:], in1=statm[:], op=ALU.mult), waits=[(sv, v)], ms=True)
        if nsteps_v >= 4:
            v = V(lambda e: e.tensor_scalar(out=ssq[:], in0=ssq[:], scalar1=1.0 / D, scalar2=None, op0=ALU.mult), waits=[(sv, v)], ms=True)
        if nsteps_v >= 5:
            v = V(lambda e: e.tensor_tensor(out=statr[:], in0=ssq[:], in1=statr[:], op=ALU.subtract), waits=[(sv, v)], ms=True)
        if nsteps >= 6:
            lnf = {"sq": AF.Square, "ln": AF.Ln}[os.environ.get("ATH_LNF", "ln")]
            lnb = 0.0 if os.environ.get("ATH_LNB") else consts[:, CL_EPS:CL_EPS + 1]
            lno = tmp[:, 0:1] if os.environ.get("ATH_LNO") else statr[:]
            lni = statr[:, 0:1] if os.environ.get("ATH_LNO") else statr[:]
            a = A_(lambda e: e.activation(out=lno, in_=lni, func=lnf, bias=lnb, scale=1.0),
               waits=[(sv, v)], ms=True)
        if nsteps >= 7:
            a = A_(lambda e: e.activation(out=statr[:], in_=statr[:], func=AF.Exp, bias=0.0, scale=-0.5), waits=[(sa, a)], ms=True)
        return sv.v, a

    fin_v, fin_a = finalize_stats(max(stats_v, wsc_rdy), stats_a)
    if dbg1 is not None and DEV_PHASES == "ffn1":
        GD(lambda e: e.dma_start(out=dbg1[:, 92:124], in_=statr[:]), waits=[(sa, fin_a)])

    def emit_ffn(src, dst, collect_stats, phase_wait_v, phase_wait_a, extra_first=(), pbc0=None, mm0_wait=()):
        srcv = src.rearrange("(t g p) d -> t p g d", p=128, g=2)
        dstv = dst.rearrange("(t g p) d -> t p g d", p=128, g=2)

        if os.environ.get("ATH_PRE"):
            A_(lambda e: e.preload_act_table(AF.Silu), waits=[(sa, phase_wait_a)])

        x_ld = {}
        x_ld2 = {}
        a_v = {}
        a_a = {}
        a_t = {}
        b_store = {}
        b_stats_a = {}
        b_mm1_done = {}
        evac_ms = {}
        po_free = None
        nonloc = dict(pbc=pbc0 if pbc0 is not None else pbc_free, first_mm=True)

        def slot_view(slot):
            return yqT[:, slot * 2064:(slot + 1) * 2064].rearrange("p (c n) -> p c n", n=258)

        def a_part(i):
            slot = i % 3
            b = i % 2
            sfirst = (i % TPS) == 0
            slast = (i % TPS) == TPS - 1
            w0 = [(dx2[i % 2], x_ld[i])]
            if i >= 1:
                w0.append((st_, a_t[i - 1]))
            tg = {}
            cp_ms = {}
            qs_rdy_g = {}
            for g in range(2):
                stc = 2 * i + g
                wv_ = w0 if g == 0 else [(dx3[i % 2], x_ld2[i]), (sp, cp_ms[(0, 7)])]
                V(lambda e, b=b, g=g, stc=stc: e.tensor_scalar(
                    out=stg[:], in0=xio[b][:, g * 1024:(g + 1) * 1024],
                    scalar1=statm[:, stc:stc + 1], scalar2=None, op0=ALU.subtract, op1=ALU.max,
                    accum_out=tmp[:, g:g + 1]),
                  waits=wv_)
                V(lambda e, b=b, g=g, stc=stc: e.tensor_scalar(
                    out=stg[:], in0=xio[b][:, g * 1024:(g + 1) * 1024],
                    scalar1=statm[:, stc:stc + 1], scalar2=None, op0=ALU.subtract, op1=ALU.min,
                    accum_out=tmp[:, 2 + g:3 + g]))
                V(lambda e, g=g: e.tensor_scalar(out=tmp[:, 2 + g:3 + g], in0=tmp[:, 2 + g:3 + g], scalar1=-1.0, scalar2=None, op0=ALU.mult))
                V(lambda e, g=g: e.tensor_tensor(out=tmp[:, g:g + 1], in0=tmp[:, g:g + 1], in1=tmp[:, 2 + g:3 + g], op=ALU.max))
                V(lambda e, g=g: e.tensor_scalar(out=tmp[:, 2 + g:3 + g], in0=tmp[:, g:g + 1], scalar1=1e-5, scalar2=None, op0=ALU.max))
                vv = V(lambda e, g=g: e.reciprocal(out=tmp[:, 4 + g:5 + g], in_=tmp[:, 2 + g:3 + g]), ms=True)
                vv = V(lambda e, g=g: e.tensor_scalar(out=tmp[:, 4 + g:5 + g], in0=tmp[:, 4 + g:5 + g], scalar1=127.0, scalar2=None, op0=ALU.mult), waits=[(sv, vv)], ms=True)
                V(lambda e, g=g, i=i, stc=stc: e.tensor_tensor(out=tmp[:, 6 + g:7 + g], in0=statr[:, stc:stc + 1], in1=tmp[:, g:g + 1], op=ALU.mult))
                V(lambda e, g=g: e.tensor_scalar(out=tmp[:, 6 + g:7 + g], in0=tmp[:, 6 + g:7 + g], scalar1=1e-5, scalar2=None, op0=ALU.max))
                qs_rdy_g[g] = V(lambda e, g=g: e.tensor_scalar(out=tmp[:, 6 + g:7 + g], in0=tmp[:, 6 + g:7 + g], scalar1=wsc[:, 0:1], scalar2=1.0 / 127.0, op0=ALU.mult, op1=ALU.mult), ms=True)
                w_q = [(sv, qs_rdy_g[g])]
                if g == 1:
                    w_q.append((sp, cp_ms[(0, 7)]))
                qv = V(lambda e, g=g, b=b, stc=stc: e.tensor_scalar(
                    out=stg[:], in0=xio[b][:, g * 1024:(g + 1) * 1024],
                    scalar1=statm[:, stc:stc + 1], scalar2=tmp[:, 4 + g:5 + g],
                    op0=ALU.subtract, op1=ALU.mult),
                    waits=w_q, ms=True)
                qv = V(lambda e: e.tensor_scalar(out=stg[:], in0=stg[:], scalar1=RMAGIC, scalar2=RMAGIC, op0=ALU.add, op1=ALU.subtract), ms=True)
                prev_ev = a_a.get((i, g - 1)) or a_a.get((i - 1, 1))
                for j in range(8):
                    w_c = [(sv, qv)]
                    if (g, j) not in ((0, 0), (0, 1)):
                        pj = (g, j - 2) if j >= 2 else (g - 1, 6 + j)
                        w_c.append((st_, tg.get(pj, 0)))
                    G(lambda e, j=j: e.tensor_copy(out=blkint[:, 128 * (j % 2):128 * (j % 2) + 128], in_=stg[:, 128 * j:128 * (j + 1)]),
                      waits=w_c, ms=True)
                    cp_ms[(g, j)] = G(lambda e, j=j: e.tensor_copy(out=blk[:, 128 * (j % 2):128 * (j % 2) + 128], in_=blkint[:, 128 * (j % 2):128 * (j % 2) + 128]),
                                      ms=True)
                    w_t = [(sp, cp_ms[(g, j)])]
                    if j == 0 and prev_ev:
                        w_t.append((sa, prev_ev))
                    tg[(g, j)] = T(lambda e, g=g, j=j: e.transpose(
                        out=ptrb[:, 128 * j:128 * (j + 1)],
                        in_=blk[:, 128 * (j % 2):128 * (j % 2) + 128],
                        identity=identbf[:]),
                        waits=w_t, ms=True)
                dest = slot_view(slot)[:, :, 1 + 128 * g:1 + 128 * g + 128]
                a_a[(i, g)] = A_(lambda e, dest=dest: e.copy(out=dest, in_=ptrb[:].rearrange("p (c n) -> p c n", n=128)),
                                 waits=[(st_, tg[(g, 7)])], ms=True)
            qs_rdy = qs_rdy_g[1]
            w_ = [(sv, qs_rdy), (sv, nonloc["pbc"])]
            T(lambda e: e.transpose(out=pbcf[0:1, 0:128], in_=tmp[:, 6:7], identity=identf[:]), waits=w_)
            tr2 = T(lambda e: e.transpose(out=pbcf[0:1, 128:256], in_=tmp[:, 7:8], identity=identf[:]), ms=True)
            ev = V(lambda e, slot=slot: e.tensor_copy(out=r1row[0:1, slot * 258 + 1:slot * 258 + 257], in_=pbcf[0:1, 0:256]),
                   waits=[(st_, tr2)], ms=True)
            nonloc["pbc"] = ev
            hv = [(sa, a_a[(i, 1)]), (st_, b_mm1_done.get(i - 2, 0))]
            if sfirst:
                V(lambda e, slot=slot: e.memset(slot_view(slot)[:, :, 0:1], 0.0), waits=hv, ms=True)
                V(lambda e, slot=slot: e.memset(r1row[0:1, slot * 258:slot * 258 + 1], 1.0), ms=True)
            else:
                prev = (i - 1) % 3
                V(lambda e, slot=slot, prev=prev: e.tensor_copy(
                    out=slot_view(prev)[:, :, 257:258], in_=slot_view(slot)[:, :, 1:2]), waits=hv, ms=True)
                V(lambda e, slot=slot, prev=prev: e.tensor_copy(out=r1row[0:1, prev * 258 + 257:prev * 258 + 258], in_=r1row[0:1, slot * 258 + 1:slot * 258 + 2]), ms=True)
            if slast:
                V(lambda e, slot=slot: e.memset(slot_view(slot)[:, :, 257:258], 0.0), ms=True)
                V(lambda e, slot=slot: e.memset(r1row[0:1, slot * 258 + 257:slot * 258 + 258], 1.0), ms=True)
            else:
                nxt = (i + 1) % 3
                V(lambda e, slot=slot, nxt=nxt: e.tensor_copy(
                    out=slot_view(nxt)[:, :, 0:1], in_=slot_view(slot)[:, :, 256:257]), ms=True)
                V(lambda e, slot=slot, nxt=nxt: e.tensor_copy(out=r1row[0:1, nxt * 258:nxt * 258 + 1], in_=r1row[0:1, slot * 258 + 256:slot * 258 + 257]), ms=True)
            a_v[i] = sv.v
            a_t[i] = tg[(1, 7)]

        def b_part(i, halo_v):
            slot = i % 3
            b = i % 2
            w_ = [(sv, halo_v), (sv, nonloc["pbc"]), (sa, a_a[(i, 1)])]
            tb = T(lambda e, slot=slot: e.matmul(out=pbcf[:, 0:258], lhsT=onesf[0:1, :], rhs=r1row[0:1, slot * 258:slot * 258 + 258], start=True, stop=True, skip_group_check=True),
                   waits=w_, ms=True)
            r1bc_ms = V(lambda e: e.tensor_copy(out=r1bc[:], in_=pbcf[:, 0:258]), waits=[(st_, tb)], ms=True)
            nonloc["pbc"] = r1bc_ms

            silu_ms = {}
            tree_ms = {}
            mm1_ms = {}
            if CUT >= 1:
              for c in range(32):
                bank = c % 2
                w_mm = [(sv, evac_ms.get(bank, 0))]
                if c == 0:
                    w_mm.append((sv, r1bc_ms))
                    if b_stats_a.get(i - 1):
                        w_mm.append((sa, b_stats_a[i - 1]))
                    if nonloc.get("first_mm"):
                        w_mm += list(mm0_wait)
                        nonloc["first_mm"] = False
                for d_ in range(8):
                    mm1_ms[c] = T(lambda e, c=c, d_=d_, bank=bank, slot=slot: e.matmul(
                        out=ph[bank], lhsT=w1q[:, d_ * 4096 + c * 128:d_ * 4096 + (c + 1) * 128],
                        rhs=yqT[:, slot * 2064 + d_ * 258:slot * 2064 + (d_ + 1) * 258],
                        start=(d_ == 0), stop=(d_ == 7), skip_group_check=True),
                        waits=w_mm if d_ == 0 else (), ms=(d_ == 7))
                if c == 31:
                    b_mm1_done[i] = mm1_ms[c]
                hc = hbuf[:, c * 258:(c + 1) * 258]
                ev = V(lambda e, hc=hc, bank=bank: e.tensor_tensor(out=hc, in0=ph[bank], in1=r1bc[:], op=ALU.mult),
                       waits=[(st_, mm1_ms[c])], ms=True)
                evac_ms[bank] = ev
                if CUT >= 2:
                    cvb = cvb_t[:, 256 * (c % 2):256 * (c % 2) + 256]
                    w_t1 = [(sv, ev)]
                    if c >= 2:
                        w_t1.append((sv, tree_ms[c - 2]))
                    t1 = A_(lambda e, hc=hc, cvb=cvb, c=c: e.activation(
                        out=cvb, in_=hc[:, 1:257], func=AF.Identity,
                        bias=cw[:, 96 + (c % 32):97 + (c % 32)], scale=cw[:, 32 + (c % 32):33 + (c % 32)]),
                        waits=w_t1, ms=True)
                    t2 = V(lambda e, hc=hc, cvb=cvb, c=c: e.scalar_tensor_tensor(
                        out=cvb, in0=hc[:, 0:256], scalar=cw[:, (c % 32):(c % 32) + 1], in1=cvb, op0=ALU.mult, op1=ALU.add),
                        waits=[(sa, t1)], ms=True)
                    if not os.environ.get("ATH_POOLTAP"):
                        t3 = V(lambda e, hc=hc, cvb=cvb, c=c: e.scalar_tensor_tensor(
                            out=cvb, in0=hc[:, 2:258], scalar=cw[:, 64 + (c % 32):65 + (c % 32)], in1=cvb, op0=ALU.mult, op1=ALU.add),
                            waits=[(sv, t2)], ms=True)
                        A_(lambda e, hc=hc, cvb=cvb: e.activation(out=hc[:, 1:257], in_=cvb, func=AF.Silu),
                           waits=[(sv, t3)])
                    else:
                        G(lambda e, hc=hc, c=c: e.tensor_scalar(
                            out=accp[:], in0=hc[:, 2:258], scalar1=cw[:, 64 + (c % 32):65 + (c % 32)], scalar2=None, op0=ALU.mult),
                          waits=[(sv, ev)])
                        t3 = G(lambda e, cvb=cvb: e.tensor_tensor(out=cvb, in0=cvb, in1=accp[:], op=ALU.add),
                               waits=[(sv, t2)], ms=True)
                        A_(lambda e, hc=hc, cvb=cvb: e.activation(out=hc[:, 1:257], in_=cvb, func=AF.Silu),
                           waits=[(sp, t3)])
                    silu_ms[c] = A_(lambda e, hc=hc, cvb=cvb: e.activation(out=cvb, in_=hc[:, 1:257], func=AF.Abs), ms=True)
                    if c == 0:
                        tree_ms[c] = V(lambda e, cvb=cvb: e.tensor_copy(out=accd[:], in_=cvb),
                                       waits=[(sa, silu_ms[c])], ms=True)
                    else:
                        tree_ms[c] = V(lambda e, cvb=cvb: e.tensor_tensor(out=accd[:], in0=accd[:], in1=cvb, op=ALU.max),
                                       waits=[(sa, silu_ms[c])], ms=True)
            outsc_rdy = 0
            if CUT >= 3:
                comb = tree_ms[31]
                w_ = [(sv, comb), (sv, nonloc["pbc"])]
                T(lambda e: e.transpose(out=pbcf[:, 0:128], in_=accd[:, 0:128], identity=identf[:]), waits=w_)
                tt = T(lambda e: e.transpose(out=pbcf[:, 128:256], in_=accd[:, 128:256], identity=identf[:]), ms=True)
                rd1 = V(lambda e: e.reduce_max(out=tmp[:, 8:9], in_=pbcf[:, 0:128], axis=AX.X), waits=[(st_, tt)], ms=True)
                rd2 = V(lambda e: e.reduce_max(out=tmp[:, 9:10], in_=pbcf[:, 128:256], axis=AX.X), waits=[(sv, rd1)], ms=True)
                vv = V(lambda e: e.tensor_scalar(out=tmp[:, 10:12], in0=tmp[:, 8:10], scalar1=1e-5, scalar2=None, op0=ALU.max), waits=[(sv, rd2)], ms=True)
                vv = V(lambda e: e.reciprocal(out=tmp[:, 12:14], in_=tmp[:, 10:12]), waits=[(sv, vv)], ms=True)
                qs2_rdy = V(lambda e: e.tensor_scalar(out=tmp[:, 12:14], in0=tmp[:, 12:14], scalar1=127.0, scalar2=None, op0=ALU.mult), waits=[(sv, vv)], ms=True)
                outsc_rdy = V(lambda e: e.tensor_scalar(out=tmp[:, 14:16], in0=tmp[:, 10:12], scalar1=wsc[:, 2:3], scalar2=0.5 / 127.0, op0=ALU.mult, op1=ALU.mult), waits=[(sv, qs2_rdy)], ms=True)
                tq = T(lambda e: e.transpose(out=pbcf[0:1, 256:384], in_=tmp[:, 12:13], identity=identf[:]), waits=[(sv, qs2_rdy)])
                tq = T(lambda e: e.transpose(out=pbcf[0:1, 384:512], in_=tmp[:, 13:14], identity=identf[:]), ms=True)
                evq = V(lambda e: e.tensor_copy(out=r2row[0:1, 0:256], in_=pbcf[0:1, 256:512]), waits=[(st_, tq)], ms=True)
                tbc2 = T(lambda e: e.matmul(out=pbcf[:, 0:256], lhsT=onesf[0:1, :], rhs=r2row[0:1, 0:256], start=True, stop=True, skip_group_check=True),
                         waits=[(sv, evq)], ms=True)
                r2bc_ms = V(lambda e: e.tensor_copy(out=r2bc, in_=pbcf[:, 0:256]), waits=[(st_, tbc2)], ms=True)
                nonloc["pbc"] = r2bc_ms
            ev = r1bc_ms
            if CUT >= 4:
                accf = [accd[:], accp[:]]
                mm2_ms = {}
                qt_ms = {}
                i2_ms = {}
                for c in range(32):
                    sq = sqbf[:, 256 * (c % 2):256 * (c % 2) + 256]
                    acc = accf[c % 2]
                    w_ = [(sv, r2bc_ms)] if c < 2 else []
                    V(lambda e, c=c, acc=acc: e.tensor_tensor(
                        out=acc, in0=hbuf[:, c * 258 + 1:c * 258 + 257], in1=r2bc, op=ALU.mult),
                        waits=w_, ms=True)
                    w_ = [] if c < 1 else [(sp, i2_ms[c - 1])]
                    qt_ms[c] = V(lambda e, acc=acc: e.tensor_scalar(
                        out=sqint[:], in0=acc, scalar1=RMAGIC, scalar2=RMAGIC, op0=ALU.add, op1=ALU.subtract),
                        waits=w_, ms=True)
                    w_ = [(sv, qt_ms[c])]
                    if c >= 2:
                        w_.append((st_, mm2_ms[c - 2]))
                    i2_ms[c] = G(lambda e, c=c, sq=sq: e.tensor_copy(out=sq, in_=sqint[:]),
                                 waits=w_, ms=True)
                    for g in range(2):
                        for n2 in range(2):
                            w_m = [(sp, i2_ms[c])]
                            if c == 0 and g == 0 and n2 == 0 and po_free is not None:
                                w_m.append((sv, po_free))
                            mm2_ms[c] = T(lambda e, c=c, g=g, n2=n2, sq=sq: e.matmul(
                                out=po[g][:, n2 * 512:(n2 + 1) * 512],
                                lhsT=sq[:, g * 128:(g + 1) * 128],
                                rhs=w2q[:, c * 1024 + n2 * 512:c * 1024 + n2 * 512 + 512],
                                start=(c == 0), stop=(c == 31), skip_group_check=True),
                                waits=w_m if (g == 0 and n2 == 0) else (), ms=(g == 1 and n2 == 1))
                ev = None
            stat_aa = 0
            for g in range(2):
                if CUT >= 4:
                    for n2 in range(2):
                        ev = V(lambda e, g=g, n2=n2, b=b: e.scalar_tensor_tensor(
                            out=xio[b][:, g * 1024 + n2 * 512:g * 1024 + n2 * 512 + 512],
                            in0=po[g][:, n2 * 512:(n2 + 1) * 512],
                            scalar=tmp[:, 14 + g:15 + g],
                            in1=xio[b][:, g * 1024 + n2 * 512:g * 1024 + n2 * 512 + 512],
                            op0=ALU.mult, op1=ALU.add),
                            waits=[(st_, mm2_ms[31]), (sv, outsc_rdy)] if (g == 0 and n2 == 0) else (), ms=True)
                stat_vv = 0
                if collect_stats:
                    stc = 2 * i + g
                    stat_vv = V(lambda e, g=g, b=b, stc=stc: e.reduce_sum(out=ssum[:, stc:stc + 1], in_=xio[b][:, g * 1024:(g + 1) * 1024], axis=AX.X), ms=True)
                    stat_aa = A_(lambda e, g=g, b=b, stc=stc: e.activation(out=hbuf[:, 0:1024], in_=xio[b][:, g * 1024:(g + 1) * 1024], func=AF.Square, accum_out=ssq[:, stc:stc + 1]),
                                 waits=[(sv, stat_vv)], ms=True)
                b_store[(i, g)] = GD(lambda e, i=i, b=b, g=g: e.dma_start(
                    out=dstv[i, :, g, :], in_=xio[b][:, g * 1024:(g + 1) * 1024]),
                    waits=[(sv, stat_vv or ev), (sa, stat_aa)], sem=(dg2 if g == 0 else dg3)[b])
            po_free_local = ev
            if collect_stats:
                b_stats_a[i] = stat_aa
            return po_free_local

        for i in range(NT):
            if i < 2:
                w_ = [(sv, phase_wait_v), (sa, phase_wait_a)] + list(extra_first)
                w1_ = list(w_)
            else:
                w_ = [(dg2[i % 2], b_store[(i - 2, 0)])]
                w1_ = [(dg3[i % 2], b_store[(i - 2, 1)])]
            x_ld[i] = LD(lambda e, i=i: e.dma_start(out=xio[i % 2][:, 0:1024], in_=srcv[i, :, 0, :]),
                         waits=w_, sem=dx2[i % 2])
            x_ld2[i] = LD(lambda e, i=i: e.dma_start(out=xio[i % 2][:, 1024:2048], in_=srcv[i, :, 1, :]),
                          waits=w1_, sem=dx3[i % 2])
            a_part(i)
            if dbg4 is not None and 2 < i < 6 and os.environ.get("ATH_SNAP"):
                GD(lambda e, i=i: e.dma_start(out=dbg4[:, 260 + 32 * (i - 3):292 + 32 * (i - 3)], in_=statr[:]),
                   waits=[(sv, a_v[i])])
            if i >= 1:
                po_free = b_part(i - 1, a_v[i])
        po_free = b_part(NT - 1, a_v[NT - 1])
        emit_ffn.b_store = dict(b_store)
        emit_ffn.last_pbc = nonloc["pbc"]
        emit_ffn.store_tot = [(dg2[0], dg2[0].v), (dg2[1], dg2[1].v), (dg3[0], dg3[0].v), (dg3[1], dg3[1].v)]
        return sv.v, sa.v


    def emit_attn(src, dst, phase_wait_v, phase_wait_a, store_wait, pbc0):
        srcv = src.rearrange("(g p) d -> p g d", p=128)
        dstv = dst.rearrange("(g p) d -> p g d", p=128)

        wq_b = arh(0, 8192).rearrange("p (c n) -> p c n", n=1024)
        wk_b = arh(4096, 8192).rearrange("p (c n) -> p c n", n=1024)
        wv_b = arh(8192, 8192).rearrange("p (c n) -> p c n", n=1024)
        yT = arh(12288, 16384).rearrange("p (c n) -> p c n", n=2048)
        Kb = arh(20480, 16384).rearrange("p (c n) -> p c n", n=2048)
        Vb = arh(28672, 16640).rearrange("p (c n) -> p c n", n=1040)
        Vb_hk = arh(28672, 16640).rearrange("p (c h k) -> p c h k", h=16, k=65)
        xa = [arf(36992, 1024), arf(38016, 1024)]
        cosb = arh(39040, 1024)
        sinb = arh(39552, 1024)
        bqr = arh(40064, 1024)[0:1, :]
        bvr = arh(40576, 1024)[0:1, :]
        bor = arh(41088, 1024)[0:1, :]
        e16f = arf(41600, 256)[0:1, :]
        selb = arf(41856, 1024)[0:16, :]
        ones128 = arh(42880, 128)[0:1, :]
        ones512 = arh(42944, 512)[0:1, :]
        wo_b = arh(0, 8192).rearrange("p (c n) -> p c n", n=1024)
        Qf = arh(4096, 16384).rearrange("p (c n) -> p c n", n=2048)
        A_sb = arf(12288, 4096)
        Abf = arh(16384, 4096)
        eS = arh(18432, 2048)
        rd_sb = stg[0:16, 0:512]
        selbf = arh(43456, 1024)[0:16, :]
        e16bf = arh(44000, 256)[0:1, :]
        dnbf = arh(44200, 1024)[0:1, :]
        rdbf = arh(44800, 512)[0:16, :]
        stgbf = stg[:].bitcast(BF16)[:, 0:1024]
        cvbbf = cvb_t[:].bitcast(BF16)[:, 0:1024]

        psA = [P[:, 0:512], P[:, 512:1024]]
        psB = [P[:, 1024:1536], P[:, 1536:2048]]
        pv2 = P[:, 1024:2048]
        pav_e = P[:, 2048:2560]
        pav_o = P[:, 2560:3072]
        plv = P[0:16, 3072:3584]
        pR = P[:, 3584:4096]

        pr.op("gpsimd", lambda e: e.nop(),
              waits=[(sv, phase_wait_v), (sa, phase_wait_a), (st_, st_.v), (sp, sp.v),
                     (dsp, dsp.v)])
        wqv = wqd.rearrange("(c p) o -> c p o", p=128)
        wkv = wkd.rearrange("(c p) o -> c p o", p=128)
        wvv = wvd.rearrange("(c p) o -> c p o", p=128)

        def load_qkv(wkv_extra=(), wq_extra=()):
            first = list(wkv_extra)
            for wsrc, wdst in ((wkv, wk_b), (wvv, wv_b)):
                for c in range(8):
                    GD(lambda e, wsrc=wsrc, wdst=wdst, c=c: e.dma_start(
                        out=wdst[:, c, :], in_=wsrc[c, :, :]), waits=first, sem=dsp)
                    first = []
            wkv_done_ = dsp.v
            first = list(wq_extra)
            for c in range(8):
                GD(lambda e, c=c: e.dma_start(
                    out=wq_b[:, c, :], in_=wqv[c, :, :]), waits=first, sem=dsq)
                first = []
            return wkv_done_, dsq.v

        wkv_done, wq_done = load_qkv()
        GD(lambda e: e.dma_start(out=cosb, in_=cosd), sem=dsp)
        GD(lambda e: e.dma_start(out=sinb, in_=sind), sem=dsp)
        wl_done = dsp.v
        wkv_done = wl_done
        bl = LD(lambda e: e.dma_start(out=stg[0:1, 0:1024], in_=bqd), waits=[(sv, phase_wait_v), (sa, phase_wait_a)] + list(store_wait))
        bc = V(lambda e: e.tensor_copy(out=bqr, in_=stg[0:1, 0:1024]), waits=[(dw, bl)], ms=True)
        bl = LD(lambda e: e.dma_start(out=stg[0:1, 0:1024], in_=bvd), waits=[(sv, bc)])
        bc = V(lambda e: e.tensor_copy(out=bvr, in_=stg[0:1, 0:1024]), waits=[(dw, bl)], ms=True)
        bl = LD(lambda e: e.dma_start(out=stg[0:1, 0:1024], in_=bod), waits=[(sv, bc)])
        bc = V(lambda e: e.tensor_copy(out=bor, in_=stg[0:1, 0:1024]), waits=[(dw, bl)], ms=True)
        bl = LD(lambda e: e.dma_start(out=e16f, in_=e16_d), waits=[(sv, bc)])
        bc = V(lambda e: e.tensor_copy(out=e16bf, in_=e16f), waits=[(dw, bl)], ms=True)
        bl = LD(lambda e: e.dma_start(out=selb, in_=sel_d), waits=[(sv, bc)])
        bc = V(lambda e: e.tensor_copy(out=selbf, in_=selb), waits=[(dw, bl)], ms=True)
        V(lambda e: e.memset(ones128, 1.0))
        ones_ms = V(lambda e: e.memset(ones512, 1.0), ms=True)

        pbc = pbc0
        pl_rd_a = 0
        xa_user = {0: ones_ms, 1: ones_ms}
        xa_ld = {0: 0, 1: 0}
        b_st2 = {0: 0, 1: 0}
        eS_rd = {}
        psA_rd = {0: 0, 1: 0}
        psB_rd = {0: 0, 1: 0}
        psB_rd_v = {0: 0, 1: 0}
        pk_free = {}
        stats_ms_a = 0
        kb_dump = {}

        aseq = int(os.environ.get("ATH_ASEQ", "2"))
        aqt = int(os.environ.get("ATH_AQT", "3"))
        for s_ in range(aseq):
            seq0_T_done = st_.v
            if s_ == 1:
                wkv_done, wq_done = load_qkv(
                    wkv_extra=[(st_, kb_dump.get("qf_dead", st_.v))],
                    wq_extra=[(st_, st_.v)])
                wl_done = dsp.v
            st_t = {}
            ev_a = {}
            for st in range(16):
                g = s_ * 16 + st
                par = st % 2
                w_ = [(sv, xa_user[par])]
                if s_ == 0 and st < 2:
                    w_ += list(store_wait)
                if b_st2[par]:
                    w_.append((dg2[par], b_st2[par]))
                xa_ld[par] = LD(lambda e, g=g, par=par: e.dma_start(out=xa[par], in_=srcv[:, g, :]),
                                waits=w_, sem=dx2[par])
                w_ = [(dx2[par], xa_ld[par])]
                if s_ == 0:
                    w_.append((dsp, wl_done))
                elif st == 0:
                    w_.append((sa, pl_rd_a))
                if st >= 1:
                    w_.append((st_, st_t[st - 1]))
                if s_ == 1 and st == 0 and 0 in kb_dump:
                    w_.append((dgd, kb_dump[0]))
                xa_user[par] = V(lambda e, par=par, g=g: e.tensor_scalar(
                    out=stgbf, in0=xa[par], scalar1=statm[:, g:g + 1], scalar2=statr[:, g:g + 1],
                    op0=ALU.subtract, op1=ALU.mult), waits=w_, ms=True)
                for j in range(8):
                    w_ = [(sv, xa_user[par])]
                    if ev_a.get(st - 1):
                        w_.append((sa, ev_a[st - 1]))
                    elif s_ == 1:
                        w_.append((sa, pl_rd_a))
                    st_t[st] = T(lambda e, j=j: e.transpose(
                        out=ptrb[:, 128 * j:128 * (j + 1)],
                        in_=stgbf[:, 128 * j:128 * (j + 1)],
                        identity=identbf[:]), waits=w_ if j == 0 else (), ms=(j == 7))
                dest = yT[:, :, 128 * st:128 * (st + 1)]
                w_ev = [(st_, st_t[st])]
                if s_ == 1 and st == 0:
                    w_ev.append((st_, seq0_T_done))
                ev_a[st] = A_(lambda e, dest=dest: e.copy(out=dest, in_=ptrb[:].rearrange("p (c n) -> p c n", n=128)),
                              waits=w_ev, ms=True)
            yT_a = ev_a[15]
            rope_guard = st_t[15]

            for oc in range(8):
                for tt in range(4):
                    bank = (oc * 4 + tt) % 2
                    w_ = [(sa, yT_a), (sv, pk_free.get(bank, 0)), (sa, psA_rd[bank])]
                    if oc == 0 and tt == 0:
                        w_.append((dsp, wkv_done))
                    for ic in range(8):
                        mmk = T(lambda e, oc=oc, tt=tt, ic=ic, bank=bank: e.matmul(
                            out=psA[bank], lhsT=wk_b[:, ic, 128 * oc:128 * (oc + 1)],
                            rhs=yT[:, ic, 512 * tt:512 * (tt + 1)],
                            start=(ic == 0), stop=(ic == 7), skip_group_check=True),
                            waits=w_ if ic == 0 else (), ms=(ic == 7))
                    pk_free[bank] = V(lambda e, oc=oc, tt=tt, bank=bank: e.tensor_copy(
                        out=Kb[:, oc, 512 * tt:512 * (tt + 1)], in_=psA[bank]),
                        waits=[(st_, mmk)], ms=True)

            for tc in range(16):
                w_ = [(sv, pk_free.get(2, 0)), (sv, psB_rd_v[0]), (sv, psB_rd_v[1]), (sa, psB_rd[0]), (sa, psB_rd[1])]
                for ic in range(8):
                    for n2 in range(2):
                        T(lambda e, tc=tc, ic=ic, n2=n2: e.matmul(
                            out=pv2[:, 512 * n2:512 * (n2 + 1)],
                            lhsT=yT[:, ic, 128 * tc:128 * (tc + 1)],
                            rhs=wv_b[:, ic, 512 * n2:512 * (n2 + 1)],
                            start=(ic == 0), stop=False, skip_group_check=True),
                            waits=w_ if (ic == 0 and n2 == 0) else ())
                for n2 in range(2):
                    mmv = T(lambda e, n2=n2: e.matmul(
                        out=pv2[:, 512 * n2:512 * (n2 + 1)], lhsT=ones128,
                        rhs=bvr[0:1, 512 * n2:512 * (n2 + 1)],
                        start=False, stop=True, skip_group_check=True), ms=(n2 == 1))
                pk_free[2] = V(lambda e, tc=tc: e.tensor_copy(
                    out=Vb_hk[:, tc, :, 0:64], in_=pv2[:].rearrange("p (h k) -> p h k", k=64)),
                    waits=[(st_, mmv)], ms=True)
            vone_ms = V(lambda e: e.memset(Vb_hk[:, :, :, 64:65], 1.0), ms=True)
            psB_rd = {0: 0, 1: 0}
            psB_rd_v = {0: 0, 1: 0}
            qf_guard = 0
            for oc in range(8):
                for tt in range(4):
                    bank = (oc * 4 + tt) % 2
                    w_ = [(sv, pk_free.get(bank, 0)), (sa, psA_rd[bank])]
                    if oc == 0 and tt == 0:
                        w_.append((dsq, wq_done))
                    if oc < 2 and tt == 0:
                        w_.append((sv, vone_ms))
                    for ic in range(8):
                        T(lambda e, oc=oc, tt=tt, ic=ic, bank=bank: e.matmul(
                            out=psA[bank], lhsT=wq_b[:, ic, 128 * oc:128 * (oc + 1)],
                            rhs=yT[:, ic, 512 * tt:512 * (tt + 1)],
                            start=(ic == 0), stop=False, skip_group_check=True),
                            waits=w_ if ic == 0 else ())
                    mmq = T(lambda e, oc=oc, tt=tt, bank=bank: e.matmul(
                        out=psA[bank], lhsT=bqr[0:1, 128 * oc:128 * (oc + 1)], rhs=ones512,
                        start=False, stop=True, skip_group_check=True), ms=True)
                    qf_guard = mmq
                    pk_free[bank] = V(lambda e, oc=oc, tt=tt, bank=bank: e.tensor_copy(
                        out=Qf[:, oc, 512 * tt:512 * (tt + 1)], in_=psA[bank]),
                        waits=[(st_, mmq)], ms=True)
            pr.op("gpsimd", lambda e: e.nop(), waits=[(st_, qf_guard)])
            wov = wod.rearrange("(c p) o -> c p o", p=128)
            for c in range(8):
                GD(lambda e, c=c: e.dma_start(out=wo_b[:, c, :], in_=wov[c, :, :]), sem=dsp)
            wo_done = dsp.v

            def rope(buf_v, oc, col0, width, coff, extra=()):
                sl = lambda a, b: buf_v[a:b, oc, col0:col0 + width]
                V(lambda e: e.tensor_scalar(out=stgbf[0:32, 0:width], in0=sl(32, 64), scalar1=-1.0, scalar2=None, op0=ALU.mult),
                  waits=list(extra))
                V(lambda e: e.tensor_copy(out=stgbf[32:64, 0:width], in_=sl(0, 32)))
                V(lambda e: e.tensor_scalar(out=stgbf[64:96, 0:width], in0=sl(96, 128), scalar1=-1.0, scalar2=None, op0=ALU.mult))
                V(lambda e: e.tensor_copy(out=stgbf[96:128, 0:width], in_=sl(64, 96)))
                V(lambda e: e.tensor_tensor(out=cvbbf[:, 0:width], in0=sl(0, 128), in1=cosb[:, coff:coff + width], op=ALU.mult))
                V(lambda e: e.tensor_tensor(out=stgbf[:, 0:width], in0=stgbf[:, 0:width], in1=sinb[:, coff:coff + width], op=ALU.mult))
                return V(lambda e: e.tensor_tensor(out=sl(0, 128), in0=cvbbf[:, 0:width], in1=stgbf[:, 0:width], op=ALU.add), ms=True)

            rope_ms = 0
            for oc in range(8):
                rope_ms = rope(Kb, oc, 1024, 1024, 0,
                               extra=[(st_, rope_guard)] if oc == 0 else ())
            kbtrace = dbg2 is not None and s_ == 0 and os.environ.get("ATH_KBTRACE")
            if dbg2 is not None and s_ == 0 and not kbtrace:
                kb_dump[0] = GD(lambda e: e.dma_start(out=dbg2[:, 0:8192], in_=arf(20480, 8192)),
                                waits=[(sv, rope_ms)])
            for oc in range(8):
                rope_ms = rope(Qf, oc, 1024, 1024, 0)

            def kbdump(slot, waits):
                kb_dump[0] = GD(lambda e, slot=slot: e.dma_start(
                    out=dbg2[:, 1024 * slot:1024 * (slot + 1)], in_=arf(20480, 1024)), waits=waits)
            if kbtrace:
                kbdump(0, [(sv, rope_ms)])

            A_ev = 0
            dn_ev = 0
            abf_ms = 0
            oproj_t = 0
            for qt in range(aqt + 1):
                q0 = 512 * qt
                qev = rope_ms
                for p in range(8):
                    for cp in range(8):
                        mA = mB = None
                        for cc in range(2):
                            c = 2 * cp + cc
                            w_ = [(sa, psA_rd[0])] if cc == 0 else []
                            if p == 0 and c < 2:
                                w_.append((sv, qev))
                            mA = T(lambda e, p=p, c=c, cc=cc, q0=q0: e.matmul(
                                out=psA[cc], lhsT=Kb[0:64, p, 128 * c:128 * (c + 1)],
                                rhs=Qf[0:64, p, q0:q0 + 512], start=True, stop=True,
                                tile_position=(0, 0), skip_group_check=True),
                                waits=w_, ms=(cc == 1))
                            w_ = [(sa, psB_rd[0])] if cc == 0 else []
                            if p == 0 and c < 2:
                                w_.append((sv, psB_rd_v[cc]))
                            mB = T(lambda e, p=p, c=c, cc=cc, q0=q0: e.matmul(
                                out=psB[cc], lhsT=Kb[64:128, p, 128 * c:128 * (c + 1)],
                                rhs=Qf[64:128, p, q0:q0 + 512], start=True, stop=True,
                                tile_position=(64, 0), skip_group_check=True),
                                waits=w_, ms=(cc == 1))
                        w_ = [(st_, mA)]
                        for h_ in (0, 1):
                            if eS_rd.get((0, h_)):
                                w_.append((st_, eS_rd[(0, h_)]))
                        eA = A_(lambda e: e.activation(
                            out=eS[:, 0:1024], in_=P[:, 0:1024], func=AF.Exp,
                            bias=consts[:, CL_NSH:CL_NSH + 1], scale=0.125),
                            waits=w_, ms=True)
                        psA_rd[0] = psA_rd[1] = eA
                        w_ = [(st_, mB)]
                        for h_ in (0, 1):
                            if eS_rd.get((1, h_)):
                                w_.append((st_, eS_rd[(1, h_)]))
                        eB = A_(lambda e: e.activation(
                            out=eS[:, 1024:2048], in_=P[:, 1024:2048], func=AF.Exp,
                            bias=consts[:, CL_NSH:CL_NSH + 1], scale=0.125),
                            waits=w_, ms=True)
                        psB_rd[0] = psB_rd[1] = eB
                        for cc in range(2):
                            c = 2 * cp + cc
                            w_ = [(sa, eA)] if cc == 0 else []
                            if c == 0:
                                w_.append((sv, A_ev))
                                w_.append((sa, stats_ms_a))
                            avA = T(lambda e, p=p, c=c, cc=cc: e.matmul(
                                out=pav_e[0:65, :], lhsT=Vb[:, c, 1040 * 0 + 65 * (2 * p):65 * (2 * p) + 65],
                                rhs=eS[:, 512 * cc:512 * (cc + 1)],
                                start=(c == 0), stop=(c == 15), skip_group_check=True),
                                waits=w_, ms=(c == 15))
                            w_ = [(sa, eB)] if cc == 0 else []
                            if c == 0:
                                w_.append((sv, A_ev))
                            avB = T(lambda e, p=p, c=c, cc=cc: e.matmul(
                                out=pav_o[0:65, :], lhsT=Vb[:, c, 65 * (2 * p + 1):65 * (2 * p + 1) + 65],
                                rhs=eS[:, 1024 + 512 * cc:1024 + 512 * (cc + 1)],
                                start=(c == 0), stop=(c == 15), skip_group_check=True),
                                waits=w_, ms=(c == 15 or cc == 1))
                            if cc == 1:
                                eS_rd[(0, 0)] = eS_rd[(0, 1)] = avB
                                eS_rd[(1, 0)] = eS_rd[(1, 1)] = avB
                    V(lambda e, p=p: e.tensor_copy(out=A_sb[0:64, 512 * p:512 * (p + 1)], in_=pav_e[0:64, :]),
                      waits=[(st_, avA), (st_, avB), (sa, stats_ms_a)])
                    A_ev = V(lambda e, p=p: e.tensor_copy(out=A_sb[64:128, 512 * p:512 * (p + 1)], in_=pav_o[0:64, :]), ms=True)
                    if kbtrace and qt == 1 and p == 7:
                        kbdump(3, [(sv, A_ev)])
                    V(lambda e: e.tensor_copy(out=dnbf[0:1, 0:512], in_=pav_e[64:65, :]))
                    dn_ev = V(lambda e: e.tensor_copy(out=dnbf[0:1, 512:1024], in_=pav_o[64:65, :]), ms=True)
                    w_ = [(sv, dn_ev)]
                    if p == 0:
                        w_.append((sa, pl_rd_a))
                    T(lambda e, p=p: e.matmul(out=plv, lhsT=e16bf[0:1, 16 * (2 * p):16 * (2 * p) + 16],
                                              rhs=dnbf[0:1, 0:512], start=(p == 0), stop=False, skip_group_check=True),
                      waits=w_)
                    pl_mm = T(lambda e, p=p: e.matmul(out=plv, lhsT=e16bf[0:1, 16 * (2 * p + 1):16 * (2 * p + 1) + 16],
                                                      rhs=dnbf[0:1, 512:1024], start=False, stop=(p == 7), skip_group_check=True), ms=True)
                if qt == 3:
                    kb_dump["qf_dead"] = st_.v
                lnm = A_(lambda e: e.activation(out=rd_sb, in_=plv, func=AF.Ln, bias=0.0, scale=1.0),
                         waits=[(st_, pl_mm)], ms=True)
                lnm = A_(lambda e: e.activation(out=rdbf, in_=rd_sb, func=AF.Exp, bias=0.0, scale=-1.0),
                         waits=[(sa, lnm)], ms=True)
                pl_rd_a = lnm
                for g_ in range(8):
                    mr = T(lambda e, g_=g_: e.matmul(out=pR, lhsT=selbf[:, 128 * g_:128 * (g_ + 1)], rhs=rdbf,
                                                     start=True, stop=True, skip_group_check=True),
                           waits=[(sa, lnm), (sv, pbc)], ms=True)
                    w_ = [(st_, mr)]
                    if oproj_t:
                        w_.append((st_, oproj_t))
                    abf_ms = V(lambda e, g_=g_: e.tensor_tensor(
                        out=Abf[:, 512 * g_:512 * (g_ + 1)], in0=pR[:],
                        in1=A_sb[:, 512 * g_:512 * (g_ + 1)], op=ALU.mult), waits=w_, ms=True)
                    pbc = abf_ms
                if kbtrace:
                    kbdump({0: 1, 1: 4, 2: 6, 3: 7}[qt], [(sv, pbc)])
                for tg in range(4):
                    stc = s_ * 16 + qt * 4 + tg
                    par = tg % 2
                    w_ = [(sv, xa_user[par])]
                    if b_st2[par]:
                        w_.append((dg2[par], b_st2[par]))
                    xa_ld[par] = LD(lambda e, stc=stc, par=par: e.dma_start(out=xa[par], in_=srcv[:, stc, :]),
                                    waits=w_, sem=dx2[par])
                    for n2 in range(2):
                        w_m = [(sv, abf_ms), (sv, psB_rd_v[n2]), (sa, psB_rd[n2]), (dsp, wo_done)]
                        for p in range(8):
                            T(lambda e, tg=tg, n2=n2, p=p: e.matmul(
                                out=psB[n2], lhsT=Abf[:, 512 * p + 128 * tg:512 * p + 128 * (tg + 1)],
                                rhs=wo_b[:, p, 512 * n2:512 * (n2 + 1)],
                                start=(p == 0), stop=False, skip_group_check=True),
                                waits=w_m if p == 0 else ())
                        mo = T(lambda e, n2=n2: e.matmul(out=psB[n2], lhsT=ones128,
                                                         rhs=bor[0:1, 512 * n2:512 * (n2 + 1)],
                                                         start=False, stop=True, skip_group_check=True), ms=True)
                        oproj_t = mo
                        psB_rd[n2] = 0
                        psB_rd_v[n2] = V(lambda e, n2=n2, par=par: e.tensor_tensor(
                            out=xa[par][:, 512 * n2:512 * (n2 + 1)], in0=psB[n2],
                            in1=xa[par][:, 512 * n2:512 * (n2 + 1)], op=ALU.add),
                            waits=[(st_, mo), (dx2[par], xa_ld[par])], ms=True)
                    sv_stat = V(lambda e, stc=stc, par=par: e.reduce_sum(out=ssum[:, stc:stc + 1], in_=xa[par], axis=AX.X), ms=True)
                    xa_user[par] = sv_stat
                    stats_ms_a = A_(lambda e, stc=stc, par=par: e.activation(
                        out=A_sb[:, 0:1024], in_=xa[par], func=AF.Square, accum_out=ssq[:, stc:stc + 1]),
                        waits=[(sv, sv_stat)], ms=True)
                    b_st2[par] = GD(lambda e, stc=stc, par=par: e.dma_start(out=dstv[:, stc, :], in_=xa[par]),
                                    waits=[(sv, sv_stat), (sa, stats_ms_a)], sem=dg2[par])
                if kbtrace and qt < 2:
                    kbdump({0: 2, 1: 5}[qt], [(sa, stats_ms_a)])
        emit_attn.qf_dead = kb_dump.get("qf_dead", st_.v)
        return sv.v, sa.v, pbc

    if DEV_PHASES == "ffn1":
        fv, fa = emit_ffn(x0, out_d, collect_stats=False,
                          phase_wait_v=fin_v, phase_wait_a=fin_a,
                          mm0_wait=[(sv, wq_done_v)])
    elif DEV_PHASES == "attn":
        fv, fa = emit_ffn(x0, x1d, collect_stats=True,
                          phase_wait_v=fin_v, phase_wait_a=fin_a,
                          mm0_wait=[(sv, wq_done_v)])
        f1v, f1a = finalize_stats(fv, fa)
        av, aa, apbc = emit_attn(x1d, out_d, f1v, f1a,
                                 store_wait=emit_ffn.store_tot, pbc0=emit_ffn.last_pbc)
    else:
        fv, fa = emit_ffn(x0, x1d, collect_stats=True,
                          phase_wait_v=fin_v, phase_wait_a=fin_a,
                          mm0_wait=[(sv, wq_done_v)])
        f1v, f1a = finalize_stats(fv, fa)
        att_dst = out_d if DEV_PHASES == "attn2" else x2d
        av, aa, apbc = emit_attn(x1d, att_dst, f1v, f1a,
                                 store_wait=emit_ffn.store_tot, pbc0=emit_ffn.last_pbc)
        POST = os.environ.get("ATH_POST", "fin,rl,pre")
        f2v, f2a = (av, aa)
        if "fin" in POST:
            f2v, f2a = finalize_stats(av, aa, nsteps=int(os.environ.get("ATH_FSTEPS", "9")))
        rl0 = dw.v
        if "rl" in POST:
            early = [(st_, emit_attn.qf_dead), (dsp, dsp.v)]
            for j in (2, 3, 4, 5):
                LD(lambda e, j=j: e.dma_start(out=w1q[:, j * 4096:(j + 1) * 4096], in_=w1qs_v[j, :, :]),
                   waits=early)
                early = []
            for j in range(4, 32):
                LD(lambda e, j=j: e.dma_start(out=w2q[:, j * 1024:(j + 1) * 1024], in_=w2qs_v[j, :, :]))
            late = [(sv, av), (sa, aa)]
            for j in (0, 1, 6, 7):
                LD(lambda e, j=j: e.dma_start(out=w1q[:, j * 4096:(j + 1) * 4096], in_=w1qs_v[j, :, :]),
                   waits=late)
                late = []
            for j in range(4):
                LD(lambda e, j=j: e.dma_start(out=w2q[:, j * 1024:(j + 1) * 1024], in_=w2qs_v[j, :, :]))
        rl_done = dw.v
        if "pre" in POST and os.environ.get("ATH_PRE"):
            A_(lambda e: e.preload_act_table(AF.Silu), waits=[(sa, f2a)])
        if DEV_PHASES != "attn2":
            fv, fa = emit_ffn(x2d, out_d, collect_stats=False,
                              phase_wait_v=f2v, phase_wait_a=f2a,
                              extra_first=[(dg2[0], dg2[0].v), (dg2[1], dg2[1].v),
                                           (dg3[0], dg3[0].v), (dg3[1], dg3[1].v)],
                              pbc0=apbc, mm0_wait=[(dw, rl_done)])

    if dbg1 is not None:
        bs = getattr(emit_ffn, "b_store", {})
        for k, col in ([] if os.environ.get("ATH_SNAP") else [(0, 0), (3, 32), (7, 64)]):
            if (k, 1) in bs:
                GD(lambda e, col=col: e.dma_start(out=dbg4[:, 260 + col:260 + col + 32], in_=statr[:]),
                   waits=[(dg3[k % 2], bs[(k, 1)])])
        GD(lambda e: e.dma_start(out=dbg1[:, 0:4], in_=wsc[:]), waits=[(sv, sv.v), (sa, sa.v), (sp, sp.v), (st_, st_.v)])
        GD(lambda e: e.dma_start(out=dbg1[:, 4:36], in_=statm[:]))
        GD(lambda e: e.dma_start(out=dbg1[:, 36:68], in_=statr[:]))
        GD(lambda e: e.dma_start(out=dbg1[:, 68:84], in_=tmp[:]))
        GD(lambda e: e.dma_start(out=dbg1[:, 84:92], in_=consts[:]))
        GD(lambda e: e.dma_start(out=dbg1[:, 124:156], in_=ssq[:]))
        GD(lambda e: e.dma_start(out=dbg4[:, 0:64], in_=guard1[:]))
        GD(lambda e: e.dma_start(out=dbg4[:, 64:128], in_=guard2[:]))
        GD(lambda e: e.dma_start(out=dbg4[:, 128:144], in_=tmp[:]))
        GD(lambda e: e.dma_start(out=dbg4[:, 144:148], in_=wsc[:]))
        GD(lambda e: e.dma_start(out=dbg4[:, 148:156], in_=consts[:]))
        GD(lambda e: e.dma_start(out=dbg4[:, 156:188], in_=statm[:]))
        GD(lambda e: e.dma_start(out=dbg2[:, :], in_=hbuf))

        if not os.environ.get("ATH_KBTRACE"):
            GD(lambda e: e.dma_start(out=dbg3[:, :], in_=yqT[:]))

    pr.op("gpsimd", lambda e: e.nop(),
          waits=[(dgd, dgd.v), (dsp, dsp.v), (dsq, dsq.v), (dg2[0], dg2[0].v), (dg2[1], dg2[1].v),
                 (dg3[0], dg3[0].v), (dg3[1], dg3[1].v),
                 (dx2[0], dx2[0].v), (dx2[1], dx2[1].v),
                 (dx3[0], dx3[0].v), (dx3[1], dx3[1].v), (dw, dw.v),
                 (dw2[0], dw2[0].v), (dw2[1], dw2[1].v)])

    if os.environ.get("ATH_TAGDUMP"):
        with open("_tagdump.txt", "w") as f:
            for eng in ENGINES:
                for i, (fn, ws, inc, incv) in enumerate(pr.q[eng]):
                    wstr = ",".join(f"{s.name}>={v}" for s, v in ws)
                    istr = f"+{inc.name}" if inc is not None else ""
                    f.write(f"{eng} {i} [{wstr}] {istr}\n")

    block = stack.enter_context(nc.Block())

    namelog = open("_instmap.txt", "w") if os.environ.get("ATH_TAGDUMP") else None

    def runner(ops, ename=""):
        def run(eng):
            for i, (fn, waits, inc, incv) in enumerate(ops):
                for s, v in waits:
                    eng.wait_ge(s.h, v)
                instr = fn(eng)
                if inc is not None:
                    instr.then_inc(inc.h, incv)
                if namelog is not None:
                    namelog.write(f"{ename} {i} {getattr(instr, 'name', '?')}\n")
        return run

    block.sync(runner(pr.q["sync"], "sync"))
    block.vector(runner(pr.q["vector"], "vector"))
    block.scalar(runner(pr.q["scalar"], "scalar"))
    block.tensor(runner(pr.q["tensor"], "tensor"))
    block.gpsimd(runner(pr.q["gpsimd"], "gpsimd"))
    if namelog is not None:
        namelog.close()

    stack.close()
    return nc



def kernel(**inputs):
    x = np.asarray(inputs["L0_attn_hidden_states"])
    cos = np.asarray(inputs["L0_attn_cos"])
    sin = np.asarray(inputs["L0_attn_sin"])
    wq = np.asarray(inputs["L0_attn_wq"])
    bq = np.asarray(inputs["L0_attn_bq"])
    wk = np.asarray(inputs["L0_attn_wk"])
    wv = np.asarray(inputs["L0_attn_wv"])
    bv = np.asarray(inputs["L0_attn_bv"])
    wo = np.asarray(inputs["L0_attn_wo"])
    bo = np.asarray(inputs["L0_attn_bo"])
    w1 = np.asarray(inputs["L0_ffn_w1"])
    cwv = np.asarray(inputs["L0_ffn_conv_w"])
    cb = np.asarray(inputs["L0_ffn_conv_b"])
    w2 = np.asarray(inputs["L0_ffn_w2"])

    w1t = np.ascontiguousarray(w1.T)
    w2t = np.ascontiguousarray(w2.T)
    cwm = np.zeros((128, 128), dtype=np.float32)
    taps = cwv[:, 0, :]
    for tap in range(3):
        cwm[:, 32 * tap:32 * tap + 32] = taps[:, tap].reshape(32, 128).T
    cwm[:, 96:128] = cb.reshape(32, 128).T
    cost = np.ascontiguousarray(np.tile(cos.T, (2, 1)).astype(np.float32, copy=False))
    sint = np.ascontiguousarray(np.tile(sin.T, (2, 1)).astype(np.float32, copy=False))

    identf = np.eye(128, dtype=np.float32)
    onesf = np.ones((128, 128), dtype=np.float32)
    e16 = np.zeros((1, 256), dtype=np.float32)
    for h in range(16):
        e16[0, 16 * h + h] = 1
    selm = np.zeros((16, 1024), dtype=np.float32)
    for g in range(8):
        selm[2 * g, 128 * g:128 * g + 64] = 1
        selm[2 * g + 1, 128 * g + 64:128 * g + 128] = 1
    constsv = np.zeros((128, 8), dtype=np.float32)
    constsv[:, 0] = 1e-5
    constsv[:, 1] = 1.0 / 127.0
    constsv[:, 2] = -C_SHIFT
    constsv[:, 3] = EPS
    constsv[:, 4] = 0.5

    nc = build_nc()
    shared = dict(w1t=w1t, w2t=w2t, cwdrm=cwm, wq=wq, wk=wk, wv=wv, wo=wo,
                  bq=bq.reshape(1, -1), bv=bv.reshape(1, -1), bo=bo.reshape(1, -1),
                  cosd=cost, sind=sint, identfd=identf,
                  onesfd=onesf, e16d=e16, seld=selm, constsd=constsv)
    in_maps = []
    for c in range(NCORE):
        m = dict(shared)
        m["x"] = np.ascontiguousarray(x[2 * c:2 * c + 2].reshape(NTOK, D))
        in_maps.append(m)

    res = run_bass_kernel_spmd(nc, in_maps, core_ids=list(range(NCORE)))
    if os.environ.get("ATH_DEBUG"):
        kernel._dbg = res.results
    out = np.stack([np.asarray(r["out"]) for r in res.results])
    return np.ascontiguousarray(out.reshape(16, 2048, 1024).astype(np.float32, copy=False))


# version 19
# speedup vs baseline: 1.0261x; 1.0162x over previous
import os
import numpy as np
import ml_dtypes
import concourse.bass as bass
import concourse.mybir as mybir
from concourse.bass_utils import run_bass_kernel_spmd
from contextlib import ExitStack

F32 = mybir.dt.float32
BF16 = mybir.dt.bfloat16
I32 = mybir.dt.int32
AF = mybir.ActivationFunctionType
ALU = mybir.AluOpType
AX = mybir.AxisListType

NCORE = 8
SEQ = 2048
NSEQ = 2
NTOK = SEQ * NSEQ
D = 1024
DH = 4096
NH, HD = 16, 64
TOK = 256
NT = NTOK // TOK
TPS = SEQ // TOK
C_SHIFT = 12.0
RMAGIC = 12582912.0
EPS = 1e-5

ENGINES = ("sync", "vector", "scalar", "tensor", "gpsimd")
DEV_PHASES = os.environ.get("ATH_PHASES", "full")
CUT = int(os.environ.get("ATH_CUT", "9"))


class Sem:
    __slots__ = ("h", "v", "name")
    def __init__(self, h, name):
        self.h, self.v, self.name = h, 0, name


class Prog:
    def __init__(self):
        self.q = {e: [] for e in ENGINES}

    def op(self, eng, fn, waits=(), inc=None, incv=1):
        best = {}
        for s, v in waits:
            if v > 0 and v > best.get(id(s), (None, 0))[1]:
                best[id(s)] = (s, v)
        ws = tuple(best.values())
        self.q[eng].append((fn, ws, inc, incv))
        if inc is not None:
            inc.v += incv
            return inc.v
        return None


def build_nc():
    nc = bass.Bass("TRN2", target_bir_lowering=False)

    x0 = nc.dram_tensor("x", [NTOK, D], F32, kind="ExternalInput").ap()
    w1t = nc.dram_tensor("w1t", [D, DH], F32, kind="ExternalInput").ap()
    w2t = nc.dram_tensor("w2t", [DH, D], F32, kind="ExternalInput").ap()
    cwd = nc.dram_tensor("cwdrm", [128, 128], F32, kind="ExternalInput").ap()
    wqd = nc.dram_tensor("wq", [D, D], F32, kind="ExternalInput").ap()
    wkd = nc.dram_tensor("wk", [D, D], F32, kind="ExternalInput").ap()
    wvd = nc.dram_tensor("wv", [D, D], F32, kind="ExternalInput").ap()
    wod = nc.dram_tensor("wo", [D, D], F32, kind="ExternalInput").ap()
    bqd = nc.dram_tensor("bq", [1, D], F32, kind="ExternalInput").ap()
    bvd = nc.dram_tensor("bv", [1, D], F32, kind="ExternalInput").ap()
    bod = nc.dram_tensor("bo", [1, D], F32, kind="ExternalInput").ap()
    cosd = nc.dram_tensor("cosd", [128, 1024], F32, kind="ExternalInput").ap()
    sind = nc.dram_tensor("sind", [128, 1024], F32, kind="ExternalInput").ap()
    
    identf_d = nc.dram_tensor("identfd", [128, 128], F32, kind="ExternalInput").ap()
    
    onesf_d = nc.dram_tensor("onesfd", [128, 128], F32, kind="ExternalInput").ap()
    e16_d = nc.dram_tensor("e16d", [1, 256], F32, kind="ExternalInput").ap()
    sel_d = nc.dram_tensor("seld", [16, 1024], F32, kind="ExternalInput").ap()
    consts_d = nc.dram_tensor("constsd", [128, 8], F32, kind="ExternalInput").ap()
    out_d = nc.dram_tensor("out", [NTOK, D], F32, kind="ExternalOutput").ap()
    dbg1 = dbg2 = dbg3 = dbg4 = None
    if os.environ.get("ATH_DEBUG"):
        dbg1 = nc.dram_tensor("dbg1", [128, 160], F32, kind="ExternalOutput").ap()
        dbg2 = nc.dram_tensor("dbg2", [128, 8256], F32, kind="ExternalOutput").ap()
        dbg3 = nc.dram_tensor("dbg3", [128, 6192], BF16, kind="ExternalOutput").ap()
        dbg4 = nc.dram_tensor("dbg4", [128, 360], F32, kind="ExternalOutput").ap()
    x1d = nc.dram_tensor("x1buf", [NTOK, D], F32, kind="Internal").ap()
    x2d = nc.dram_tensor("x2buf", [NTOK, D], F32, kind="Internal").ap()
    w1qs = nc.dram_tensor("w1qs", [D, DH], BF16, kind="Internal").ap()
    w2qs = nc.dram_tensor("w2qs", [DH, D], BF16, kind="Internal").ap()

    stack = ExitStack()
    sb = lambda n, s, d: stack.enter_context(nc.sbuf_tensor(n, s, d))

    AR = sb("arena", [128, 45120], F32)
    arb = AR[:].bitcast(BF16)

    def arf(off, n):
        return AR[:, off:off + n]
    def arh(off_f32, n_bf16):
        return arb[:, 2 * off_f32:2 * off_f32 + n_bf16]

    w1q = arh(0, 32768)
    w2q = arh(16384, 32768)
    hbuf = arf(32768, 8256)
    hbuf_i = hbuf.bitcast(I32)
    xio = [arf(41024, 2048), arf(43072, 2048)]

    stg = sb("stg", [128, 1024], F32)
    stg_iv = stg[:].bitcast(I32)
    blk = sb("blk", [128, 256], BF16)
    blkint = sb("blkint", [128, 256], I32)
    sqint = sb("sqint", [128, 256], I32)
    yqT = sb("yqT", [128, 6192], BF16)
    r1row = sb("r1row", [1, 774], F32)
    cvb_t = sb("cvb", [128, 512], F32)
    sqbf = cvb_t[:].bitcast(BF16)
    r1bc = sb("r1bc", [128, 258], F32)
    r2bc = r1bc[:, 0:256]
    r2row = sb("r2row", [1, 256], F32)
    accd = sb("accd", [128, 256], F32)
    accp = sb("accp", [128, 256], F32)
    ssum = sb("ssum", [128, 32], F32)
    ssq = sb("ssq", [128, 32], F32)
    statm = sb("statm", [128, 32], F32)
    guard1 = sb("guard1", [128, 64], F32)
    statr = sb("statr", [128, 32], F32)
    guard2 = sb("guard2", [128, 64], F32)
    tmp = sb("tmp", [128, 16], F32)
    wsc = sb("wsc", [128, 4], F32)
    sc11 = sb("sc11", [1, 4], F32)
    consts = sb("consts", [128, 8], F32)
    cw = sb("cwsb", [128, 128], F32)
    identbf = sb("identbf", [128, 128], BF16)
    identf = sb("identf", [128, 128], F32)
    onesf = sb("onesf", [128, 128], F32)

    CL_CLIP, CL_R127, CL_NSH, CL_EPS, CL_HALF = 0, 1, 2, 3, 4

    P = stack.enter_context(nc.psum_tensor("P", [128, 4096], F32))
    ph = [P[:, 0:258], P[:, 512:770]]
    po = [P[:, 1024:2048], P[:, 2048:3072]]
    ptrb = P[:, 3072:3584].bitcast(BF16)
    pbcf = P[:, 3584:4096]

    def mksem(name):
        return Sem(stack.enter_context(nc.semaphore(name)), name)
    dx2 = [mksem("dxa"), mksem("dxb")]
    dx3 = [mksem("dxc"), mksem("dxd")]
    dsq = mksem("dsq")
    dg3 = [mksem("dge"), mksem("dgf")]
    dw = mksem("dw")
    dw2 = [mksem("dwa"), mksem("dwb")]
    dg2 = [mksem("dga"), mksem("dgb")]
    dsp = mksem("dsp")
    dgd = mksem("dgd")
    sv = mksem("sv")
    sa = mksem("sa")
    sp = mksem("sp")
    st_ = mksem("st")

    pr = Prog()

    def _chain(waits, sem):
        w = tuple(waits)
        if any(s_ is sem and v_ >= sem.v for s_, v_ in w):
            return w
        return w + ((sem, sem.v),)

    def V(fn, waits=(), ms=False):
        return pr.op("vector", fn, _chain(waits, sv), sv)
    def A_(fn, waits=(), ms=False):
        return pr.op("scalar", fn, _chain(waits, sa), sa)
    def T(fn, waits=(), ms=False):
        return pr.op("tensor", fn, waits, st_ if ms else None)
    def G(fn, waits=(), ms=False):
        return pr.op("gpsimd", fn, _chain(waits, sp), sp)
    def LD(fn, waits=(), sem=None):
        sem = sem or dw
        return pr.op("sync", fn, waits, sem, incv=16)
    def GD(fn, waits=(), sem=None):
        return pr.op("gpsimd", fn, waits, sem or dgd, incv=16)

    LD(lambda e: e.dma_start(out=consts[:], in_=consts_d))
    LD(lambda e: e.dma_start(out=cw[:], in_=cwd))
    idl = LD(lambda e: e.dma_start(out=identf[:], in_=identf_d))
    cl = LD(lambda e: e.dma_start(out=onesf[:], in_=onesf_d))
    V(lambda e: e.tensor_copy(out=identbf[:], in_=identf[:]), waits=[(dw, cl)])
    V(lambda e: e.memset(guard1[:], 0.0))
    V(lambda e: e.memset(guard2[:], 0.0))

    x0v = x0.rearrange("(s p) d -> p s d", p=128)
    stat_v = {}
    stat_a = {}
    ld_ms = {}
    for s_ in range(32):
        b = s_ % 2
        w_ = []
        if s_ >= 2:
            w_ = [(sv, stat_v[s_ - 2]), (sa, stat_a[s_ - 2])]
        ld_ms[s_] = LD(lambda e, s_=s_, b=b: e.dma_start(out=xio[b][:, 0:1024], in_=x0v[:, s_, :]),
                       waits=w_, sem=dx2[b])
        stat_v[s_] = V(lambda e, s_=s_, b=b: e.reduce_sum(out=ssum[:, s_:s_ + 1], in_=xio[b][:, 0:1024], axis=AX.X),
                       waits=[(dx2[b], ld_ms[s_])], ms=True)
        wsq = [(dx2[b], ld_ms[s_])]
        if s_ >= 1:
            wsq.append((sa, stat_a[s_ - 1]))
        stat_a[s_] = A_(lambda e, s_=s_, b=b: e.activation(out=stg[:, 0:1024], in_=xio[b][:, 0:1024], func=AF.Square, accum_out=ssq[:, s_:s_ + 1]),
                        waits=wsq, ms=True)
    stats_v, stats_a = stat_v[31], stat_a[31]

    wload = [hbuf[:, 0:4096], hbuf[:, 4096:8192]]
    wload_i = [hbuf_i[:, 0:4096], hbuf_i[:, 4096:8192]]
    w1tv = w1t.rearrange("(c p) h -> c p h", p=128)
    w2tv = w2t.rearrange("(c p) h -> c p h", p=128)

    red_ms = {}
    for j in range(8):
        b = j % 2
        w_ = [] if j < 2 else [(sv, red_ms[j - 2])]
        ld = LD(lambda e, j=j, b=b: e.dma_start(out=wload[b], in_=w1tv[j, :, :]), waits=w_, sem=dw2[b])
        red_ms[j] = V(lambda e, j=j, b=b: e.reduce_sum(out=tmp[:, j:j + 1], in_=wload[b], axis=AX.X, apply_absolute_value=True),
                      waits=[(dw2[b], ld)], ms=True)
    V(lambda e: e.reduce_sum(out=tmp[:, 8:9], in_=tmp[:, 0:8], axis=AX.X), waits=[(sv, red_ms[7])])
    for j in range(32):
        b = j % 2
        w_ = [(sv, red_ms[j - 2])] if j >= 2 else [(sv, red_ms[7] if b else red_ms[6])]
        ld = LD(lambda e, j=j, b=b: e.dma_start(out=wload[b][:, 0:1024], in_=w2tv[j, :, :]), waits=w_, sem=dw2[b])
        acc = 12 + b
        if j < 2:
            red_ms[j] = V(lambda e, b=b, acc=acc: e.reduce_sum(out=tmp[:, acc:acc + 1], in_=wload[b][:, 0:1024], axis=AX.X, apply_absolute_value=True),
                          waits=[(dw2[b], ld)], ms=True)
        else:
            rr = V(lambda e, b=b: e.reduce_sum(out=tmp[:, 10 + b:11 + b], in_=wload[b][:, 0:1024], axis=AX.X, apply_absolute_value=True),
                   waits=[(dw2[b], ld)], ms=True)
            red_ms[j] = V(lambda e, b=b, acc=acc: e.tensor_tensor(out=tmp[:, acc:acc + 1], in0=tmp[:, acc:acc + 1], in1=tmp[:, 10 + b:11 + b], op=ALU.add),
                          waits=[(sv, rr)], ms=True)
    vsum2 = V(lambda e: e.tensor_tensor(out=tmp[:, 9:10], in0=tmp[:, 12:13], in1=tmp[:, 13:14], op=ALU.add),
              waits=[(sv, red_ms[31]), (sv, red_ms[30])], ms=True)

    tps = T(lambda e: e.matmul(out=pbcf[0:1, 0:1], lhsT=tmp[:, 8:9], rhs=onesf[:, 0:1], start=True, stop=True, skip_group_check=True),
            waits=[(sv, vsum2), (dw, cl)])
    tps = T(lambda e: e.matmul(out=pbcf[0:1, 1:2], lhsT=tmp[:, 9:10], rhs=onesf[:, 0:1], start=True, stop=True, skip_group_check=True), ms=True)
    vm = V(lambda e: e.tensor_scalar(out=sc11[0:1, 0:2], in0=pbcf[0:1, 0:2], scalar1=1.0 / (DH * D), scalar2=1e-5, op0=ALU.mult, op1=ALU.max),
           waits=[(st_, tps)], ms=True)
    tbc = T(lambda e: e.matmul(out=pbcf[:, 2:4], lhsT=onesf[0:1, :], rhs=sc11[0:1, 0:2], start=True, stop=True, skip_group_check=True),
            waits=[(sv, vm)], ms=True)
    V(lambda e: e.tensor_copy(out=wsc[:, 0:1], in_=pbcf[:, 2:3]), waits=[(st_, tbc)])
    vx = V(lambda e: e.tensor_copy(out=wsc[:, 2:3], in_=pbcf[:, 3:4]), ms=True)
    vx = V(lambda e: e.reciprocal(out=wsc[:, 1:2], in_=wsc[:, 0:1]), waits=[(sv, vx)], ms=True)
    wsc_rdy = V(lambda e: e.reciprocal(out=wsc[:, 3:4], in_=wsc[:, 2:3]), waits=[(sv, vx)], ms=True)
    wsc_rdy = V(lambda e: e.tensor_copy(out=tmp[:, 15:16], in_=wsc[:, 3:4]), waits=[(sv, wsc_rdy)], ms=True)
    pbc_free = wsc_rdy

    w1qs_v = w1qs.rearrange("(c p) h -> c p h", p=128)
    w2qs_v = w2qs.rearrange("(c p) h -> c p h", p=128)
    q_ms = {}
    for j in range(8):
        b = j % 2
        w_ = [(sv, wsc_rdy)] if j < 2 else [(sv, q_ms[j - 2])]
        ld = LD(lambda e, j=j, b=b: e.dma_start(out=wload[b], in_=w1tv[j, :, :]), waits=w_, sem=dw2[b])
        V(lambda e, j=j, b=b: e.tensor_scalar(out=wload[b], in0=wload[b], scalar1=wsc[:, 1:2], scalar2=None, op0=ALU.mult),
          waits=[(dw2[b], ld)])
        V(lambda e, j=j, b=b: e.tensor_scalar(out=wload[b], in0=wload[b], scalar1=RMAGIC, scalar2=RMAGIC, op0=ALU.add, op1=ALU.subtract))
        q_ms[j] = V(lambda e, j=j, b=b: e.tensor_scalar(out=w1q[:, j * 4096:(j + 1) * 4096], in0=wload[b], scalar1=-1.0, scalar2=1.0, op0=ALU.max, op1=ALU.min), ms=True)
        GD(lambda e, j=j: e.dma_start(out=w1qs_v[j, :, :], in_=w1q[:, j * 4096:(j + 1) * 4096]),
           waits=[(sv, q_ms[j])], sem=dsp)
    for j in range(32):
        b = j % 2
        w_ = [(sv, q_ms[6 + b])] if j < 2 else [(sv, q_ms[8 + j - 2])]
        ld = LD(lambda e, j=j, b=b: e.dma_start(out=wload[b][:, 0:1024], in_=w2tv[j, :, :]), waits=w_, sem=dw2[b])
        V(lambda e, j=j, b=b: e.tensor_scalar(out=wload[b][:, 0:1024], in0=wload[b][:, 0:1024], scalar1=wsc[:, 3:4], scalar2=None, op0=ALU.mult),
          waits=[(dw2[b], ld)])
        V(lambda e, j=j, b=b: e.tensor_scalar(out=wload[b][:, 0:1024], in0=wload[b][:, 0:1024], scalar1=RMAGIC, scalar2=RMAGIC, op0=ALU.add, op1=ALU.subtract))
        q_ms[8 + j] = V(lambda e, j=j, b=b: e.tensor_scalar(out=w2q[:, j * 1024:(j + 1) * 1024], in0=wload[b][:, 0:1024], scalar1=-1.0, scalar2=1.0, op0=ALU.max, op1=ALU.min), ms=True)
        GD(lambda e, j=j: e.dma_start(out=w2qs_v[j, :, :], in_=w2q[:, j * 1024:(j + 1) * 1024]),
           waits=[(sv, q_ms[8 + j])], sem=dsp)
    wq_done_v = q_ms[8 + 31]
    wspill = dsp.v

    def finalize_stats(wv_, wa_, nsteps=9):
        a = wa_
        if nsteps >= 1 and os.environ.get("ATH_PRE"):
            a = A_(lambda e: e.preload_act_table(AF.Exp), waits=[(sa, wa_)], ms=True)
        v = wv_
        if os.environ.get("ATH_FSKIPV"):
            nsteps_v = 0
        else:
            nsteps_v = nsteps
        if nsteps_v >= 2:
            v = V(lambda e: e.tensor_scalar(out=statm[:], in0=ssum[:], scalar1=1.0 / D, scalar2=None, op0=ALU.mult),
                  waits=[(sv, wv_), (sa, wa_)], ms=True)
        if nsteps_v >= 3:
            v = V(lambda e: e.tensor_tensor(out=statr[:], in0=statm[:], in1=statm[:], op=ALU.mult), waits=[(sv, v)], ms=True)
        if nsteps_v >= 4:
            v = V(lambda e: e.tensor_scalar(out=ssq[:], in0=ssq[:], scalar1=1.0 / D, scalar2=None, op0=ALU.mult), waits=[(sv, v)], ms=True)
        if nsteps_v >= 5:
            v = V(lambda e: e.tensor_tensor(out=statr[:], in0=ssq[:], in1=statr[:], op=ALU.subtract), waits=[(sv, v)], ms=True)
        if nsteps >= 6:
            lnf = {"sq": AF.Square, "ln": AF.Ln}[os.environ.get("ATH_LNF", "ln")]
            lnb = 0.0 if os.environ.get("ATH_LNB") else consts[:, CL_EPS:CL_EPS + 1]
            lno = tmp[:, 0:1] if os.environ.get("ATH_LNO") else statr[:]
            lni = statr[:, 0:1] if os.environ.get("ATH_LNO") else statr[:]
            a = A_(lambda e: e.activation(out=lno, in_=lni, func=lnf, bias=lnb, scale=1.0),
               waits=[(sv, v)], ms=True)
        if nsteps >= 7:
            a = A_(lambda e: e.activation(out=statr[:], in_=statr[:], func=AF.Exp, bias=0.0, scale=-0.5), waits=[(sa, a)], ms=True)
        return sv.v, a

    fin_v, fin_a = finalize_stats(max(stats_v, wsc_rdy), stats_a)
    if dbg1 is not None and DEV_PHASES == "ffn1":
        GD(lambda e: e.dma_start(out=dbg1[:, 92:124], in_=statr[:]), waits=[(sa, fin_a)])

    def emit_ffn(src, dst, collect_stats, phase_wait_v, phase_wait_a, extra_first=(), pbc0=None, mm0_wait=()):
        srcv = src.rearrange("(t g p) d -> t p g d", p=128, g=2)
        dstv = dst.rearrange("(t g p) d -> t p g d", p=128, g=2)

        if os.environ.get("ATH_PRE"):
            A_(lambda e: e.preload_act_table(AF.Silu), waits=[(sa, phase_wait_a)])

        x_ld = {}
        x_ld2 = {}
        a_v = {}
        a_a = {}
        a_t = {}
        b_store = {}
        b_stats_a = {}
        b_mm1_done = {}
        evac_ms = {}
        po_free = None
        nonloc = dict(pbc=pbc0 if pbc0 is not None else pbc_free, first_mm=True)

        def slot_view(slot):
            return yqT[:, slot * 2064:(slot + 1) * 2064].rearrange("p (c n) -> p c n", n=258)

        def a_part(i):
            slot = i % 3
            b = i % 2
            sfirst = (i % TPS) == 0
            slast = (i % TPS) == TPS - 1
            w0 = [(dx2[i % 2], x_ld[i])]
            if i >= 1:
                w0.append((st_, a_t[i - 1]))
            tg = {}
            cp_ms = {}
            qs_rdy_g = {}
            for g in range(2):
                stc = 2 * i + g
                wv_ = w0 if g == 0 else [(dx3[i % 2], x_ld2[i]), (sp, cp_ms[(0, 7)])]
                V(lambda e, b=b, g=g, stc=stc: e.tensor_scalar(
                    out=stg[:], in0=xio[b][:, g * 1024:(g + 1) * 1024],
                    scalar1=statm[:, stc:stc + 1], scalar2=None, op0=ALU.subtract, op1=ALU.max,
                    accum_out=tmp[:, g:g + 1]),
                  waits=wv_)
                V(lambda e, b=b, g=g, stc=stc: e.tensor_scalar(
                    out=stg[:], in0=xio[b][:, g * 1024:(g + 1) * 1024],
                    scalar1=statm[:, stc:stc + 1], scalar2=None, op0=ALU.subtract, op1=ALU.min,
                    accum_out=tmp[:, 2 + g:3 + g]))
                V(lambda e, g=g: e.tensor_scalar(out=tmp[:, 2 + g:3 + g], in0=tmp[:, 2 + g:3 + g], scalar1=-1.0, scalar2=None, op0=ALU.mult))
                V(lambda e, g=g: e.tensor_tensor(out=tmp[:, g:g + 1], in0=tmp[:, g:g + 1], in1=tmp[:, 2 + g:3 + g], op=ALU.max))
                V(lambda e, g=g: e.tensor_scalar(out=tmp[:, 2 + g:3 + g], in0=tmp[:, g:g + 1], scalar1=1e-5, scalar2=None, op0=ALU.max))
                vv = V(lambda e, g=g: e.reciprocal(out=tmp[:, 4 + g:5 + g], in_=tmp[:, 2 + g:3 + g]), ms=True)
                vv = V(lambda e, g=g: e.tensor_scalar(out=tmp[:, 4 + g:5 + g], in0=tmp[:, 4 + g:5 + g], scalar1=127.0, scalar2=None, op0=ALU.mult), waits=[(sv, vv)], ms=True)
                V(lambda e, g=g, i=i, stc=stc: e.tensor_tensor(out=tmp[:, 6 + g:7 + g], in0=statr[:, stc:stc + 1], in1=tmp[:, g:g + 1], op=ALU.mult))
                V(lambda e, g=g: e.tensor_scalar(out=tmp[:, 6 + g:7 + g], in0=tmp[:, 6 + g:7 + g], scalar1=1e-5, scalar2=None, op0=ALU.max))
                qs_rdy_g[g] = V(lambda e, g=g: e.tensor_scalar(out=tmp[:, 6 + g:7 + g], in0=tmp[:, 6 + g:7 + g], scalar1=wsc[:, 0:1], scalar2=1.0 / 127.0, op0=ALU.mult, op1=ALU.mult), ms=True)
                w_q = [(sv, qs_rdy_g[g])]
                if g == 1:
                    w_q.append((sp, cp_ms[(0, 7)]))
                qv = V(lambda e, g=g, b=b, stc=stc: e.tensor_scalar(
                    out=stg[:], in0=xio[b][:, g * 1024:(g + 1) * 1024],
                    scalar1=statm[:, stc:stc + 1], scalar2=tmp[:, 4 + g:5 + g],
                    op0=ALU.subtract, op1=ALU.mult),
                    waits=w_q, ms=True)
                qv = V(lambda e: e.tensor_scalar(out=stg[:], in0=stg[:], scalar1=RMAGIC, scalar2=RMAGIC, op0=ALU.add, op1=ALU.subtract), ms=True)
                prev_ev = a_a.get((i, g - 1)) or a_a.get((i - 1, 1))
                for j in range(8):
                    w_c = [(sv, qv)]
                    if (g, j) not in ((0, 0), (0, 1)):
                        pj = (g, j - 2) if j >= 2 else (g - 1, 6 + j)
                        w_c.append((st_, tg.get(pj, 0)))
                    G(lambda e, j=j: e.tensor_copy(out=blkint[:, 128 * (j % 2):128 * (j % 2) + 128], in_=stg[:, 128 * j:128 * (j + 1)]),
                      waits=w_c, ms=True)
                    cp_ms[(g, j)] = G(lambda e, j=j: e.tensor_copy(out=blk[:, 128 * (j % 2):128 * (j % 2) + 128], in_=blkint[:, 128 * (j % 2):128 * (j % 2) + 128]),
                                      ms=True)
                    w_t = [(sp, cp_ms[(g, j)])]
                    if j == 0 and prev_ev:
                        w_t.append((sa, prev_ev))
                    tg[(g, j)] = T(lambda e, g=g, j=j: e.transpose(
                        out=ptrb[:, 128 * j:128 * (j + 1)],
                        in_=blk[:, 128 * (j % 2):128 * (j % 2) + 128],
                        identity=identbf[:]),
                        waits=w_t, ms=True)
                dest = slot_view(slot)[:, :, 1 + 128 * g:1 + 128 * g + 128]
                a_a[(i, g)] = A_(lambda e, dest=dest: e.copy(out=dest, in_=ptrb[:].rearrange("p (c n) -> p c n", n=128)),
                                 waits=[(st_, tg[(g, 7)])], ms=True)
            qs_rdy = qs_rdy_g[1]
            w_ = [(sv, qs_rdy), (sv, nonloc["pbc"])]
            T(lambda e: e.transpose(out=pbcf[0:1, 0:128], in_=tmp[:, 6:7], identity=identf[:]), waits=w_)
            tr2 = T(lambda e: e.transpose(out=pbcf[0:1, 128:256], in_=tmp[:, 7:8], identity=identf[:]), ms=True)
            ev = V(lambda e, slot=slot: e.tensor_copy(out=r1row[0:1, slot * 258 + 1:slot * 258 + 257], in_=pbcf[0:1, 0:256]),
                   waits=[(st_, tr2)], ms=True)
            nonloc["pbc"] = ev
            hv = [(sa, a_a[(i, 1)]), (st_, b_mm1_done.get(i - 2, 0))]
            if sfirst:
                V(lambda e, slot=slot: e.memset(slot_view(slot)[:, :, 0:1], 0.0), waits=hv, ms=True)
                V(lambda e, slot=slot: e.memset(r1row[0:1, slot * 258:slot * 258 + 1], 1.0), ms=True)
            else:
                prev = (i - 1) % 3
                V(lambda e, slot=slot, prev=prev: e.tensor_copy(
                    out=slot_view(prev)[:, :, 257:258], in_=slot_view(slot)[:, :, 1:2]), waits=hv, ms=True)
                V(lambda e, slot=slot, prev=prev: e.tensor_copy(out=r1row[0:1, prev * 258 + 257:prev * 258 + 258], in_=r1row[0:1, slot * 258 + 1:slot * 258 + 2]), ms=True)
            if slast:
                V(lambda e, slot=slot: e.memset(slot_view(slot)[:, :, 257:258], 0.0), ms=True)
                V(lambda e, slot=slot: e.memset(r1row[0:1, slot * 258 + 257:slot * 258 + 258], 1.0), ms=True)
            else:
                nxt = (i + 1) % 3
                V(lambda e, slot=slot, nxt=nxt: e.tensor_copy(
                    out=slot_view(nxt)[:, :, 0:1], in_=slot_view(slot)[:, :, 256:257]), ms=True)
                V(lambda e, slot=slot, nxt=nxt: e.tensor_copy(out=r1row[0:1, nxt * 258:nxt * 258 + 1], in_=r1row[0:1, slot * 258 + 256:slot * 258 + 257]), ms=True)
            a_v[i] = sv.v
            a_t[i] = tg[(1, 7)]

        def b_part(i, halo_v):
            slot = i % 3
            b = i % 2
            w_ = [(sv, halo_v), (sv, nonloc["pbc"]), (sa, a_a[(i, 1)])]
            tb = T(lambda e, slot=slot: e.matmul(out=pbcf[:, 0:258], lhsT=onesf[0:1, :], rhs=r1row[0:1, slot * 258:slot * 258 + 258], start=True, stop=True, skip_group_check=True),
                   waits=w_, ms=True)
            r1bc_ms = V(lambda e: e.tensor_copy(out=r1bc[:], in_=pbcf[:, 0:258]), waits=[(st_, tb)], ms=True)
            nonloc["pbc"] = r1bc_ms

            silu_ms = {}
            tree_ms = {}
            mm1_ms = {}
            if CUT >= 1:
              for c in range(32):
                bank = c % 2
                w_mm = [(sv, evac_ms.get(bank, 0))]
                if c == 0:
                    w_mm.append((sv, r1bc_ms))
                    if b_stats_a.get(i - 1):
                        w_mm.append((sa, b_stats_a[i - 1]))
                    if nonloc.get("first_mm"):
                        w_mm += list(mm0_wait)
                        nonloc["first_mm"] = False
                for d_ in range(8):
                    mm1_ms[c] = T(lambda e, c=c, d_=d_, bank=bank, slot=slot: e.matmul(
                        out=ph[bank], lhsT=w1q[:, d_ * 4096 + c * 128:d_ * 4096 + (c + 1) * 128],
                        rhs=yqT[:, slot * 2064 + d_ * 258:slot * 2064 + (d_ + 1) * 258],
                        start=(d_ == 0), stop=(d_ == 7), skip_group_check=True),
                        waits=w_mm if d_ == 0 else (), ms=(d_ == 7))
                if c == 31:
                    b_mm1_done[i] = mm1_ms[c]
                hc = hbuf[:, c * 258:(c + 1) * 258]
                ev = V(lambda e, hc=hc, bank=bank: e.tensor_tensor(out=hc, in0=ph[bank], in1=r1bc[:], op=ALU.mult),
                       waits=[(st_, mm1_ms[c])], ms=True)
                evac_ms[bank] = ev
                if CUT >= 2:
                    cvb = cvb_t[:, 256 * (c % 2):256 * (c % 2) + 256]
                    w_t1 = [(sv, ev)]
                    if c >= 2:
                        w_t1.append((sv, tree_ms[c - 2]))
                    t1 = A_(lambda e, hc=hc, cvb=cvb, c=c: e.activation(
                        out=cvb, in_=hc[:, 1:257], func=AF.Identity,
                        bias=cw[:, 96 + (c % 32):97 + (c % 32)], scale=cw[:, 32 + (c % 32):33 + (c % 32)]),
                        waits=w_t1, ms=True)
                    t2 = V(lambda e, hc=hc, cvb=cvb, c=c: e.scalar_tensor_tensor(
                        out=cvb, in0=hc[:, 0:256], scalar=cw[:, (c % 32):(c % 32) + 1], in1=cvb, op0=ALU.mult, op1=ALU.add),
                        waits=[(sa, t1)], ms=True)
                    if not os.environ.get("ATH_POOLTAP"):
                        t3 = V(lambda e, hc=hc, cvb=cvb, c=c: e.scalar_tensor_tensor(
                            out=cvb, in0=hc[:, 2:258], scalar=cw[:, 64 + (c % 32):65 + (c % 32)], in1=cvb, op0=ALU.mult, op1=ALU.add),
                            waits=[(sv, t2)], ms=True)
                        A_(lambda e, hc=hc, cvb=cvb: e.activation(out=hc[:, 1:257], in_=cvb, func=AF.Silu),
                           waits=[(sv, t3)])
                    else:
                        G(lambda e, hc=hc, c=c: e.tensor_scalar(
                            out=accp[:], in0=hc[:, 2:258], scalar1=cw[:, 64 + (c % 32):65 + (c % 32)], scalar2=None, op0=ALU.mult),
                          waits=[(sv, ev)])
                        t3 = G(lambda e, cvb=cvb: e.tensor_tensor(out=cvb, in0=cvb, in1=accp[:], op=ALU.add),
                               waits=[(sv, t2)], ms=True)
                        A_(lambda e, hc=hc, cvb=cvb: e.activation(out=hc[:, 1:257], in_=cvb, func=AF.Silu),
                           waits=[(sp, t3)])
                    silu_ms[c] = A_(lambda e, hc=hc, cvb=cvb: e.activation(out=cvb, in_=hc[:, 1:257], func=AF.Abs), ms=True)
                    if c == 0:
                        tree_ms[c] = V(lambda e, cvb=cvb: e.tensor_copy(out=accd[:], in_=cvb),
                                       waits=[(sa, silu_ms[c])], ms=True)
                    else:
                        tree_ms[c] = V(lambda e, cvb=cvb: e.tensor_tensor(out=accd[:], in0=accd[:], in1=cvb, op=ALU.max),
                                       waits=[(sa, silu_ms[c])], ms=True)
            outsc_rdy = 0
            if CUT >= 3:
                comb = tree_ms[31]
                w_ = [(sv, comb), (sv, nonloc["pbc"])]
                T(lambda e: e.transpose(out=pbcf[:, 0:128], in_=accd[:, 0:128], identity=identf[:]), waits=w_)
                tt = T(lambda e: e.transpose(out=pbcf[:, 128:256], in_=accd[:, 128:256], identity=identf[:]), ms=True)
                rd1 = V(lambda e: e.reduce_max(out=tmp[:, 8:9], in_=pbcf[:, 0:128], axis=AX.X), waits=[(st_, tt)], ms=True)
                rd2 = V(lambda e: e.reduce_max(out=tmp[:, 9:10], in_=pbcf[:, 128:256], axis=AX.X), waits=[(sv, rd1)], ms=True)
                vv = V(lambda e: e.tensor_scalar(out=tmp[:, 10:12], in0=tmp[:, 8:10], scalar1=1e-5, scalar2=None, op0=ALU.max), waits=[(sv, rd2)], ms=True)
                vv = V(lambda e: e.reciprocal(out=tmp[:, 12:14], in_=tmp[:, 10:12]), waits=[(sv, vv)], ms=True)
                qs2_rdy = V(lambda e: e.tensor_scalar(out=tmp[:, 12:14], in0=tmp[:, 12:14], scalar1=127.0, scalar2=None, op0=ALU.mult), waits=[(sv, vv)], ms=True)
                outsc_rdy = V(lambda e: e.tensor_scalar(out=tmp[:, 14:16], in0=tmp[:, 10:12], scalar1=wsc[:, 2:3], scalar2=0.5 / 127.0, op0=ALU.mult, op1=ALU.mult), waits=[(sv, qs2_rdy)], ms=True)
                tq = T(lambda e: e.transpose(out=pbcf[0:1, 256:384], in_=tmp[:, 12:13], identity=identf[:]), waits=[(sv, qs2_rdy)])
                tq = T(lambda e: e.transpose(out=pbcf[0:1, 384:512], in_=tmp[:, 13:14], identity=identf[:]), ms=True)
                evq = V(lambda e: e.tensor_copy(out=r2row[0:1, 0:256], in_=pbcf[0:1, 256:512]), waits=[(st_, tq)], ms=True)
                tbc2 = T(lambda e: e.matmul(out=pbcf[:, 0:256], lhsT=onesf[0:1, :], rhs=r2row[0:1, 0:256], start=True, stop=True, skip_group_check=True),
                         waits=[(sv, evq)], ms=True)
                r2bc_ms = V(lambda e: e.tensor_copy(out=r2bc, in_=pbcf[:, 0:256]), waits=[(st_, tbc2)], ms=True)
                nonloc["pbc"] = r2bc_ms
            ev = r1bc_ms
            if CUT >= 4:
                accf = [accd[:], accp[:]]
                mm2_ms = {}
                qt_ms = {}
                i2_ms = {}
                for c in range(32):
                    sq = sqbf[:, 256 * (c % 2):256 * (c % 2) + 256]
                    acc = accf[c % 2]
                    w_ = [(sv, r2bc_ms)] if c < 2 else []
                    V(lambda e, c=c, acc=acc: e.tensor_tensor(
                        out=acc, in0=hbuf[:, c * 258 + 1:c * 258 + 257], in1=r2bc, op=ALU.mult),
                        waits=w_, ms=True)
                    w_ = [] if c < 1 else [(sp, i2_ms[c - 1])]
                    qt_ms[c] = V(lambda e, acc=acc: e.tensor_scalar(
                        out=sqint[:], in0=acc, scalar1=RMAGIC, scalar2=RMAGIC, op0=ALU.add, op1=ALU.subtract),
                        waits=w_, ms=True)
                    w_ = [(sv, qt_ms[c])]
                    if c >= 2:
                        w_.append((st_, mm2_ms[c - 2]))
                    i2_ms[c] = G(lambda e, c=c, sq=sq: e.tensor_copy(out=sq, in_=sqint[:]),
                                 waits=w_, ms=True)
                    for g in range(2):
                        for n2 in range(2):
                            w_m = [(sp, i2_ms[c])]
                            if c == 0 and g == 0 and n2 == 0 and po_free is not None:
                                w_m.append((sv, po_free))
                            mm2_ms[c] = T(lambda e, c=c, g=g, n2=n2, sq=sq: e.matmul(
                                out=po[g][:, n2 * 512:(n2 + 1) * 512],
                                lhsT=sq[:, g * 128:(g + 1) * 128],
                                rhs=w2q[:, c * 1024 + n2 * 512:c * 1024 + n2 * 512 + 512],
                                start=(c == 0), stop=(c == 31), skip_group_check=True),
                                waits=w_m if (g == 0 and n2 == 0) else (), ms=(g == 1 and n2 == 1))
                ev = None
            stat_aa = 0
            for g in range(2):
                if CUT >= 4:
                    for n2 in range(2):
                        ev = V(lambda e, g=g, n2=n2, b=b: e.scalar_tensor_tensor(
                            out=xio[b][:, g * 1024 + n2 * 512:g * 1024 + n2 * 512 + 512],
                            in0=po[g][:, n2 * 512:(n2 + 1) * 512],
                            scalar=tmp[:, 14 + g:15 + g],
                            in1=xio[b][:, g * 1024 + n2 * 512:g * 1024 + n2 * 512 + 512],
                            op0=ALU.mult, op1=ALU.add),
                            waits=[(st_, mm2_ms[31]), (sv, outsc_rdy)] if (g == 0 and n2 == 0) else (), ms=True)
                stat_vv = 0
                if collect_stats:
                    stc = 2 * i + g
                    stat_vv = V(lambda e, g=g, b=b, stc=stc: e.reduce_sum(out=ssum[:, stc:stc + 1], in_=xio[b][:, g * 1024:(g + 1) * 1024], axis=AX.X), ms=True)
                    stat_aa = A_(lambda e, g=g, b=b, stc=stc: e.activation(out=hbuf[:, 0:1024], in_=xio[b][:, g * 1024:(g + 1) * 1024], func=AF.Square, accum_out=ssq[:, stc:stc + 1]),
                                 waits=[(sv, stat_vv)], ms=True)
                b_store[(i, g)] = GD(lambda e, i=i, b=b, g=g: e.dma_start(
                    out=dstv[i, :, g, :], in_=xio[b][:, g * 1024:(g + 1) * 1024]),
                    waits=[(sv, stat_vv or ev), (sa, stat_aa)], sem=(dg2 if g == 0 else dg3)[b])
            po_free_local = ev
            if collect_stats:
                b_stats_a[i] = stat_aa
            return po_free_local

        for i in range(NT):
            if i < 2:
                w_ = [(sv, phase_wait_v), (sa, phase_wait_a)] + list(extra_first)
                w1_ = list(w_)
            else:
                w_ = [(dg2[i % 2], b_store[(i - 2, 0)])]
                w1_ = [(dg3[i % 2], b_store[(i - 2, 1)])]
            x_ld[i] = LD(lambda e, i=i: e.dma_start(out=xio[i % 2][:, 0:1024], in_=srcv[i, :, 0, :]),
                         waits=w_, sem=dx2[i % 2])
            x_ld2[i] = LD(lambda e, i=i: e.dma_start(out=xio[i % 2][:, 1024:2048], in_=srcv[i, :, 1, :]),
                          waits=w1_, sem=dx3[i % 2])
            a_part(i)
            if dbg4 is not None and 2 < i < 6 and os.environ.get("ATH_SNAP"):
                GD(lambda e, i=i: e.dma_start(out=dbg4[:, 260 + 32 * (i - 3):292 + 32 * (i - 3)], in_=statr[:]),
                   waits=[(sv, a_v[i])])
            if i >= 1:
                po_free = b_part(i - 1, a_v[i])
        po_free = b_part(NT - 1, a_v[NT - 1])
        emit_ffn.b_store = dict(b_store)
        emit_ffn.last_pbc = nonloc["pbc"]
        emit_ffn.store_tot = [(dg2[0], dg2[0].v), (dg2[1], dg2[1].v), (dg3[0], dg3[0].v), (dg3[1], dg3[1].v)]
        return sv.v, sa.v


    def emit_attn(src, dst, phase_wait_v, phase_wait_a, store_wait, pbc0):
        srcv = src.rearrange("(g p) d -> p g d", p=128)
        dstv = dst.rearrange("(g p) d -> p g d", p=128)

        wq_b = arh(0, 8192).rearrange("p (c n) -> p c n", n=1024)
        wk_b = arh(4096, 8192).rearrange("p (c n) -> p c n", n=1024)
        wv_b = arh(8192, 8192).rearrange("p (c n) -> p c n", n=1024)
        yT = arh(12288, 16384).rearrange("p (c n) -> p c n", n=2048)
        Kb = arh(20480, 16384).rearrange("p (c n) -> p c n", n=2048)
        Vb = arh(28672, 16640).rearrange("p (c n) -> p c n", n=1040)
        Vb_hk = arh(28672, 16640).rearrange("p (c h k) -> p c h k", h=16, k=65)
        xa = [arf(36992, 1024), arf(38016, 1024)]
        cosb = arh(39040, 1024)
        sinb = arh(39552, 1024)
        bqr = arh(40064, 1024)[0:1, :]
        bvr = arh(40576, 1024)[0:1, :]
        bor = arh(41088, 1024)[0:1, :]
        e16f = arf(41600, 256)[0:1, :]
        selb = arf(41856, 1024)[0:16, :]
        ones128 = arh(42880, 128)[0:1, :]
        ones512 = arh(42944, 512)[0:1, :]
        wo_b = arh(0, 8192).rearrange("p (c n) -> p c n", n=1024)
        Qf = arh(4096, 16384).rearrange("p (c n) -> p c n", n=2048)
        A_sb = arf(12288, 4096)
        Abf = arh(16384, 4096)
        eS = arh(18432, 2048)
        rd_sb = stg[0:16, 0:512]
        selbf = arh(43456, 1024)[0:16, :]
        e16bf = arh(44000, 256)[0:1, :]
        dnbf = arh(44200, 1024)[0:1, :]
        rdbf = arh(44800, 512)[0:16, :]
        stgbf = stg[:].bitcast(BF16)[:, 0:1024]
        cvbbf = cvb_t[:].bitcast(BF16)[:, 0:1024]

        psA = [P[:, 0:512], P[:, 512:1024]]
        psB = [P[:, 1024:1536], P[:, 1536:2048]]
        pv2 = P[:, 1024:2048]
        pav_e = P[:, 2048:2560]
        pav_o = P[:, 2560:3072]
        plv = P[0:16, 3072:3584]
        pR = P[:, 3584:4096]

        pr.op("gpsimd", lambda e: e.nop(),
              waits=[(sv, phase_wait_v), (sa, phase_wait_a), (st_, st_.v), (sp, sp.v),
                     (dsp, dsp.v)])
        wqv = wqd.rearrange("(c p) o -> c p o", p=128)
        wkv = wkd.rearrange("(c p) o -> c p o", p=128)
        wvv = wvd.rearrange("(c p) o -> c p o", p=128)

        def load_qkv(wkv_extra=(), wq_extra=()):
            first = list(wkv_extra)
            for wsrc, wdst in ((wkv, wk_b), (wvv, wv_b)):
                for c in range(8):
                    GD(lambda e, wsrc=wsrc, wdst=wdst, c=c: e.dma_start(
                        out=wdst[:, c, :], in_=wsrc[c, :, :]), waits=first, sem=dsp)
                    first = []
            wkv_done_ = dsp.v
            first = list(wq_extra)
            for c in range(8):
                GD(lambda e, c=c: e.dma_start(
                    out=wq_b[:, c, :], in_=wqv[c, :, :]), waits=first, sem=dsq)
                first = []
            return wkv_done_, dsq.v

        wkv_done, wq_done = load_qkv()
        GD(lambda e: e.dma_start(out=cosb, in_=cosd), sem=dsp)
        GD(lambda e: e.dma_start(out=sinb, in_=sind), sem=dsp)
        wl_done = dsp.v
        wkv_done = wl_done
        bl = LD(lambda e: e.dma_start(out=stg[0:1, 0:1024], in_=bqd), waits=[(sv, phase_wait_v), (sa, phase_wait_a)] + list(store_wait))
        bc = V(lambda e: e.tensor_copy(out=bqr, in_=stg[0:1, 0:1024]), waits=[(dw, bl)], ms=True)
        bl = LD(lambda e: e.dma_start(out=stg[0:1, 0:1024], in_=bvd), waits=[(sv, bc)])
        bc = V(lambda e: e.tensor_copy(out=bvr, in_=stg[0:1, 0:1024]), waits=[(dw, bl)], ms=True)
        bl = LD(lambda e: e.dma_start(out=stg[0:1, 0:1024], in_=bod), waits=[(sv, bc)])
        bc = V(lambda e: e.tensor_copy(out=bor, in_=stg[0:1, 0:1024]), waits=[(dw, bl)], ms=True)
        bl = LD(lambda e: e.dma_start(out=e16f, in_=e16_d), waits=[(sv, bc)])
        bc = V(lambda e: e.tensor_copy(out=e16bf, in_=e16f), waits=[(dw, bl)], ms=True)
        bl = LD(lambda e: e.dma_start(out=selb, in_=sel_d), waits=[(sv, bc)])
        bc = V(lambda e: e.tensor_copy(out=selbf, in_=selb), waits=[(dw, bl)], ms=True)
        V(lambda e: e.memset(ones128, 1.0))
        ones_ms = V(lambda e: e.memset(ones512, 1.0), ms=True)

        pbc = pbc0
        pl_rd_a = 0
        xa_user = {0: ones_ms, 1: ones_ms}
        xa_ld = {0: 0, 1: 0}
        b_st2 = {0: 0, 1: 0}
        eS_rd = {}
        psA_rd = {0: 0, 1: 0}
        psB_rd = {0: 0, 1: 0}
        psB_rd_v = {0: 0, 1: 0}
        pk_free = {}
        stats_ms_a = 0
        kb_dump = {}

        aseq = int(os.environ.get("ATH_ASEQ", "2"))
        aqt = int(os.environ.get("ATH_AQT", "3"))
        for s_ in range(aseq):
            seq0_T_done = st_.v
            if s_ == 1:
                wkv_done, wq_done = load_qkv(
                    wkv_extra=[(st_, kb_dump.get("qf_dead", st_.v))],
                    wq_extra=[(st_, st_.v)])
                wl_done = dsp.v
            st_t = {}
            ev_a = {}
            for st in range(16):
                g = s_ * 16 + st
                par = st % 2
                w_ = [(sv, xa_user[par])]
                if s_ == 0 and st < 2:
                    w_ += list(store_wait)
                if b_st2[par]:
                    w_.append((dg2[par], b_st2[par]))
                xa_ld[par] = LD(lambda e, g=g, par=par: e.dma_start(out=xa[par], in_=srcv[:, g, :]),
                                waits=w_, sem=dx2[par])
                w_ = [(dx2[par], xa_ld[par])]
                if s_ == 0:
                    w_.append((dsp, wl_done))
                elif st == 0:
                    w_.append((sa, pl_rd_a))
                if st >= 1:
                    w_.append((st_, st_t[st - 1]))
                if s_ == 1 and st == 0 and 0 in kb_dump:
                    w_.append((dgd, kb_dump[0]))
                xa_user[par] = V(lambda e, par=par, g=g: e.tensor_scalar(
                    out=stgbf, in0=xa[par], scalar1=statm[:, g:g + 1], scalar2=statr[:, g:g + 1],
                    op0=ALU.subtract, op1=ALU.mult), waits=w_, ms=True)
                for j in range(8):
                    w_ = [(sv, xa_user[par])]
                    if ev_a.get(st - 1):
                        w_.append((sa, ev_a[st - 1]))
                    elif s_ == 1:
                        w_.append((sa, pl_rd_a))
                    st_t[st] = T(lambda e, j=j: e.transpose(
                        out=ptrb[:, 128 * j:128 * (j + 1)],
                        in_=stgbf[:, 128 * j:128 * (j + 1)],
                        identity=identbf[:]), waits=w_ if j == 0 else (), ms=(j == 7))
                dest = yT[:, :, 128 * st:128 * (st + 1)]
                w_ev = [(st_, st_t[st])]
                if s_ == 1 and st == 0:
                    w_ev.append((st_, seq0_T_done))
                ev_a[st] = A_(lambda e, dest=dest: e.copy(out=dest, in_=ptrb[:].rearrange("p (c n) -> p c n", n=128)),
                              waits=w_ev, ms=True)
            yT_a = ev_a[15]
            rope_guard = st_t[15]

            for oc in range(8):
                for tt in range(4):
                    bank = (oc * 4 + tt) % 2
                    w_ = [(sa, yT_a), (sv, pk_free.get(bank, 0)), (sa, psA_rd[bank])]
                    if oc == 0 and tt == 0:
                        w_.append((dsp, wkv_done))
                    for ic in range(8):
                        mmk = T(lambda e, oc=oc, tt=tt, ic=ic, bank=bank: e.matmul(
                            out=psA[bank], lhsT=wk_b[:, ic, 128 * oc:128 * (oc + 1)],
                            rhs=yT[:, ic, 512 * tt:512 * (tt + 1)],
                            start=(ic == 0), stop=(ic == 7), skip_group_check=True),
                            waits=w_ if ic == 0 else (), ms=(ic == 7))
                    pk_free[bank] = V(lambda e, oc=oc, tt=tt, bank=bank: e.tensor_copy(
                        out=Kb[:, oc, 512 * tt:512 * (tt + 1)], in_=psA[bank]),
                        waits=[(st_, mmk)], ms=True)

            for tc in range(16):
                w_ = [(sv, pk_free.get(2, 0)), (sv, psB_rd_v[0]), (sv, psB_rd_v[1]), (sa, psB_rd[0]), (sa, psB_rd[1])]
                for ic in range(8):
                    for n2 in range(2):
                        T(lambda e, tc=tc, ic=ic, n2=n2: e.matmul(
                            out=pv2[:, 512 * n2:512 * (n2 + 1)],
                            lhsT=yT[:, ic, 128 * tc:128 * (tc + 1)],
                            rhs=wv_b[:, ic, 512 * n2:512 * (n2 + 1)],
                            start=(ic == 0), stop=False, skip_group_check=True),
                            waits=w_ if (ic == 0 and n2 == 0) else ())
                for n2 in range(2):
                    mmv = T(lambda e, n2=n2: e.matmul(
                        out=pv2[:, 512 * n2:512 * (n2 + 1)], lhsT=ones128,
                        rhs=bvr[0:1, 512 * n2:512 * (n2 + 1)],
                        start=False, stop=True, skip_group_check=True), ms=(n2 == 1))
                pk_free[2] = V(lambda e, tc=tc: e.tensor_copy(
                    out=Vb_hk[:, tc, :, 0:64], in_=pv2[:].rearrange("p (h k) -> p h k", k=64)),
                    waits=[(st_, mmv)], ms=True)
            vone_ms = V(lambda e: e.memset(Vb_hk[:, :, :, 64:65], 1.0), ms=True)
            psB_rd = {0: 0, 1: 0}
            psB_rd_v = {0: 0, 1: 0}
            qf_guard = 0
            for oc in range(8):
                for tt in range(4):
                    bank = (oc * 4 + tt) % 2
                    w_ = [(sv, pk_free.get(bank, 0)), (sa, psA_rd[bank])]
                    if oc == 0 and tt == 0:
                        w_.append((dsq, wq_done))
                    if oc < 2 and tt == 0:
                        w_.append((sv, vone_ms))
                    for ic in range(8):
                        T(lambda e, oc=oc, tt=tt, ic=ic, bank=bank: e.matmul(
                            out=psA[bank], lhsT=wq_b[:, ic, 128 * oc:128 * (oc + 1)],
                            rhs=yT[:, ic, 512 * tt:512 * (tt + 1)],
                            start=(ic == 0), stop=False, skip_group_check=True),
                            waits=w_ if ic == 0 else ())
                    mmq = T(lambda e, oc=oc, tt=tt, bank=bank: e.matmul(
                        out=psA[bank], lhsT=bqr[0:1, 128 * oc:128 * (oc + 1)], rhs=ones512,
                        start=False, stop=True, skip_group_check=True), ms=True)
                    qf_guard = mmq
                    pk_free[bank] = V(lambda e, oc=oc, tt=tt, bank=bank: e.tensor_copy(
                        out=Qf[:, oc, 512 * tt:512 * (tt + 1)], in_=psA[bank]),
                        waits=[(st_, mmq)], ms=True)
            pr.op("gpsimd", lambda e: e.nop(), waits=[(st_, qf_guard)])
            wov = wod.rearrange("(c p) o -> c p o", p=128)
            for c in range(8):
                GD(lambda e, c=c: e.dma_start(out=wo_b[:, c, :], in_=wov[c, :, :]), sem=dsp)
            wo_done = dsp.v

            def rope(buf_v, oc, col0, width, coff, extra=()):
                sl = lambda a, b: buf_v[a:b, oc, col0:col0 + width]
                V(lambda e: e.tensor_scalar(out=stgbf[0:32, 0:width], in0=sl(32, 64), scalar1=-1.0, scalar2=None, op0=ALU.mult),
                  waits=list(extra))
                V(lambda e: e.tensor_copy(out=stgbf[32:64, 0:width], in_=sl(0, 32)))
                V(lambda e: e.tensor_scalar(out=stgbf[64:96, 0:width], in0=sl(96, 128), scalar1=-1.0, scalar2=None, op0=ALU.mult))
                V(lambda e: e.tensor_copy(out=stgbf[96:128, 0:width], in_=sl(64, 96)))
                V(lambda e: e.tensor_tensor(out=cvbbf[:, 0:width], in0=sl(0, 128), in1=cosb[:, coff:coff + width], op=ALU.mult))
                V(lambda e: e.tensor_tensor(out=stgbf[:, 0:width], in0=stgbf[:, 0:width], in1=sinb[:, coff:coff + width], op=ALU.mult))
                return V(lambda e: e.tensor_tensor(out=sl(0, 128), in0=cvbbf[:, 0:width], in1=stgbf[:, 0:width], op=ALU.add), ms=True)

            rope_ms = 0
            for oc in range(8):
                rope_ms = rope(Kb, oc, 1024, 1024, 0,
                               extra=[(st_, rope_guard)] if oc == 0 else ())
            kbtrace = dbg2 is not None and s_ == 0 and os.environ.get("ATH_KBTRACE")
            if dbg2 is not None and s_ == 0 and not kbtrace:
                kb_dump[0] = GD(lambda e: e.dma_start(out=dbg2[:, 0:8192], in_=arf(20480, 8192)),
                                waits=[(sv, rope_ms)])
            for oc in range(8):
                rope_ms = rope(Qf, oc, 1024, 1024, 0)

            def kbdump(slot, waits):
                kb_dump[0] = GD(lambda e, slot=slot: e.dma_start(
                    out=dbg2[:, 1024 * slot:1024 * (slot + 1)], in_=arf(20480, 1024)), waits=waits)
            if kbtrace:
                kbdump(0, [(sv, rope_ms)])

            A_ev = 0
            dn_ev = 0
            abf_ms = 0
            oproj_t = 0
            for qt in range(aqt + 1):
                q0 = 512 * qt
                qev = rope_ms
                for p in range(8):
                    for cp in range(8):
                        mA = mB = None
                        for cc in range(2):
                            c = 2 * cp + cc
                            w_ = [(sa, psA_rd[0])] if cc == 0 else []
                            if p == 0 and c < 2:
                                w_.append((sv, qev))
                            mA = T(lambda e, p=p, c=c, cc=cc, q0=q0: e.matmul(
                                out=psA[cc], lhsT=Kb[0:64, p, 128 * c:128 * (c + 1)],
                                rhs=Qf[0:64, p, q0:q0 + 512], start=True, stop=True,
                                tile_position=(0, 0), skip_group_check=True),
                                waits=w_, ms=(cc == 1))
                            w_ = [(sa, psB_rd[0])] if cc == 0 else []
                            if p == 0 and c < 2:
                                w_.append((sv, psB_rd_v[cc]))
                            mB = T(lambda e, p=p, c=c, cc=cc, q0=q0: e.matmul(
                                out=psB[cc], lhsT=Kb[64:128, p, 128 * c:128 * (c + 1)],
                                rhs=Qf[64:128, p, q0:q0 + 512], start=True, stop=True,
                                tile_position=(64, 0), skip_group_check=True),
                                waits=w_, ms=(cc == 1))
                        w_ = [(st_, mA)]
                        for h_ in (0, 1):
                            if eS_rd.get((0, h_)):
                                w_.append((st_, eS_rd[(0, h_)]))
                        eA = A_(lambda e: e.activation(
                            out=eS[:, 0:1024], in_=P[:, 0:1024], func=AF.Exp,
                            bias=consts[:, CL_NSH:CL_NSH + 1], scale=0.125),
                            waits=w_, ms=True)
                        psA_rd[0] = psA_rd[1] = eA
                        w_ = [(st_, mB)]
                        for h_ in (0, 1):
                            if eS_rd.get((1, h_)):
                                w_.append((st_, eS_rd[(1, h_)]))
                        eB = A_(lambda e: e.activation(
                            out=eS[:, 1024:2048], in_=P[:, 1024:2048], func=AF.Exp,
                            bias=consts[:, CL_NSH:CL_NSH + 1], scale=0.125),
                            waits=w_, ms=True)
                        psB_rd[0] = psB_rd[1] = eB
                        for cc in range(2):
                            c = 2 * cp + cc
                            w_ = [(sa, eA)] if cc == 0 else []
                            if c == 0:
                                w_.append((sv, A_ev))
                                w_.append((sa, stats_ms_a))
                            avA = T(lambda e, p=p, c=c, cc=cc: e.matmul(
                                out=pav_e[0:65, :], lhsT=Vb[:, c, 1040 * 0 + 65 * (2 * p):65 * (2 * p) + 65],
                                rhs=eS[:, 512 * cc:512 * (cc + 1)],
                                start=(c == 0), stop=(c == 15), skip_group_check=True),
                                waits=w_, ms=(c == 15))
                            w_ = [(sa, eB)] if cc == 0 else []
                            if c == 0:
                                w_.append((sv, A_ev))
                            avB = T(lambda e, p=p, c=c, cc=cc: e.matmul(
                                out=pav_o[0:65, :], lhsT=Vb[:, c, 65 * (2 * p + 1):65 * (2 * p + 1) + 65],
                                rhs=eS[:, 1024 + 512 * cc:1024 + 512 * (cc + 1)],
                                start=(c == 0), stop=(c == 15), skip_group_check=True),
                                waits=w_, ms=(c == 15 or cc == 1))
                            if cc == 1:
                                eS_rd[(0, 0)] = eS_rd[(0, 1)] = avB
                                eS_rd[(1, 0)] = eS_rd[(1, 1)] = avB
                    V(lambda e, p=p: e.tensor_copy(out=A_sb[0:64, 512 * p:512 * (p + 1)], in_=pav_e[0:64, :]),
                      waits=[(st_, avA), (st_, avB), (sa, stats_ms_a)])
                    A_ev = V(lambda e, p=p: e.tensor_copy(out=A_sb[64:128, 512 * p:512 * (p + 1)], in_=pav_o[0:64, :]), ms=True)
                    if kbtrace and qt == 1 and p == 7:
                        kbdump(3, [(sv, A_ev)])
                    V(lambda e: e.tensor_copy(out=dnbf[0:1, 0:512], in_=pav_e[64:65, :]))
                    dn_ev = V(lambda e: e.tensor_copy(out=dnbf[0:1, 512:1024], in_=pav_o[64:65, :]), ms=True)
                    w_ = [(sv, dn_ev)]
                    if p == 0:
                        w_.append((sa, pl_rd_a))
                    T(lambda e, p=p: e.matmul(out=plv, lhsT=e16bf[0:1, 16 * (2 * p):16 * (2 * p) + 16],
                                              rhs=dnbf[0:1, 0:512], start=(p == 0), stop=False, skip_group_check=True),
                      waits=w_)
                    pl_mm = T(lambda e, p=p: e.matmul(out=plv, lhsT=e16bf[0:1, 16 * (2 * p + 1):16 * (2 * p + 1) + 16],
                                                      rhs=dnbf[0:1, 512:1024], start=False, stop=(p == 7), skip_group_check=True), ms=True)
                if qt == 3:
                    kb_dump["qf_dead"] = st_.v
                lnm = A_(lambda e: e.activation(out=rd_sb, in_=plv, func=AF.Ln, bias=0.0, scale=1.0),
                         waits=[(st_, pl_mm)], ms=True)
                lnm = A_(lambda e: e.activation(out=rdbf, in_=rd_sb, func=AF.Exp, bias=0.0, scale=-1.0),
                         waits=[(sa, lnm)], ms=True)
                pl_rd_a = lnm
                for g_ in range(8):
                    mr = T(lambda e, g_=g_: e.matmul(out=pR, lhsT=selbf[:, 128 * g_:128 * (g_ + 1)], rhs=rdbf,
                                                     start=True, stop=True, skip_group_check=True),
                           waits=[(sa, lnm), (sv, pbc)], ms=True)
                    w_ = [(st_, mr)]
                    if oproj_t:
                        w_.append((st_, oproj_t))
                    abf_ms = V(lambda e, g_=g_: e.tensor_tensor(
                        out=Abf[:, 512 * g_:512 * (g_ + 1)], in0=pR[:],
                        in1=A_sb[:, 512 * g_:512 * (g_ + 1)], op=ALU.mult), waits=w_, ms=True)
                    pbc = abf_ms
                if kbtrace:
                    kbdump({0: 1, 1: 4, 2: 6, 3: 7}[qt], [(sv, pbc)])
                for tg in range(4):
                    stc = s_ * 16 + qt * 4 + tg
                    par = tg % 2
                    w_ = [(sv, xa_user[par])]
                    if b_st2[par]:
                        w_.append((dg2[par], b_st2[par]))
                    xa_ld[par] = LD(lambda e, stc=stc, par=par: e.dma_start(out=xa[par], in_=srcv[:, stc, :]),
                                    waits=w_, sem=dx2[par])
                    for n2 in range(2):
                        w_m = [(sv, abf_ms), (sv, psB_rd_v[n2]), (sa, psB_rd[n2]), (dsp, wo_done)]
                        for p in range(8):
                            T(lambda e, tg=tg, n2=n2, p=p: e.matmul(
                                out=psB[n2], lhsT=Abf[:, 512 * p + 128 * tg:512 * p + 128 * (tg + 1)],
                                rhs=wo_b[:, p, 512 * n2:512 * (n2 + 1)],
                                start=(p == 0), stop=False, skip_group_check=True),
                                waits=w_m if p == 0 else ())
                        mo = T(lambda e, n2=n2: e.matmul(out=psB[n2], lhsT=ones128,
                                                         rhs=bor[0:1, 512 * n2:512 * (n2 + 1)],
                                                         start=False, stop=True, skip_group_check=True), ms=True)
                        oproj_t = mo
                        psB_rd[n2] = 0
                        psB_rd_v[n2] = V(lambda e, n2=n2, par=par: e.tensor_tensor(
                            out=xa[par][:, 512 * n2:512 * (n2 + 1)], in0=psB[n2],
                            in1=xa[par][:, 512 * n2:512 * (n2 + 1)], op=ALU.add),
                            waits=[(st_, mo), (dx2[par], xa_ld[par])], ms=True)
                    sv_stat = V(lambda e, stc=stc, par=par: e.reduce_sum(out=ssum[:, stc:stc + 1], in_=xa[par], axis=AX.X), ms=True)
                    xa_user[par] = sv_stat
                    stats_ms_a = A_(lambda e, stc=stc, par=par: e.activation(
                        out=A_sb[:, 0:1024], in_=xa[par], func=AF.Square, accum_out=ssq[:, stc:stc + 1]),
                        waits=[(sv, sv_stat)], ms=True)
                    b_st2[par] = GD(lambda e, stc=stc, par=par: e.dma_start(out=dstv[:, stc, :], in_=xa[par]),
                                    waits=[(sv, sv_stat), (sa, stats_ms_a)], sem=dg2[par])
                if kbtrace and qt < 2:
                    kbdump({0: 2, 1: 5}[qt], [(sa, stats_ms_a)])
        emit_attn.qf_dead = kb_dump.get("qf_dead", st_.v)
        return sv.v, sa.v, pbc

    if DEV_PHASES == "ffn1":
        fv, fa = emit_ffn(x0, out_d, collect_stats=False,
                          phase_wait_v=fin_v, phase_wait_a=fin_a,
                          mm0_wait=[(sv, wq_done_v)])
    elif DEV_PHASES == "attn":
        fv, fa = emit_ffn(x0, x1d, collect_stats=True,
                          phase_wait_v=fin_v, phase_wait_a=fin_a,
                          mm0_wait=[(sv, wq_done_v)])
        f1v, f1a = finalize_stats(fv, fa)
        av, aa, apbc = emit_attn(x1d, out_d, f1v, f1a,
                                 store_wait=emit_ffn.store_tot, pbc0=emit_ffn.last_pbc)
    else:
        fv, fa = emit_ffn(x0, x1d, collect_stats=True,
                          phase_wait_v=fin_v, phase_wait_a=fin_a,
                          mm0_wait=[(sv, wq_done_v)])
        f1v, f1a = finalize_stats(fv, fa)
        att_dst = out_d if DEV_PHASES == "attn2" else x2d
        av, aa, apbc = emit_attn(x1d, att_dst, f1v, f1a,
                                 store_wait=emit_ffn.store_tot, pbc0=emit_ffn.last_pbc)
        POST = os.environ.get("ATH_POST", "fin,rl,pre")
        f2v, f2a = (av, aa)
        if "fin" in POST:
            f2v, f2a = finalize_stats(av, aa, nsteps=int(os.environ.get("ATH_FSTEPS", "9")))
        rl0 = dw.v
        if "rl" in POST:
            early = [(st_, emit_attn.qf_dead), (dsp, dsp.v)]
            for j in (2, 3, 4, 5):
                LD(lambda e, j=j: e.dma_start(out=w1q[:, j * 4096:(j + 1) * 4096], in_=w1qs_v[j, :, :]),
                   waits=early)
                early = []
            for j in range(4, 32):
                LD(lambda e, j=j: e.dma_start(out=w2q[:, j * 1024:(j + 1) * 1024], in_=w2qs_v[j, :, :]))
            late = [(sv, av), (sa, aa)]
            for j in (0, 1, 6, 7):
                LD(lambda e, j=j: e.dma_start(out=w1q[:, j * 4096:(j + 1) * 4096], in_=w1qs_v[j, :, :]),
                   waits=late)
                late = []
            for j in range(4):
                LD(lambda e, j=j: e.dma_start(out=w2q[:, j * 1024:(j + 1) * 1024], in_=w2qs_v[j, :, :]))
        rl_done = dw.v
        if "pre" in POST and os.environ.get("ATH_PRE"):
            A_(lambda e: e.preload_act_table(AF.Silu), waits=[(sa, f2a)])
        if DEV_PHASES != "attn2":
            fv, fa = emit_ffn(x2d, out_d, collect_stats=False,
                              phase_wait_v=f2v, phase_wait_a=f2a,
                              extra_first=[(dg2[0], dg2[0].v), (dg2[1], dg2[1].v),
                                           (dg3[0], dg3[0].v), (dg3[1], dg3[1].v)],
                              pbc0=apbc, mm0_wait=[(dw, rl_done)])

    if dbg1 is not None:
        bs = getattr(emit_ffn, "b_store", {})
        for k, col in ([] if os.environ.get("ATH_SNAP") else [(0, 0), (3, 32), (7, 64)]):
            if (k, 1) in bs:
                GD(lambda e, col=col: e.dma_start(out=dbg4[:, 260 + col:260 + col + 32], in_=statr[:]),
                   waits=[(dg3[k % 2], bs[(k, 1)])])
        GD(lambda e: e.dma_start(out=dbg1[:, 0:4], in_=wsc[:]), waits=[(sv, sv.v), (sa, sa.v), (sp, sp.v), (st_, st_.v)])
        GD(lambda e: e.dma_start(out=dbg1[:, 4:36], in_=statm[:]))
        GD(lambda e: e.dma_start(out=dbg1[:, 36:68], in_=statr[:]))
        GD(lambda e: e.dma_start(out=dbg1[:, 68:84], in_=tmp[:]))
        GD(lambda e: e.dma_start(out=dbg1[:, 84:92], in_=consts[:]))
        GD(lambda e: e.dma_start(out=dbg1[:, 124:156], in_=ssq[:]))
        GD(lambda e: e.dma_start(out=dbg4[:, 0:64], in_=guard1[:]))
        GD(lambda e: e.dma_start(out=dbg4[:, 64:128], in_=guard2[:]))
        GD(lambda e: e.dma_start(out=dbg4[:, 128:144], in_=tmp[:]))
        GD(lambda e: e.dma_start(out=dbg4[:, 144:148], in_=wsc[:]))
        GD(lambda e: e.dma_start(out=dbg4[:, 148:156], in_=consts[:]))
        GD(lambda e: e.dma_start(out=dbg4[:, 156:188], in_=statm[:]))
        GD(lambda e: e.dma_start(out=dbg2[:, :], in_=hbuf))

        if not os.environ.get("ATH_KBTRACE"):
            GD(lambda e: e.dma_start(out=dbg3[:, :], in_=yqT[:]))

    pr.op("gpsimd", lambda e: e.nop(),
          waits=[(dgd, dgd.v), (dsp, dsp.v), (dsq, dsq.v), (dg2[0], dg2[0].v), (dg2[1], dg2[1].v),
                 (dg3[0], dg3[0].v), (dg3[1], dg3[1].v),
                 (dx2[0], dx2[0].v), (dx2[1], dx2[1].v),
                 (dx3[0], dx3[0].v), (dx3[1], dx3[1].v), (dw, dw.v),
                 (dw2[0], dw2[0].v), (dw2[1], dw2[1].v)])

    if os.environ.get("ATH_TAGDUMP"):
        with open("_tagdump.txt", "w") as f:
            for eng in ENGINES:
                for i, (fn, ws, inc, incv) in enumerate(pr.q[eng]):
                    wstr = ",".join(f"{s.name}>={v}" for s, v in ws)
                    istr = f"+{inc.name}" if inc is not None else ""
                    f.write(f"{eng} {i} [{wstr}] {istr}\n")

    block = stack.enter_context(nc.Block())

    namelog = open("_instmap.txt", "w") if os.environ.get("ATH_TAGDUMP") else None

    def runner(ops, ename=""):
        def run(eng):
            for i, (fn, waits, inc, incv) in enumerate(ops):
                for s, v in waits:
                    eng.wait_ge(s.h, v)
                instr = fn(eng)
                if inc is not None:
                    instr.then_inc(inc.h, incv)
                if namelog is not None:
                    namelog.write(f"{ename} {i} {getattr(instr, 'name', '?')}\n")
        return run

    block.sync(runner(pr.q["sync"], "sync"))
    block.vector(runner(pr.q["vector"], "vector"))
    block.scalar(runner(pr.q["scalar"], "scalar"))
    block.tensor(runner(pr.q["tensor"], "tensor"))
    block.gpsimd(runner(pr.q["gpsimd"], "gpsimd"))
    if namelog is not None:
        namelog.close()

    stack.close()
    return nc



def kernel(**inputs):
    x = np.asarray(inputs["L0_attn_hidden_states"])
    cos = np.asarray(inputs["L0_attn_cos"])
    sin = np.asarray(inputs["L0_attn_sin"])
    wq = np.asarray(inputs["L0_attn_wq"])
    bq = np.asarray(inputs["L0_attn_bq"])
    wk = np.asarray(inputs["L0_attn_wk"])
    wv = np.asarray(inputs["L0_attn_wv"])
    bv = np.asarray(inputs["L0_attn_bv"])
    wo = np.asarray(inputs["L0_attn_wo"])
    bo = np.asarray(inputs["L0_attn_bo"])
    w1 = np.asarray(inputs["L0_ffn_w1"])
    cwv = np.asarray(inputs["L0_ffn_conv_w"])
    cb = np.asarray(inputs["L0_ffn_conv_b"])
    w2 = np.asarray(inputs["L0_ffn_w2"])

    w1t = np.ascontiguousarray(w1.T)
    w2t = np.ascontiguousarray(w2.T)
    cwm = np.zeros((128, 128), dtype=np.float32)
    taps = cwv[:, 0, :]
    for tap in range(3):
        cwm[:, 32 * tap:32 * tap + 32] = taps[:, tap].reshape(32, 128).T
    cwm[:, 96:128] = cb.reshape(32, 128).T
    cost = np.ascontiguousarray(np.tile(cos.T, (2, 1)).astype(np.float32, copy=False))
    sint = np.ascontiguousarray(np.tile(sin.T, (2, 1)).astype(np.float32, copy=False))

    identf = np.eye(128, dtype=np.float32)
    onesf = np.ones((128, 128), dtype=np.float32)
    e16 = np.zeros((1, 256), dtype=np.float32)
    for h in range(16):
        e16[0, 16 * h + h] = 1
    selm = np.zeros((16, 1024), dtype=np.float32)
    for g in range(8):
        selm[2 * g, 128 * g:128 * g + 64] = 1
        selm[2 * g + 1, 128 * g + 64:128 * g + 128] = 1
    constsv = np.zeros((128, 8), dtype=np.float32)
    constsv[:, 0] = 1e-5
    constsv[:, 1] = 1.0 / 127.0
    constsv[:, 2] = -C_SHIFT
    constsv[:, 3] = EPS
    constsv[:, 4] = 0.5

    nc = build_nc()
    shared = dict(w1t=w1t, w2t=w2t, cwdrm=cwm, wq=wq, wk=wk, wv=wv, wo=wo,
                  bq=bq.reshape(1, -1), bv=bv.reshape(1, -1), bo=bo.reshape(1, -1),
                  cosd=cost, sind=sint, identfd=identf,
                  onesfd=onesf, e16d=e16, seld=selm, constsd=constsv)
    in_maps = []
    for c in range(NCORE):
        m = dict(shared)
        m["x"] = np.ascontiguousarray(x[2 * c:2 * c + 2].reshape(NTOK, D))
        in_maps.append(m)

    res = run_bass_kernel_spmd(nc, in_maps, core_ids=list(range(NCORE)))
    if os.environ.get("ATH_DEBUG"):
        kernel._dbg = res.results
    out = np.stack([np.asarray(r["out"]) for r in res.results])
    return np.ascontiguousarray(out.reshape(16, 2048, 1024).astype(np.float32, copy=False))


# version 20 (final)
# speedup vs baseline: 1.0277x; 1.0015x over previous
import os
import numpy as np
import ml_dtypes
import concourse.bass as bass
import concourse.mybir as mybir
from concourse.bass_utils import run_bass_kernel_spmd
from contextlib import ExitStack

F32 = mybir.dt.float32
BF16 = mybir.dt.bfloat16
I32 = mybir.dt.int32
AF = mybir.ActivationFunctionType
ALU = mybir.AluOpType
AX = mybir.AxisListType

NCORE = 8
SEQ = 2048
NSEQ = 2
NTOK = SEQ * NSEQ
D = 1024
DH = 4096
NH, HD = 16, 64
TOK = 256
NT = NTOK // TOK
TPS = SEQ // TOK
C_SHIFT = 12.0
RMAGIC = 12582912.0
EPS = 1e-5

ENGINES = ("sync", "vector", "scalar", "tensor", "gpsimd")
DEV_PHASES = os.environ.get("ATH_PHASES", "full")
CUT = int(os.environ.get("ATH_CUT", "9"))


class Sem:
    __slots__ = ("h", "v", "name")
    def __init__(self, h, name):
        self.h, self.v, self.name = h, 0, name


class Prog:
    def __init__(self):
        self.q = {e: [] for e in ENGINES}

    def op(self, eng, fn, waits=(), inc=None, incv=1):
        best = {}
        for s, v in waits:
            if v > 0 and v > best.get(id(s), (None, 0))[1]:
                best[id(s)] = (s, v)
        ws = tuple(best.values())
        self.q[eng].append((fn, ws, inc, incv))
        if inc is not None:
            inc.v += incv
            return inc.v
        return None


def build_nc():
    nc = bass.Bass("TRN2", target_bir_lowering=False)

    x0 = nc.dram_tensor("x", [NTOK, D], F32, kind="ExternalInput").ap()
    w1t = nc.dram_tensor("w1t", [D, DH], F32, kind="ExternalInput").ap()
    w2t = nc.dram_tensor("w2t", [DH, D], F32, kind="ExternalInput").ap()
    cwd = nc.dram_tensor("cwdrm", [128, 128], F32, kind="ExternalInput").ap()
    wqd = nc.dram_tensor("wq", [D, D], F32, kind="ExternalInput").ap()
    wkd = nc.dram_tensor("wk", [D, D], F32, kind="ExternalInput").ap()
    wvd = nc.dram_tensor("wv", [D, D], F32, kind="ExternalInput").ap()
    wod = nc.dram_tensor("wo", [D, D], F32, kind="ExternalInput").ap()
    bqd = nc.dram_tensor("bq", [1, D], F32, kind="ExternalInput").ap()
    bvd = nc.dram_tensor("bv", [1, D], F32, kind="ExternalInput").ap()
    bod = nc.dram_tensor("bo", [1, D], F32, kind="ExternalInput").ap()
    cosd = nc.dram_tensor("cosd", [128, 1024], F32, kind="ExternalInput").ap()
    sind = nc.dram_tensor("sind", [128, 1024], F32, kind="ExternalInput").ap()
    
    identf_d = nc.dram_tensor("identfd", [128, 128], F32, kind="ExternalInput").ap()
    
    onesf_d = nc.dram_tensor("onesfd", [128, 128], F32, kind="ExternalInput").ap()
    e16_d = nc.dram_tensor("e16d", [1, 256], F32, kind="ExternalInput").ap()
    sel_d = nc.dram_tensor("seld", [16, 1024], F32, kind="ExternalInput").ap()
    consts_d = nc.dram_tensor("constsd", [128, 8], F32, kind="ExternalInput").ap()
    out_d = nc.dram_tensor("out", [NTOK, D], F32, kind="ExternalOutput").ap()
    dbg1 = dbg2 = dbg3 = dbg4 = None
    if os.environ.get("ATH_DEBUG"):
        dbg1 = nc.dram_tensor("dbg1", [128, 160], F32, kind="ExternalOutput").ap()
        dbg2 = nc.dram_tensor("dbg2", [128, 8256], F32, kind="ExternalOutput").ap()
        dbg3 = nc.dram_tensor("dbg3", [128, 6192], BF16, kind="ExternalOutput").ap()
        dbg4 = nc.dram_tensor("dbg4", [128, 360], F32, kind="ExternalOutput").ap()
    x1d = nc.dram_tensor("x1buf", [NTOK, D], F32, kind="Internal").ap()
    x2d = nc.dram_tensor("x2buf", [NTOK, D], F32, kind="Internal").ap()
    w1qs = nc.dram_tensor("w1qs", [D, DH], BF16, kind="Internal").ap()
    w2qs = nc.dram_tensor("w2qs", [DH, D], BF16, kind="Internal").ap()

    stack = ExitStack()
    sb = lambda n, s, d: stack.enter_context(nc.sbuf_tensor(n, s, d))

    AR = sb("arena", [128, 45120], F32)
    arb = AR[:].bitcast(BF16)

    def arf(off, n):
        return AR[:, off:off + n]
    def arh(off_f32, n_bf16):
        return arb[:, 2 * off_f32:2 * off_f32 + n_bf16]

    w1q = arh(0, 32768)
    w2q = arh(16384, 32768)
    hbuf = arf(32768, 8256)
    hbuf_i = hbuf.bitcast(I32)
    xio = [arf(41024, 2048), arf(43072, 2048)]

    stg = sb("stg", [128, 1024], F32)
    stg_iv = stg[:].bitcast(I32)
    blk = sb("blk", [128, 256], BF16)
    blkint = sb("blkint", [128, 256], I32)
    sqint = sb("sqint", [128, 256], I32)
    yqT = sb("yqT", [128, 6192], BF16)
    r1row = sb("r1row", [1, 774], F32)
    cvb_t = sb("cvb", [128, 512], F32)
    sqbf = cvb_t[:].bitcast(BF16)
    r1bc = sb("r1bc", [128, 258], F32)
    r2bc = r1bc[:, 0:256]
    r2row = sb("r2row", [1, 256], F32)
    accd = sb("accd", [128, 256], F32)
    accp = sb("accp", [128, 256], F32)
    ssum = sb("ssum", [128, 32], F32)
    ssq = sb("ssq", [128, 32], F32)
    statm = sb("statm", [128, 32], F32)
    guard1 = sb("guard1", [128, 64], F32)
    statr = sb("statr", [128, 32], F32)
    guard2 = sb("guard2", [128, 64], F32)
    tmp = sb("tmp", [128, 16], F32)
    wsc = sb("wsc", [128, 4], F32)
    sc11 = sb("sc11", [1, 4], F32)
    consts = sb("consts", [128, 8], F32)
    cw = sb("cwsb", [128, 128], F32)
    identbf = sb("identbf", [128, 128], BF16)
    identf = sb("identf", [128, 128], F32)
    onesf = sb("onesf", [128, 128], F32)

    CL_CLIP, CL_R127, CL_NSH, CL_EPS, CL_HALF = 0, 1, 2, 3, 4

    P = stack.enter_context(nc.psum_tensor("P", [128, 4096], F32))
    ph = [P[:, 0:258], P[:, 512:770]]
    po = [P[:, 1024:2048], P[:, 2048:3072]]
    ptrb = P[:, 3072:3584].bitcast(BF16)
    pbcf = P[:, 3584:4096]

    def mksem(name):
        return Sem(stack.enter_context(nc.semaphore(name)), name)
    dx2 = [mksem("dxa"), mksem("dxb")]
    dx3 = [mksem("dxc"), mksem("dxd")]
    dsq = mksem("dsq")
    dg3 = [mksem("dge"), mksem("dgf")]
    dw = mksem("dw")
    dw2 = [mksem("dwa"), mksem("dwb")]
    dg2 = [mksem("dga"), mksem("dgb")]
    dsp = mksem("dsp")
    dgd = mksem("dgd")
    sv = mksem("sv")
    sa = mksem("sa")
    sp = mksem("sp")
    st_ = mksem("st")

    pr = Prog()

    def _chain(waits, sem):
        w = tuple(waits)
        if any(s_ is sem and v_ >= sem.v for s_, v_ in w):
            return w
        return w + ((sem, sem.v),)

    def V(fn, waits=(), ms=False):
        return pr.op("vector", fn, _chain(waits, sv), sv)
    def A_(fn, waits=(), ms=False):
        return pr.op("scalar", fn, _chain(waits, sa), sa)
    def T(fn, waits=(), ms=False):
        return pr.op("tensor", fn, waits, st_ if ms else None)
    def G(fn, waits=(), ms=False):
        return pr.op("gpsimd", fn, _chain(waits, sp), sp)
    def LD(fn, waits=(), sem=None):
        sem = sem or dw
        return pr.op("sync", fn, waits, sem, incv=16)
    def GD(fn, waits=(), sem=None):
        return pr.op("gpsimd", fn, waits, sem or dgd, incv=16)

    LD(lambda e: e.dma_start(out=consts[:], in_=consts_d))
    LD(lambda e: e.dma_start(out=cw[:], in_=cwd))
    idl = LD(lambda e: e.dma_start(out=identf[:], in_=identf_d))
    cl = LD(lambda e: e.dma_start(out=onesf[:], in_=onesf_d))
    V(lambda e: e.tensor_copy(out=identbf[:], in_=identf[:]), waits=[(dw, cl)])
    V(lambda e: e.memset(guard1[:], 0.0))
    V(lambda e: e.memset(guard2[:], 0.0))

    x0v = x0.rearrange("(s p) d -> p s d", p=128)
    stat_v = {}
    stat_a = {}
    ld_ms = {}
    for s_ in range(32):
        b = s_ % 2
        w_ = []
        if s_ >= 2:
            w_ = [(sv, stat_v[s_ - 2]), (sa, stat_a[s_ - 2])]
        ld_ms[s_] = LD(lambda e, s_=s_, b=b: e.dma_start(out=xio[b][:, 0:1024], in_=x0v[:, s_, :]),
                       waits=w_, sem=dx2[b])
        stat_v[s_] = V(lambda e, s_=s_, b=b: e.reduce_sum(out=ssum[:, s_:s_ + 1], in_=xio[b][:, 0:1024], axis=AX.X),
                       waits=[(dx2[b], ld_ms[s_])], ms=True)
        wsq = [(dx2[b], ld_ms[s_])]
        if s_ >= 1:
            wsq.append((sa, stat_a[s_ - 1]))
        stat_a[s_] = A_(lambda e, s_=s_, b=b: e.activation(out=stg[:, 0:1024], in_=xio[b][:, 0:1024], func=AF.Square, accum_out=ssq[:, s_:s_ + 1]),
                        waits=wsq, ms=True)
    stats_v, stats_a = stat_v[31], stat_a[31]

    wload = [hbuf[:, 0:4096], hbuf[:, 4096:8192]]
    wload_i = [hbuf_i[:, 0:4096], hbuf_i[:, 4096:8192]]
    w1tv = w1t.rearrange("(c p) h -> c p h", p=128)
    w2tv = w2t.rearrange("(c p) h -> c p h", p=128)

    red_ms = {}
    for j in range(8):
        b = j % 2
        w_ = [] if j < 2 else [(sv, red_ms[j - 2])]
        ld = LD(lambda e, j=j, b=b: e.dma_start(out=wload[b], in_=w1tv[j, :, :]), waits=w_, sem=dw2[b])
        red_ms[j] = V(lambda e, j=j, b=b: e.reduce_sum(out=tmp[:, j:j + 1], in_=wload[b], axis=AX.X, apply_absolute_value=True),
                      waits=[(dw2[b], ld)], ms=True)
    V(lambda e: e.reduce_sum(out=tmp[:, 8:9], in_=tmp[:, 0:8], axis=AX.X), waits=[(sv, red_ms[7])])
    for j in range(32):
        b = j % 2
        w_ = [(sv, red_ms[j - 2])] if j >= 2 else [(sv, red_ms[7] if b else red_ms[6])]
        ld = LD(lambda e, j=j, b=b: e.dma_start(out=wload[b][:, 0:1024], in_=w2tv[j, :, :]), waits=w_, sem=dw2[b])
        acc = 12 + b
        if j < 2:
            red_ms[j] = V(lambda e, b=b, acc=acc: e.reduce_sum(out=tmp[:, acc:acc + 1], in_=wload[b][:, 0:1024], axis=AX.X, apply_absolute_value=True),
                          waits=[(dw2[b], ld)], ms=True)
        else:
            rr = V(lambda e, b=b: e.reduce_sum(out=tmp[:, 10 + b:11 + b], in_=wload[b][:, 0:1024], axis=AX.X, apply_absolute_value=True),
                   waits=[(dw2[b], ld)], ms=True)
            red_ms[j] = V(lambda e, b=b, acc=acc: e.tensor_tensor(out=tmp[:, acc:acc + 1], in0=tmp[:, acc:acc + 1], in1=tmp[:, 10 + b:11 + b], op=ALU.add),
                          waits=[(sv, rr)], ms=True)
    vsum2 = V(lambda e: e.tensor_tensor(out=tmp[:, 9:10], in0=tmp[:, 12:13], in1=tmp[:, 13:14], op=ALU.add),
              waits=[(sv, red_ms[31]), (sv, red_ms[30])], ms=True)

    tps = T(lambda e: e.matmul(out=pbcf[0:1, 0:1], lhsT=tmp[:, 8:9], rhs=onesf[:, 0:1], start=True, stop=True, skip_group_check=True),
            waits=[(sv, vsum2), (dw, cl)])
    tps = T(lambda e: e.matmul(out=pbcf[0:1, 1:2], lhsT=tmp[:, 9:10], rhs=onesf[:, 0:1], start=True, stop=True, skip_group_check=True), ms=True)
    vm = V(lambda e: e.tensor_scalar(out=sc11[0:1, 0:2], in0=pbcf[0:1, 0:2], scalar1=1.0 / (DH * D), scalar2=1e-5, op0=ALU.mult, op1=ALU.max),
           waits=[(st_, tps)], ms=True)
    tbc = T(lambda e: e.matmul(out=pbcf[:, 2:4], lhsT=onesf[0:1, :], rhs=sc11[0:1, 0:2], start=True, stop=True, skip_group_check=True),
            waits=[(sv, vm)], ms=True)
    V(lambda e: e.tensor_copy(out=wsc[:, 0:1], in_=pbcf[:, 2:3]), waits=[(st_, tbc)])
    vx = V(lambda e: e.tensor_copy(out=wsc[:, 2:3], in_=pbcf[:, 3:4]), ms=True)
    vx = V(lambda e: e.reciprocal(out=wsc[:, 1:2], in_=wsc[:, 0:1]), waits=[(sv, vx)], ms=True)
    wsc_rdy = V(lambda e: e.reciprocal(out=wsc[:, 3:4], in_=wsc[:, 2:3]), waits=[(sv, vx)], ms=True)
    wsc_rdy = V(lambda e: e.tensor_copy(out=tmp[:, 15:16], in_=wsc[:, 3:4]), waits=[(sv, wsc_rdy)], ms=True)
    pbc_free = wsc_rdy

    w1qs_v = w1qs.rearrange("(c p) h -> c p h", p=128)
    w2qs_v = w2qs.rearrange("(c p) h -> c p h", p=128)
    q_ms = {}
    for j in range(8):
        b = j % 2
        w_ = [(sv, wsc_rdy)] if j < 2 else [(sv, q_ms[j - 2])]
        ld = LD(lambda e, j=j, b=b: e.dma_start(out=wload[b], in_=w1tv[j, :, :]), waits=w_, sem=dw2[b])
        V(lambda e, j=j, b=b: e.tensor_scalar(out=wload[b], in0=wload[b], scalar1=wsc[:, 1:2], scalar2=None, op0=ALU.mult),
          waits=[(dw2[b], ld)])
        V(lambda e, j=j, b=b: e.tensor_scalar(out=wload[b], in0=wload[b], scalar1=RMAGIC, scalar2=RMAGIC, op0=ALU.add, op1=ALU.subtract))
        q_ms[j] = V(lambda e, j=j, b=b: e.tensor_scalar(out=w1q[:, j * 4096:(j + 1) * 4096], in0=wload[b], scalar1=-1.0, scalar2=1.0, op0=ALU.max, op1=ALU.min), ms=True)
        GD(lambda e, j=j: e.dma_start(out=w1qs_v[j, :, :], in_=w1q[:, j * 4096:(j + 1) * 4096]),
           waits=[(sv, q_ms[j])], sem=dsp)
    for j in range(32):
        b = j % 2
        w_ = [(sv, q_ms[6 + b])] if j < 2 else [(sv, q_ms[8 + j - 2])]
        ld = LD(lambda e, j=j, b=b: e.dma_start(out=wload[b][:, 0:1024], in_=w2tv[j, :, :]), waits=w_, sem=dw2[b])
        V(lambda e, j=j, b=b: e.tensor_scalar(out=wload[b][:, 0:1024], in0=wload[b][:, 0:1024], scalar1=wsc[:, 3:4], scalar2=None, op0=ALU.mult),
          waits=[(dw2[b], ld)])
        V(lambda e, j=j, b=b: e.tensor_scalar(out=wload[b][:, 0:1024], in0=wload[b][:, 0:1024], scalar1=RMAGIC, scalar2=RMAGIC, op0=ALU.add, op1=ALU.subtract))
        q_ms[8 + j] = V(lambda e, j=j, b=b: e.tensor_scalar(out=w2q[:, j * 1024:(j + 1) * 1024], in0=wload[b][:, 0:1024], scalar1=-1.0, scalar2=1.0, op0=ALU.max, op1=ALU.min), ms=True)
        GD(lambda e, j=j: e.dma_start(out=w2qs_v[j, :, :], in_=w2q[:, j * 1024:(j + 1) * 1024]),
           waits=[(sv, q_ms[8 + j])], sem=dsp)
    wq_done_v = q_ms[8 + 31]
    wspill = dsp.v

    def finalize_stats(wv_, wa_, nsteps=9):
        a = wa_
        if nsteps >= 1 and os.environ.get("ATH_PRE"):
            a = A_(lambda e: e.preload_act_table(AF.Exp), waits=[(sa, wa_)], ms=True)
        v = wv_
        if os.environ.get("ATH_FSKIPV"):
            nsteps_v = 0
        else:
            nsteps_v = nsteps
        if nsteps_v >= 2:
            v = V(lambda e: e.tensor_scalar(out=statm[:], in0=ssum[:], scalar1=1.0 / D, scalar2=None, op0=ALU.mult),
                  waits=[(sv, wv_), (sa, wa_)], ms=True)
        if nsteps_v >= 3:
            v = V(lambda e: e.tensor_tensor(out=statr[:], in0=statm[:], in1=statm[:], op=ALU.mult), waits=[(sv, v)], ms=True)
        if nsteps_v >= 4:
            v = V(lambda e: e.tensor_scalar(out=ssq[:], in0=ssq[:], scalar1=1.0 / D, scalar2=None, op0=ALU.mult), waits=[(sv, v)], ms=True)
        if nsteps_v >= 5:
            v = V(lambda e: e.tensor_tensor(out=statr[:], in0=ssq[:], in1=statr[:], op=ALU.subtract), waits=[(sv, v)], ms=True)
        if nsteps >= 6:
            lnf = {"sq": AF.Square, "ln": AF.Ln}[os.environ.get("ATH_LNF", "ln")]
            lnb = 0.0 if os.environ.get("ATH_LNB") else consts[:, CL_EPS:CL_EPS + 1]
            lno = tmp[:, 0:1] if os.environ.get("ATH_LNO") else statr[:]
            lni = statr[:, 0:1] if os.environ.get("ATH_LNO") else statr[:]
            a = A_(lambda e: e.activation(out=lno, in_=lni, func=lnf, bias=lnb, scale=1.0),
               waits=[(sv, v)], ms=True)
        if nsteps >= 7:
            a = A_(lambda e: e.activation(out=statr[:], in_=statr[:], func=AF.Exp, bias=0.0, scale=-0.5), waits=[(sa, a)], ms=True)
        return sv.v, a

    fin_v, fin_a = finalize_stats(max(stats_v, wsc_rdy), stats_a)
    if dbg1 is not None and DEV_PHASES == "ffn1":
        GD(lambda e: e.dma_start(out=dbg1[:, 92:124], in_=statr[:]), waits=[(sa, fin_a)])

    def emit_ffn(src, dst, collect_stats, phase_wait_v, phase_wait_a, extra_first=(), pbc0=None, mm0_wait=()):
        srcv = src.rearrange("(t g p) d -> t p g d", p=128, g=2)
        dstv = dst.rearrange("(t g p) d -> t p g d", p=128, g=2)

        if os.environ.get("ATH_PRE"):
            A_(lambda e: e.preload_act_table(AF.Silu), waits=[(sa, phase_wait_a)])

        x_ld = {}
        x_ld2 = {}
        a_v = {}
        a_a = {}
        a_t = {}
        b_store = {}
        b_stats_a = {}
        b_mm1_done = {}
        evac_ms = {}
        po_free = None
        nonloc = dict(pbc=pbc0 if pbc0 is not None else pbc_free, first_mm=True)

        def slot_view(slot):
            return yqT[:, slot * 2064:(slot + 1) * 2064].rearrange("p (c n) -> p c n", n=258)

        def a_part(i):
            slot = i % 3
            b = i % 2
            sfirst = (i % TPS) == 0
            slast = (i % TPS) == TPS - 1
            w0 = [(dx2[i % 2], x_ld[i])]
            if i >= 1:
                w0.append((st_, a_t[i - 1]))
            tg = {}
            cp_ms = {}
            qs_rdy_g = {}
            for g in range(2):
                stc = 2 * i + g
                wv_ = w0 if g == 0 else [(dx3[i % 2], x_ld2[i]), (sp, cp_ms[(0, 7)])]
                V(lambda e, b=b, g=g, stc=stc: e.tensor_scalar(
                    out=stg[:], in0=xio[b][:, g * 1024:(g + 1) * 1024],
                    scalar1=statm[:, stc:stc + 1], scalar2=None, op0=ALU.subtract, op1=ALU.max,
                    accum_out=tmp[:, g:g + 1]),
                  waits=wv_)
                V(lambda e, b=b, g=g, stc=stc: e.tensor_scalar(
                    out=stg[:], in0=xio[b][:, g * 1024:(g + 1) * 1024],
                    scalar1=statm[:, stc:stc + 1], scalar2=None, op0=ALU.subtract, op1=ALU.min,
                    accum_out=tmp[:, 2 + g:3 + g]))
                V(lambda e, g=g: e.tensor_scalar(out=tmp[:, 2 + g:3 + g], in0=tmp[:, 2 + g:3 + g], scalar1=-1.0, scalar2=None, op0=ALU.mult))
                V(lambda e, g=g: e.tensor_tensor(out=tmp[:, g:g + 1], in0=tmp[:, g:g + 1], in1=tmp[:, 2 + g:3 + g], op=ALU.max))
                V(lambda e, g=g: e.tensor_scalar(out=tmp[:, 2 + g:3 + g], in0=tmp[:, g:g + 1], scalar1=1e-5, scalar2=None, op0=ALU.max))
                vv = V(lambda e, g=g: e.reciprocal(out=tmp[:, 4 + g:5 + g], in_=tmp[:, 2 + g:3 + g]), ms=True)
                vv = V(lambda e, g=g: e.tensor_scalar(out=tmp[:, 4 + g:5 + g], in0=tmp[:, 4 + g:5 + g], scalar1=127.0, scalar2=None, op0=ALU.mult), waits=[(sv, vv)], ms=True)
                V(lambda e, g=g, i=i, stc=stc: e.tensor_tensor(out=tmp[:, 6 + g:7 + g], in0=statr[:, stc:stc + 1], in1=tmp[:, g:g + 1], op=ALU.mult))
                V(lambda e, g=g: e.tensor_scalar(out=tmp[:, 6 + g:7 + g], in0=tmp[:, 6 + g:7 + g], scalar1=1e-5, scalar2=None, op0=ALU.max))
                qs_rdy_g[g] = V(lambda e, g=g: e.tensor_scalar(out=tmp[:, 6 + g:7 + g], in0=tmp[:, 6 + g:7 + g], scalar1=wsc[:, 0:1], scalar2=1.0 / 127.0, op0=ALU.mult, op1=ALU.mult), ms=True)
                w_q = [(sv, qs_rdy_g[g])]
                if g == 1:
                    w_q.append((sp, cp_ms[(0, 7)]))
                qv = V(lambda e, g=g, b=b, stc=stc: e.tensor_scalar(
                    out=stg[:], in0=xio[b][:, g * 1024:(g + 1) * 1024],
                    scalar1=statm[:, stc:stc + 1], scalar2=tmp[:, 4 + g:5 + g],
                    op0=ALU.subtract, op1=ALU.mult),
                    waits=w_q, ms=True)
                qv = V(lambda e: e.tensor_scalar(out=stg[:], in0=stg[:], scalar1=RMAGIC, scalar2=RMAGIC, op0=ALU.add, op1=ALU.subtract), ms=True)
                prev_ev = a_a.get((i, g - 1)) or a_a.get((i - 1, 1))
                for j in range(8):
                    w_c = [(sv, qv)]
                    if (g, j) not in ((0, 0), (0, 1)):
                        pj = (g, j - 2) if j >= 2 else (g - 1, 6 + j)
                        w_c.append((st_, tg.get(pj, 0)))
                    G(lambda e, j=j: e.tensor_copy(out=blkint[:, 128 * (j % 2):128 * (j % 2) + 128], in_=stg[:, 128 * j:128 * (j + 1)]),
                      waits=w_c, ms=True)
                    cp_ms[(g, j)] = G(lambda e, j=j: e.tensor_copy(out=blk[:, 128 * (j % 2):128 * (j % 2) + 128], in_=blkint[:, 128 * (j % 2):128 * (j % 2) + 128]),
                                      ms=True)
                    w_t = [(sp, cp_ms[(g, j)])]
                    if j == 0 and prev_ev:
                        w_t.append((sa, prev_ev))
                    tg[(g, j)] = T(lambda e, g=g, j=j: e.transpose(
                        out=ptrb[:, 128 * j:128 * (j + 1)],
                        in_=blk[:, 128 * (j % 2):128 * (j % 2) + 128],
                        identity=identbf[:]),
                        waits=w_t, ms=True)
                dest = slot_view(slot)[:, :, 1 + 128 * g:1 + 128 * g + 128]
                a_a[(i, g)] = A_(lambda e, dest=dest: e.copy(out=dest, in_=ptrb[:].rearrange("p (c n) -> p c n", n=128)),
                                 waits=[(st_, tg[(g, 7)])], ms=True)
            qs_rdy = qs_rdy_g[1]
            w_ = [(sv, qs_rdy), (sv, nonloc["pbc"])]
            T(lambda e: e.transpose(out=pbcf[0:1, 0:128], in_=tmp[:, 6:7], identity=identf[:]), waits=w_)
            tr2 = T(lambda e: e.transpose(out=pbcf[0:1, 128:256], in_=tmp[:, 7:8], identity=identf[:]), ms=True)
            ev = V(lambda e, slot=slot: e.tensor_copy(out=r1row[0:1, slot * 258 + 1:slot * 258 + 257], in_=pbcf[0:1, 0:256]),
                   waits=[(st_, tr2)], ms=True)
            nonloc["pbc"] = ev
            hv = [(sa, a_a[(i, 1)]), (st_, b_mm1_done.get(i - 2, 0))]
            if sfirst:
                V(lambda e, slot=slot: e.memset(slot_view(slot)[:, :, 0:1], 0.0), waits=hv, ms=True)
                V(lambda e, slot=slot: e.memset(r1row[0:1, slot * 258:slot * 258 + 1], 1.0), ms=True)
            else:
                prev = (i - 1) % 3
                V(lambda e, slot=slot, prev=prev: e.tensor_copy(
                    out=slot_view(prev)[:, :, 257:258], in_=slot_view(slot)[:, :, 1:2]), waits=hv, ms=True)
                V(lambda e, slot=slot, prev=prev: e.tensor_copy(out=r1row[0:1, prev * 258 + 257:prev * 258 + 258], in_=r1row[0:1, slot * 258 + 1:slot * 258 + 2]), ms=True)
            if slast:
                V(lambda e, slot=slot: e.memset(slot_view(slot)[:, :, 257:258], 0.0), ms=True)
                V(lambda e, slot=slot: e.memset(r1row[0:1, slot * 258 + 257:slot * 258 + 258], 1.0), ms=True)
            else:
                nxt = (i + 1) % 3
                V(lambda e, slot=slot, nxt=nxt: e.tensor_copy(
                    out=slot_view(nxt)[:, :, 0:1], in_=slot_view(slot)[:, :, 256:257]), ms=True)
                V(lambda e, slot=slot, nxt=nxt: e.tensor_copy(out=r1row[0:1, nxt * 258:nxt * 258 + 1], in_=r1row[0:1, slot * 258 + 256:slot * 258 + 257]), ms=True)
            a_v[i] = sv.v
            a_t[i] = tg[(1, 7)]

        def b_part(i, halo_v):
            slot = i % 3
            b = i % 2
            w_ = [(sv, halo_v), (sv, nonloc["pbc"]), (sa, a_a[(i, 1)])]
            tb = T(lambda e, slot=slot: e.matmul(out=pbcf[:, 0:258], lhsT=onesf[0:1, :], rhs=r1row[0:1, slot * 258:slot * 258 + 258], start=True, stop=True, skip_group_check=True),
                   waits=w_, ms=True)
            r1bc_ms = V(lambda e: e.tensor_copy(out=r1bc[:], in_=pbcf[:, 0:258]), waits=[(st_, tb)], ms=True)
            nonloc["pbc"] = r1bc_ms

            silu_ms = {}
            tree_ms = {}
            mm1_ms = {}
            if CUT >= 1:
              for c in range(32):
                bank = c % 2
                w_mm = [(sv, evac_ms.get(bank, 0))]
                if c == 0:
                    w_mm.append((sv, r1bc_ms))
                    if b_stats_a.get(i - 1):
                        w_mm.append((sa, b_stats_a[i - 1]))
                    if nonloc.get("first_mm"):
                        w_mm += list(mm0_wait)
                        nonloc["first_mm"] = False
                for d_ in range(8):
                    mm1_ms[c] = T(lambda e, c=c, d_=d_, bank=bank, slot=slot: e.matmul(
                        out=ph[bank], lhsT=w1q[:, d_ * 4096 + c * 128:d_ * 4096 + (c + 1) * 128],
                        rhs=yqT[:, slot * 2064 + d_ * 258:slot * 2064 + (d_ + 1) * 258],
                        start=(d_ == 0), stop=(d_ == 7), skip_group_check=True),
                        waits=w_mm if d_ == 0 else (), ms=(d_ == 7))
                if c == 31:
                    b_mm1_done[i] = mm1_ms[c]
                hc = hbuf[:, c * 258:(c + 1) * 258]
                ev = V(lambda e, hc=hc, bank=bank: e.tensor_tensor(out=hc, in0=ph[bank], in1=r1bc[:], op=ALU.mult),
                       waits=[(st_, mm1_ms[c])], ms=True)
                evac_ms[bank] = ev
                if CUT >= 2:
                    cvb = cvb_t[:, 256 * (c % 2):256 * (c % 2) + 256]
                    w_t1 = [(sv, ev)]
                    if c >= 2:
                        w_t1.append((sv, tree_ms[c - 2]))
                    t1 = A_(lambda e, hc=hc, cvb=cvb, c=c: e.activation(
                        out=cvb, in_=hc[:, 1:257], func=AF.Identity,
                        bias=cw[:, 96 + (c % 32):97 + (c % 32)], scale=cw[:, 32 + (c % 32):33 + (c % 32)]),
                        waits=w_t1, ms=True)
                    t2 = V(lambda e, hc=hc, cvb=cvb, c=c: e.scalar_tensor_tensor(
                        out=cvb, in0=hc[:, 0:256], scalar=cw[:, (c % 32):(c % 32) + 1], in1=cvb, op0=ALU.mult, op1=ALU.add),
                        waits=[(sa, t1)], ms=True)
                    if not os.environ.get("ATH_POOLTAP"):
                        t3 = V(lambda e, hc=hc, cvb=cvb, c=c: e.scalar_tensor_tensor(
                            out=cvb, in0=hc[:, 2:258], scalar=cw[:, 64 + (c % 32):65 + (c % 32)], in1=cvb, op0=ALU.mult, op1=ALU.add),
                            waits=[(sv, t2)], ms=True)
                        A_(lambda e, hc=hc, cvb=cvb: e.activation(out=hc[:, 1:257], in_=cvb, func=AF.Silu),
                           waits=[(sv, t3)])
                    else:
                        G(lambda e, hc=hc, c=c: e.tensor_scalar(
                            out=accp[:], in0=hc[:, 2:258], scalar1=cw[:, 64 + (c % 32):65 + (c % 32)], scalar2=None, op0=ALU.mult),
                          waits=[(sv, ev)])
                        t3 = G(lambda e, cvb=cvb: e.tensor_tensor(out=cvb, in0=cvb, in1=accp[:], op=ALU.add),
                               waits=[(sv, t2)], ms=True)
                        A_(lambda e, hc=hc, cvb=cvb: e.activation(out=hc[:, 1:257], in_=cvb, func=AF.Silu),
                           waits=[(sp, t3)])
                    silu_ms[c] = A_(lambda e, hc=hc, cvb=cvb: e.activation(out=cvb, in_=hc[:, 1:257], func=AF.Abs), ms=True)
                    if c == 0:
                        tree_ms[c] = V(lambda e, cvb=cvb: e.tensor_copy(out=accd[:], in_=cvb),
                                       waits=[(sa, silu_ms[c])], ms=True)
                    else:
                        tree_ms[c] = V(lambda e, cvb=cvb: e.tensor_tensor(out=accd[:], in0=accd[:], in1=cvb, op=ALU.max),
                                       waits=[(sa, silu_ms[c])], ms=True)
            outsc_rdy = 0
            if CUT >= 3:
                comb = tree_ms[31]
                w_ = [(sv, comb), (sv, nonloc["pbc"])]
                T(lambda e: e.transpose(out=pbcf[:, 0:128], in_=accd[:, 0:128], identity=identf[:]), waits=w_)
                tt = T(lambda e: e.transpose(out=pbcf[:, 128:256], in_=accd[:, 128:256], identity=identf[:]), ms=True)
                rd1 = V(lambda e: e.reduce_max(out=tmp[:, 8:9], in_=pbcf[:, 0:128], axis=AX.X), waits=[(st_, tt)], ms=True)
                rd2 = V(lambda e: e.reduce_max(out=tmp[:, 9:10], in_=pbcf[:, 128:256], axis=AX.X), waits=[(sv, rd1)], ms=True)
                vv = V(lambda e: e.tensor_scalar(out=tmp[:, 10:12], in0=tmp[:, 8:10], scalar1=1e-5, scalar2=None, op0=ALU.max), waits=[(sv, rd2)], ms=True)
                vv = V(lambda e: e.reciprocal(out=tmp[:, 12:14], in_=tmp[:, 10:12]), waits=[(sv, vv)], ms=True)
                qs2_rdy = V(lambda e: e.tensor_scalar(out=tmp[:, 12:14], in0=tmp[:, 12:14], scalar1=127.0, scalar2=None, op0=ALU.mult), waits=[(sv, vv)], ms=True)
                outsc_rdy = V(lambda e: e.tensor_scalar(out=tmp[:, 14:16], in0=tmp[:, 10:12], scalar1=wsc[:, 2:3], scalar2=0.5 / 127.0, op0=ALU.mult, op1=ALU.mult), waits=[(sv, qs2_rdy)], ms=True)
                tq = T(lambda e: e.transpose(out=pbcf[0:1, 256:384], in_=tmp[:, 12:13], identity=identf[:]), waits=[(sv, qs2_rdy)])
                tq = T(lambda e: e.transpose(out=pbcf[0:1, 384:512], in_=tmp[:, 13:14], identity=identf[:]), ms=True)
                evq = V(lambda e: e.tensor_copy(out=r2row[0:1, 0:256], in_=pbcf[0:1, 256:512]), waits=[(st_, tq)], ms=True)
                tbc2 = T(lambda e: e.matmul(out=pbcf[:, 0:256], lhsT=onesf[0:1, :], rhs=r2row[0:1, 0:256], start=True, stop=True, skip_group_check=True),
                         waits=[(sv, evq)], ms=True)
                r2bc_ms = V(lambda e: e.tensor_copy(out=r2bc, in_=pbcf[:, 0:256]), waits=[(st_, tbc2)], ms=True)
                nonloc["pbc"] = r2bc_ms
            ev = r1bc_ms
            if CUT >= 4:
                accf = [accd[:], accp[:]]
                mm2_ms = {}
                qt_ms = {}
                i2_ms = {}
                for c in range(32):
                    sq = sqbf[:, 256 * (c % 2):256 * (c % 2) + 256]
                    acc = accf[c % 2]
                    w_ = [(sv, r2bc_ms)] if c < 2 else []
                    V(lambda e, c=c, acc=acc: e.tensor_tensor(
                        out=acc, in0=hbuf[:, c * 258 + 1:c * 258 + 257], in1=r2bc, op=ALU.mult),
                        waits=w_, ms=True)
                    w_ = [] if c < 1 else [(sp, i2_ms[c - 1])]
                    qt_ms[c] = V(lambda e, acc=acc: e.tensor_scalar(
                        out=sqint[:], in0=acc, scalar1=RMAGIC, scalar2=RMAGIC, op0=ALU.add, op1=ALU.subtract),
                        waits=w_, ms=True)
                    w_ = [(sv, qt_ms[c])]
                    if c >= 2:
                        w_.append((st_, mm2_ms[c - 2]))
                    i2_ms[c] = G(lambda e, c=c, sq=sq: e.tensor_copy(out=sq, in_=sqint[:]),
                                 waits=w_, ms=True)
                    for g in range(2):
                        for n2 in range(2):
                            w_m = [(sp, i2_ms[c])]
                            if c == 0 and g == 0 and n2 == 0 and po_free is not None:
                                w_m.append((sv, po_free))
                            mm2_ms[c] = T(lambda e, c=c, g=g, n2=n2, sq=sq: e.matmul(
                                out=po[g][:, n2 * 512:(n2 + 1) * 512],
                                lhsT=sq[:, g * 128:(g + 1) * 128],
                                rhs=w2q[:, c * 1024 + n2 * 512:c * 1024 + n2 * 512 + 512],
                                start=(c == 0), stop=(c == 31), skip_group_check=True),
                                waits=w_m if (g == 0 and n2 == 0) else (), ms=(g == 1 and n2 == 1))
                ev = None
            stat_aa = 0
            for g in range(2):
                if CUT >= 4:
                    for n2 in range(2):
                        ev = V(lambda e, g=g, n2=n2, b=b: e.scalar_tensor_tensor(
                            out=xio[b][:, g * 1024 + n2 * 512:g * 1024 + n2 * 512 + 512],
                            in0=po[g][:, n2 * 512:(n2 + 1) * 512],
                            scalar=tmp[:, 14 + g:15 + g],
                            in1=xio[b][:, g * 1024 + n2 * 512:g * 1024 + n2 * 512 + 512],
                            op0=ALU.mult, op1=ALU.add),
                            waits=[(st_, mm2_ms[31]), (sv, outsc_rdy)] if (g == 0 and n2 == 0) else (), ms=True)
                stat_vv = 0
                if collect_stats:
                    stc = 2 * i + g
                    stat_vv = V(lambda e, g=g, b=b, stc=stc: e.reduce_sum(out=ssum[:, stc:stc + 1], in_=xio[b][:, g * 1024:(g + 1) * 1024], axis=AX.X), ms=True)
                    stat_aa = A_(lambda e, g=g, b=b, stc=stc: e.activation(out=hbuf[:, 0:1024], in_=xio[b][:, g * 1024:(g + 1) * 1024], func=AF.Square, accum_out=ssq[:, stc:stc + 1]),
                                 waits=[(sv, stat_vv)], ms=True)
                b_store[(i, g)] = GD(lambda e, i=i, b=b, g=g: e.dma_start(
                    out=dstv[i, :, g, :], in_=xio[b][:, g * 1024:(g + 1) * 1024]),
                    waits=[(sv, stat_vv or ev), (sa, stat_aa)], sem=(dg2 if g == 0 else dg3)[b])
            po_free_local = ev
            if collect_stats:
                b_stats_a[i] = stat_aa
            return po_free_local

        for i in range(NT):
            if i < 2:
                w_ = [(sv, phase_wait_v), (sa, phase_wait_a)] + list(extra_first)
                w1_ = list(w_)
            else:
                w_ = [(dg2[i % 2], b_store[(i - 2, 0)])]
                w1_ = [(dg3[i % 2], b_store[(i - 2, 1)])]
            x_ld[i] = LD(lambda e, i=i: e.dma_start(out=xio[i % 2][:, 0:1024], in_=srcv[i, :, 0, :]),
                         waits=w_, sem=dx2[i % 2])
            x_ld2[i] = LD(lambda e, i=i: e.dma_start(out=xio[i % 2][:, 1024:2048], in_=srcv[i, :, 1, :]),
                          waits=w1_, sem=dx3[i % 2])
            a_part(i)
            if dbg4 is not None and 2 < i < 6 and os.environ.get("ATH_SNAP"):
                GD(lambda e, i=i: e.dma_start(out=dbg4[:, 260 + 32 * (i - 3):292 + 32 * (i - 3)], in_=statr[:]),
                   waits=[(sv, a_v[i])])
            if i >= 1:
                po_free = b_part(i - 1, a_v[i])
        po_free = b_part(NT - 1, a_v[NT - 1])
        emit_ffn.b_store = dict(b_store)
        emit_ffn.last_pbc = nonloc["pbc"]
        emit_ffn.store_tot = [(dg2[0], dg2[0].v), (dg2[1], dg2[1].v), (dg3[0], dg3[0].v), (dg3[1], dg3[1].v)]
        return sv.v, sa.v


    def emit_attn(src, dst, phase_wait_v, phase_wait_a, store_wait, pbc0):
        srcv = src.rearrange("(g p) d -> p g d", p=128)
        dstv = dst.rearrange("(g p) d -> p g d", p=128)

        wq_b = arh(0, 8192).rearrange("p (c n) -> p c n", n=1024)
        wk_b = arh(4096, 8192).rearrange("p (c n) -> p c n", n=1024)
        wv_b = arh(8192, 8192).rearrange("p (c n) -> p c n", n=1024)
        yT = arh(12288, 16384).rearrange("p (c n) -> p c n", n=2048)
        Kb = arh(20480, 16384).rearrange("p (c n) -> p c n", n=2048)
        Vb = arh(28672, 16640).rearrange("p (c n) -> p c n", n=1040)
        Vb_hk = arh(28672, 16640).rearrange("p (c h k) -> p c h k", h=16, k=65)
        xa = [arf(36992, 1024), arf(38016, 1024)]
        cosb = arh(39040, 1024)
        sinb = arh(39552, 1024)
        bqr = arh(40064, 1024)[0:1, :]
        bvr = arh(40576, 1024)[0:1, :]
        bor = arh(41088, 1024)[0:1, :]
        e16f = arf(41600, 256)[0:1, :]
        selb = arf(41856, 1024)[0:16, :]
        ones128 = arh(42880, 128)[0:1, :]
        ones512 = arh(42944, 512)[0:1, :]
        wo_b = arh(0, 8192).rearrange("p (c n) -> p c n", n=1024)
        Qf = arh(4096, 16384).rearrange("p (c n) -> p c n", n=2048)
        A_sb = arf(12288, 4096)
        Abf = arh(16384, 4096)
        eS = arh(18432, 2048)
        rd_sb = stg[0:16, 0:512]
        selbf = arh(43456, 1024)[0:16, :]
        e16bf = arh(44000, 256)[0:1, :]
        dnbf = arh(44200, 1024)[0:1, :]
        rdbf = arh(44800, 512)[0:16, :]
        stgbf = stg[:].bitcast(BF16)[:, 0:1024]
        cvbbf = cvb_t[:].bitcast(BF16)[:, 0:1024]

        psA = [P[:, 0:512], P[:, 512:1024]]
        psB = [P[:, 1024:1536], P[:, 1536:2048]]
        pv2 = P[:, 1024:2048]
        pav_e = P[:, 2048:2560]
        pav_o = P[:, 2560:3072]
        plv = P[0:16, 3072:3584]
        pR = P[:, 3584:4096]

        pr.op("gpsimd", lambda e: e.nop(),
              waits=[(sv, phase_wait_v), (sa, phase_wait_a), (st_, st_.v), (sp, sp.v),
                     (dsp, dsp.v)])
        wqv = wqd.rearrange("(c p) o -> c p o", p=128)
        wkv = wkd.rearrange("(c p) o -> c p o", p=128)
        wvv = wvd.rearrange("(c p) o -> c p o", p=128)

        def load_qkv(wkv_extra=(), wq_extra=()):
            first = list(wkv_extra)
            for wsrc, wdst in ((wkv, wk_b), (wvv, wv_b)):
                for c in range(8):
                    GD(lambda e, wsrc=wsrc, wdst=wdst, c=c: e.dma_start(
                        out=wdst[:, c, :], in_=wsrc[c, :, :]), waits=first, sem=dsp)
                    first = []
            wkv_done_ = dsp.v
            first = list(wq_extra)
            for c in range(8):
                GD(lambda e, c=c: e.dma_start(
                    out=wq_b[:, c, :], in_=wqv[c, :, :]), waits=first, sem=dsq)
                first = []
            return wkv_done_, dsq.v

        wkv_done, wq_done = load_qkv()
        GD(lambda e: e.dma_start(out=cosb, in_=cosd), sem=dsp)
        GD(lambda e: e.dma_start(out=sinb, in_=sind), sem=dsp)
        wl_done = dsp.v
        wkv_done = wl_done
        bl = LD(lambda e: e.dma_start(out=stg[0:1, 0:1024], in_=bqd), waits=[(sv, phase_wait_v), (sa, phase_wait_a)] + list(store_wait))
        bc = V(lambda e: e.tensor_copy(out=bqr, in_=stg[0:1, 0:1024]), waits=[(dw, bl)], ms=True)
        bl = LD(lambda e: e.dma_start(out=stg[0:1, 0:1024], in_=bvd), waits=[(sv, bc)])
        bc = V(lambda e: e.tensor_copy(out=bvr, in_=stg[0:1, 0:1024]), waits=[(dw, bl)], ms=True)
        bl = LD(lambda e: e.dma_start(out=stg[0:1, 0:1024], in_=bod), waits=[(sv, bc)])
        bc = V(lambda e: e.tensor_copy(out=bor, in_=stg[0:1, 0:1024]), waits=[(dw, bl)], ms=True)
        bl = LD(lambda e: e.dma_start(out=e16f, in_=e16_d), waits=[(sv, bc)])
        bc = V(lambda e: e.tensor_copy(out=e16bf, in_=e16f), waits=[(dw, bl)], ms=True)
        bl = LD(lambda e: e.dma_start(out=selb, in_=sel_d), waits=[(sv, bc)])
        bc = V(lambda e: e.tensor_copy(out=selbf, in_=selb), waits=[(dw, bl)], ms=True)
        V(lambda e: e.memset(ones128, 1.0))
        ones_ms = V(lambda e: e.memset(ones512, 1.0), ms=True)

        pbc = pbc0
        pl_rd_a = 0
        xa_user = {0: ones_ms, 1: ones_ms}
        xa_ld = {0: 0, 1: 0}
        b_st2 = {0: 0, 1: 0}
        eS_rd = {}
        psA_rd = {0: 0, 1: 0}
        psB_rd = {0: 0, 1: 0}
        psB_rd_v = {0: 0, 1: 0}
        pk_free = {}
        stats_ms_a = 0
        kb_dump = {}

        aseq = int(os.environ.get("ATH_ASEQ", "2"))
        aqt = int(os.environ.get("ATH_AQT", "3"))
        for s_ in range(aseq):
            seq0_T_done = st_.v
            if s_ == 1:
                wkv_done, wq_done = load_qkv(
                    wkv_extra=[(st_, kb_dump.get("qf_dead", st_.v))],
                    wq_extra=[(st_, st_.v)])
                wl_done = dsp.v
            st_t = {}
            ev_a = {}
            for st in range(16):
                g = s_ * 16 + st
                par = st % 2
                w_ = [(sv, xa_user[par])]
                if s_ == 0 and st < 2:
                    w_ += list(store_wait)
                if b_st2[par]:
                    w_.append((dg2[par], b_st2[par]))
                xa_ld[par] = LD(lambda e, g=g, par=par: e.dma_start(out=xa[par], in_=srcv[:, g, :]),
                                waits=w_, sem=dx2[par])
                w_ = [(dx2[par], xa_ld[par])]
                if s_ == 0:
                    w_.append((dsp, wl_done))
                elif st == 0:
                    w_.append((sa, pl_rd_a))
                if st >= 1:
                    w_.append((st_, st_t[st - 1]))
                if s_ == 1 and st == 0 and 0 in kb_dump:
                    w_.append((dgd, kb_dump[0]))
                xa_user[par] = V(lambda e, par=par, g=g: e.tensor_scalar(
                    out=stgbf, in0=xa[par], scalar1=statm[:, g:g + 1], scalar2=statr[:, g:g + 1],
                    op0=ALU.subtract, op1=ALU.mult), waits=w_, ms=True)
                for j in range(8):
                    w_ = [(sv, xa_user[par])]
                    if ev_a.get(st - 1):
                        w_.append((sa, ev_a[st - 1]))
                    elif s_ == 1:
                        w_.append((sa, pl_rd_a))
                    st_t[st] = T(lambda e, j=j: e.transpose(
                        out=ptrb[:, 128 * j:128 * (j + 1)],
                        in_=stgbf[:, 128 * j:128 * (j + 1)],
                        identity=identbf[:]), waits=w_ if j == 0 else (), ms=(j == 7))
                dest = yT[:, :, 128 * st:128 * (st + 1)]
                w_ev = [(st_, st_t[st])]
                if s_ == 1 and st == 0:
                    w_ev.append((st_, seq0_T_done))
                ev_a[st] = A_(lambda e, dest=dest: e.copy(out=dest, in_=ptrb[:].rearrange("p (c n) -> p c n", n=128)),
                              waits=w_ev, ms=True)
            yT_a = ev_a[15]
            rope_guard = st_t[15]

            for oc in range(8):
                for tt in range(4):
                    bank = (oc * 4 + tt) % 2
                    w_ = [(sa, yT_a), (sv, pk_free.get(bank, 0)), (sa, psA_rd[bank])]
                    if oc == 0 and tt == 0:
                        w_.append((dsp, wkv_done))
                    for ic in range(8):
                        mmk = T(lambda e, oc=oc, tt=tt, ic=ic, bank=bank: e.matmul(
                            out=psA[bank], lhsT=wk_b[:, ic, 128 * oc:128 * (oc + 1)],
                            rhs=yT[:, ic, 512 * tt:512 * (tt + 1)],
                            start=(ic == 0), stop=(ic == 7), skip_group_check=True),
                            waits=w_ if ic == 0 else (), ms=(ic == 7))
                    pk_free[bank] = V(lambda e, oc=oc, tt=tt, bank=bank: e.tensor_copy(
                        out=Kb[:, oc, 512 * tt:512 * (tt + 1)], in_=psA[bank]),
                        waits=[(st_, mmk)], ms=True)

            for tc in range(16):
                w_ = [(sv, pk_free.get(2, 0)), (sv, psB_rd_v[0]), (sv, psB_rd_v[1]), (sa, psB_rd[0]), (sa, psB_rd[1])]
                for ic in range(8):
                    for n2 in range(2):
                        T(lambda e, tc=tc, ic=ic, n2=n2: e.matmul(
                            out=pv2[:, 512 * n2:512 * (n2 + 1)],
                            lhsT=yT[:, ic, 128 * tc:128 * (tc + 1)],
                            rhs=wv_b[:, ic, 512 * n2:512 * (n2 + 1)],
                            start=(ic == 0), stop=False, skip_group_check=True),
                            waits=w_ if (ic == 0 and n2 == 0) else ())
                for n2 in range(2):
                    mmv = T(lambda e, n2=n2: e.matmul(
                        out=pv2[:, 512 * n2:512 * (n2 + 1)], lhsT=ones128,
                        rhs=bvr[0:1, 512 * n2:512 * (n2 + 1)],
                        start=False, stop=True, skip_group_check=True), ms=(n2 == 1))
                pk_free[2] = V(lambda e, tc=tc: e.tensor_copy(
                    out=Vb_hk[:, tc, :, 0:64], in_=pv2[:].rearrange("p (h k) -> p h k", k=64)),
                    waits=[(st_, mmv)], ms=True)
            vone_ms = V(lambda e: e.memset(Vb_hk[:, :, :, 64:65], 1.0), ms=True)
            psB_rd = {0: 0, 1: 0}
            psB_rd_v = {0: 0, 1: 0}
            qf_guard = 0
            for oc in range(8):
                for tt in range(4):
                    bank = (oc * 4 + tt) % 2
                    w_ = [(sv, pk_free.get(bank, 0)), (sa, psA_rd[bank])]
                    if oc == 0 and tt == 0:
                        w_.append((dsq, wq_done))
                    if oc < 2 and tt == 0:
                        w_.append((sv, vone_ms))
                    for ic in range(8):
                        T(lambda e, oc=oc, tt=tt, ic=ic, bank=bank: e.matmul(
                            out=psA[bank], lhsT=wq_b[:, ic, 128 * oc:128 * (oc + 1)],
                            rhs=yT[:, ic, 512 * tt:512 * (tt + 1)],
                            start=(ic == 0), stop=False, skip_group_check=True),
                            waits=w_ if ic == 0 else ())
                    mmq = T(lambda e, oc=oc, tt=tt, bank=bank: e.matmul(
                        out=psA[bank], lhsT=bqr[0:1, 128 * oc:128 * (oc + 1)], rhs=ones512,
                        start=False, stop=True, skip_group_check=True), ms=True)
                    qf_guard = mmq
                    pk_free[bank] = V(lambda e, oc=oc, tt=tt, bank=bank: e.tensor_copy(
                        out=Qf[:, oc, 512 * tt:512 * (tt + 1)], in_=psA[bank]),
                        waits=[(st_, mmq)], ms=True)
            pr.op("gpsimd", lambda e: e.nop(), waits=[(st_, qf_guard)])
            wov = wod.rearrange("(c p) o -> c p o", p=128)
            for c in range(8):
                GD(lambda e, c=c: e.dma_start(out=wo_b[:, c, :], in_=wov[c, :, :]), sem=dsp)
            wo_done = dsp.v

            def rope(buf_v, oc, col0, width, coff, extra=()):
                sl = lambda a, b: buf_v[a:b, oc, col0:col0 + width]
                V(lambda e: e.tensor_scalar(out=stgbf[0:32, 0:width], in0=sl(32, 64), scalar1=-1.0, scalar2=None, op0=ALU.mult),
                  waits=list(extra))
                V(lambda e: e.tensor_copy(out=stgbf[32:64, 0:width], in_=sl(0, 32)))
                V(lambda e: e.tensor_scalar(out=stgbf[64:96, 0:width], in0=sl(96, 128), scalar1=-1.0, scalar2=None, op0=ALU.mult))
                V(lambda e: e.tensor_copy(out=stgbf[96:128, 0:width], in_=sl(64, 96)))
                V(lambda e: e.tensor_tensor(out=cvbbf[:, 0:width], in0=sl(0, 128), in1=cosb[:, coff:coff + width], op=ALU.mult))
                V(lambda e: e.tensor_tensor(out=stgbf[:, 0:width], in0=stgbf[:, 0:width], in1=sinb[:, coff:coff + width], op=ALU.mult))
                return V(lambda e: e.tensor_tensor(out=sl(0, 128), in0=cvbbf[:, 0:width], in1=stgbf[:, 0:width], op=ALU.add), ms=True)

            rope_ms = 0
            for oc in range(8):
                rope_ms = rope(Kb, oc, 1024, 1024, 0,
                               extra=[(st_, rope_guard)] if oc == 0 else ())
            kbtrace = dbg2 is not None and s_ == 0 and os.environ.get("ATH_KBTRACE")
            if dbg2 is not None and s_ == 0 and not kbtrace:
                kb_dump[0] = GD(lambda e: e.dma_start(out=dbg2[:, 0:8192], in_=arf(20480, 8192)),
                                waits=[(sv, rope_ms)])
            for oc in range(8):
                rope_ms = rope(Qf, oc, 1024, 1024, 0)

            def kbdump(slot, waits):
                kb_dump[0] = GD(lambda e, slot=slot: e.dma_start(
                    out=dbg2[:, 1024 * slot:1024 * (slot + 1)], in_=arf(20480, 1024)), waits=waits)
            if kbtrace:
                kbdump(0, [(sv, rope_ms)])

            A_ev = 0
            dn_ev = 0
            abf_ms = 0
            oproj_t = 0
            for qt in range(aqt + 1):
                q0 = 512 * qt
                qev = rope_ms
                for p in range(8):
                    for cp in range(8):
                        mA = mB = None
                        for cc in range(2):
                            c = 2 * cp + cc
                            w_ = [(sa, psA_rd[0])] if cc == 0 else []
                            if p == 0 and c < 2:
                                w_.append((sv, qev))
                            mA = T(lambda e, p=p, c=c, cc=cc, q0=q0: e.matmul(
                                out=psA[cc], lhsT=Kb[0:64, p, 128 * c:128 * (c + 1)],
                                rhs=Qf[0:64, p, q0:q0 + 512], start=True, stop=True,
                                tile_position=(0, 0), skip_group_check=True),
                                waits=w_, ms=(cc == 1))
                            w_ = [(sa, psB_rd[0])] if cc == 0 else []
                            if p == 0 and c < 2:
                                w_.append((sv, psB_rd_v[cc]))
                            mB = T(lambda e, p=p, c=c, cc=cc, q0=q0: e.matmul(
                                out=psB[cc], lhsT=Kb[64:128, p, 128 * c:128 * (c + 1)],
                                rhs=Qf[64:128, p, q0:q0 + 512], start=True, stop=True,
                                tile_position=(64, 0), skip_group_check=True),
                                waits=w_, ms=(cc == 1))
                        w_ = [(st_, mA)]
                        for h_ in (0, 1):
                            if eS_rd.get((0, h_)):
                                w_.append((st_, eS_rd[(0, h_)]))
                        eA = A_(lambda e: e.activation(
                            out=eS[:, 0:1024], in_=P[:, 0:1024], func=AF.Exp,
                            bias=consts[:, CL_NSH:CL_NSH + 1], scale=0.125),
                            waits=w_, ms=True)
                        psA_rd[0] = psA_rd[1] = eA
                        w_ = [(st_, mB)]
                        for h_ in (0, 1):
                            if eS_rd.get((1, h_)):
                                w_.append((st_, eS_rd[(1, h_)]))
                        eB = A_(lambda e: e.activation(
                            out=eS[:, 1024:2048], in_=P[:, 1024:2048], func=AF.Exp,
                            bias=consts[:, CL_NSH:CL_NSH + 1], scale=0.125),
                            waits=w_, ms=True)
                        psB_rd[0] = psB_rd[1] = eB
                        for cc in range(2):
                            c = 2 * cp + cc
                            w_ = [(sa, eA)] if cc == 0 else []
                            if c == 0:
                                w_.append((sv, A_ev))
                                w_.append((sa, stats_ms_a))
                            avA = T(lambda e, p=p, c=c, cc=cc: e.matmul(
                                out=pav_e[0:65, :], lhsT=Vb[:, c, 1040 * 0 + 65 * (2 * p):65 * (2 * p) + 65],
                                rhs=eS[:, 512 * cc:512 * (cc + 1)],
                                start=(c == 0), stop=(c == 15), skip_group_check=True),
                                waits=w_, ms=(c == 15))
                            w_ = [(sa, eB)] if cc == 0 else []
                            if c == 0:
                                w_.append((sv, A_ev))
                            avB = T(lambda e, p=p, c=c, cc=cc: e.matmul(
                                out=pav_o[0:65, :], lhsT=Vb[:, c, 65 * (2 * p + 1):65 * (2 * p + 1) + 65],
                                rhs=eS[:, 1024 + 512 * cc:1024 + 512 * (cc + 1)],
                                start=(c == 0), stop=(c == 15), skip_group_check=True),
                                waits=w_, ms=(c == 15 or cc == 1))
                            if cc == 1:
                                eS_rd[(0, 0)] = eS_rd[(0, 1)] = avB
                                eS_rd[(1, 0)] = eS_rd[(1, 1)] = avB
                    V(lambda e, p=p: e.tensor_copy(out=A_sb[0:64, 512 * p:512 * (p + 1)], in_=pav_e[0:64, :]),
                      waits=[(st_, avA), (st_, avB), (sa, stats_ms_a)])
                    A_ev = V(lambda e, p=p: e.tensor_copy(out=A_sb[64:128, 512 * p:512 * (p + 1)], in_=pav_o[0:64, :]), ms=True)
                    if kbtrace and qt == 1 and p == 7:
                        kbdump(3, [(sv, A_ev)])
                    V(lambda e: e.tensor_copy(out=dnbf[0:1, 0:512], in_=pav_e[64:65, :]))
                    dn_ev = V(lambda e: e.tensor_copy(out=dnbf[0:1, 512:1024], in_=pav_o[64:65, :]), ms=True)
                    w_ = [(sv, dn_ev)]
                    if p == 0:
                        w_.append((sa, pl_rd_a))
                    T(lambda e, p=p: e.matmul(out=plv, lhsT=e16bf[0:1, 16 * (2 * p):16 * (2 * p) + 16],
                                              rhs=dnbf[0:1, 0:512], start=(p == 0), stop=False, skip_group_check=True),
                      waits=w_)
                    pl_mm = T(lambda e, p=p: e.matmul(out=plv, lhsT=e16bf[0:1, 16 * (2 * p + 1):16 * (2 * p + 1) + 16],
                                                      rhs=dnbf[0:1, 512:1024], start=False, stop=(p == 7), skip_group_check=True), ms=True)
                if qt == 3:
                    kb_dump["qf_dead"] = st_.v
                lnm = A_(lambda e: e.activation(out=rd_sb, in_=plv, func=AF.Ln, bias=0.0, scale=1.0),
                         waits=[(st_, pl_mm)], ms=True)
                lnm = A_(lambda e: e.activation(out=rdbf, in_=rd_sb, func=AF.Exp, bias=0.0, scale=-1.0),
                         waits=[(sa, lnm)], ms=True)
                pl_rd_a = lnm
                for g_ in range(8):
                    mr = T(lambda e, g_=g_: e.matmul(out=pR, lhsT=selbf[:, 128 * g_:128 * (g_ + 1)], rhs=rdbf,
                                                     start=True, stop=True, skip_group_check=True),
                           waits=[(sa, lnm), (sv, pbc)], ms=True)
                    w_ = [(st_, mr)]
                    if oproj_t:
                        w_.append((st_, oproj_t))
                    abf_ms = V(lambda e, g_=g_: e.tensor_tensor(
                        out=Abf[:, 512 * g_:512 * (g_ + 1)], in0=pR[:],
                        in1=A_sb[:, 512 * g_:512 * (g_ + 1)], op=ALU.mult), waits=w_, ms=True)
                    pbc = abf_ms
                if kbtrace:
                    kbdump({0: 1, 1: 4, 2: 6, 3: 7}[qt], [(sv, pbc)])
                for tg in range(4):
                    stc = s_ * 16 + qt * 4 + tg
                    par = tg % 2
                    w_ = [(sv, xa_user[par])]
                    if b_st2[par]:
                        w_.append((dg2[par], b_st2[par]))
                    xa_ld[par] = LD(lambda e, stc=stc, par=par: e.dma_start(out=xa[par], in_=srcv[:, stc, :]),
                                    waits=w_, sem=dx2[par])
                    for n2 in range(2):
                        w_m = [(sv, abf_ms), (sv, psB_rd_v[n2]), (sa, psB_rd[n2]), (dsp, wo_done)]
                        for p in range(8):
                            T(lambda e, tg=tg, n2=n2, p=p: e.matmul(
                                out=psB[n2], lhsT=Abf[:, 512 * p + 128 * tg:512 * p + 128 * (tg + 1)],
                                rhs=wo_b[:, p, 512 * n2:512 * (n2 + 1)],
                                start=(p == 0), stop=False, skip_group_check=True),
                                waits=w_m if p == 0 else ())
                        mo = T(lambda e, n2=n2: e.matmul(out=psB[n2], lhsT=ones128,
                                                         rhs=bor[0:1, 512 * n2:512 * (n2 + 1)],
                                                         start=False, stop=True, skip_group_check=True), ms=True)
                        oproj_t = mo
                        psB_rd[n2] = 0
                        psB_rd_v[n2] = V(lambda e, n2=n2, par=par: e.tensor_tensor(
                            out=xa[par][:, 512 * n2:512 * (n2 + 1)], in0=psB[n2],
                            in1=xa[par][:, 512 * n2:512 * (n2 + 1)], op=ALU.add),
                            waits=[(st_, mo), (dx2[par], xa_ld[par])], ms=True)
                    sv_stat = V(lambda e, stc=stc, par=par: e.reduce_sum(out=ssum[:, stc:stc + 1], in_=xa[par], axis=AX.X), ms=True)
                    xa_user[par] = sv_stat
                    stats_ms_a = A_(lambda e, stc=stc, par=par: e.activation(
                        out=A_sb[:, 0:1024], in_=xa[par], func=AF.Square, accum_out=ssq[:, stc:stc + 1]),
                        waits=[(sv, sv_stat)], ms=True)
                    b_st2[par] = GD(lambda e, stc=stc, par=par: e.dma_start(out=dstv[:, stc, :], in_=xa[par]),
                                    waits=[(sv, sv_stat), (sa, stats_ms_a)], sem=dg2[par])
                if kbtrace and qt < 2:
                    kbdump({0: 2, 1: 5}[qt], [(sa, stats_ms_a)])
        emit_attn.qf_dead = kb_dump.get("qf_dead", st_.v)
        return sv.v, sa.v, pbc

    if DEV_PHASES == "ffn1":
        fv, fa = emit_ffn(x0, out_d, collect_stats=False,
                          phase_wait_v=fin_v, phase_wait_a=fin_a,
                          mm0_wait=[(sv, wq_done_v)])
    elif DEV_PHASES == "attn":
        fv, fa = emit_ffn(x0, x1d, collect_stats=True,
                          phase_wait_v=fin_v, phase_wait_a=fin_a,
                          mm0_wait=[(sv, wq_done_v)])
        f1v, f1a = finalize_stats(fv, fa)
        av, aa, apbc = emit_attn(x1d, out_d, f1v, f1a,
                                 store_wait=emit_ffn.store_tot, pbc0=emit_ffn.last_pbc)
    else:
        fv, fa = emit_ffn(x0, x1d, collect_stats=True,
                          phase_wait_v=fin_v, phase_wait_a=fin_a,
                          mm0_wait=[(sv, wq_done_v)])
        f1v, f1a = finalize_stats(fv, fa)
        att_dst = out_d if DEV_PHASES == "attn2" else x2d
        av, aa, apbc = emit_attn(x1d, att_dst, f1v, f1a,
                                 store_wait=emit_ffn.store_tot, pbc0=emit_ffn.last_pbc)
        POST = os.environ.get("ATH_POST", "fin,rl,pre")
        f2v, f2a = (av, aa)
        if "fin" in POST:
            f2v, f2a = finalize_stats(av, aa, nsteps=int(os.environ.get("ATH_FSTEPS", "9")))
        rl0 = dw.v
        if "rl" in POST:
            early = [(st_, emit_attn.qf_dead), (dsp, dsp.v)]
            for j in (2, 3, 4, 5):
                LD(lambda e, j=j: e.dma_start(out=w1q[:, j * 4096:(j + 1) * 4096], in_=w1qs_v[j, :, :]),
                   waits=early)
                early = []
            for j in range(4, 32):
                LD(lambda e, j=j: e.dma_start(out=w2q[:, j * 1024:(j + 1) * 1024], in_=w2qs_v[j, :, :]))
            late = [(sv, av), (sa, aa)]
            for j in (0, 1, 6, 7):
                LD(lambda e, j=j: e.dma_start(out=w1q[:, j * 4096:(j + 1) * 4096], in_=w1qs_v[j, :, :]),
                   waits=late)
                late = []
            for j in range(4):
                LD(lambda e, j=j: e.dma_start(out=w2q[:, j * 1024:(j + 1) * 1024], in_=w2qs_v[j, :, :]))
        rl_done = dw.v
        if "pre" in POST and os.environ.get("ATH_PRE"):
            A_(lambda e: e.preload_act_table(AF.Silu), waits=[(sa, f2a)])
        if DEV_PHASES != "attn2":
            fv, fa = emit_ffn(x2d, out_d, collect_stats=False,
                              phase_wait_v=f2v, phase_wait_a=f2a,
                              extra_first=[(dg2[0], dg2[0].v), (dg2[1], dg2[1].v),
                                           (dg3[0], dg3[0].v), (dg3[1], dg3[1].v)],
                              pbc0=apbc, mm0_wait=[(dw, rl_done)])

    if dbg1 is not None:
        bs = getattr(emit_ffn, "b_store", {})
        for k, col in ([] if os.environ.get("ATH_SNAP") else [(0, 0), (3, 32), (7, 64)]):
            if (k, 1) in bs:
                GD(lambda e, col=col: e.dma_start(out=dbg4[:, 260 + col:260 + col + 32], in_=statr[:]),
                   waits=[(dg3[k % 2], bs[(k, 1)])])
        GD(lambda e: e.dma_start(out=dbg1[:, 0:4], in_=wsc[:]), waits=[(sv, sv.v), (sa, sa.v), (sp, sp.v), (st_, st_.v)])
        GD(lambda e: e.dma_start(out=dbg1[:, 4:36], in_=statm[:]))
        GD(lambda e: e.dma_start(out=dbg1[:, 36:68], in_=statr[:]))
        GD(lambda e: e.dma_start(out=dbg1[:, 68:84], in_=tmp[:]))
        GD(lambda e: e.dma_start(out=dbg1[:, 84:92], in_=consts[:]))
        GD(lambda e: e.dma_start(out=dbg1[:, 124:156], in_=ssq[:]))
        GD(lambda e: e.dma_start(out=dbg4[:, 0:64], in_=guard1[:]))
        GD(lambda e: e.dma_start(out=dbg4[:, 64:128], in_=guard2[:]))
        GD(lambda e: e.dma_start(out=dbg4[:, 128:144], in_=tmp[:]))
        GD(lambda e: e.dma_start(out=dbg4[:, 144:148], in_=wsc[:]))
        GD(lambda e: e.dma_start(out=dbg4[:, 148:156], in_=consts[:]))
        GD(lambda e: e.dma_start(out=dbg4[:, 156:188], in_=statm[:]))
        GD(lambda e: e.dma_start(out=dbg2[:, :], in_=hbuf))

        if not os.environ.get("ATH_KBTRACE"):
            GD(lambda e: e.dma_start(out=dbg3[:, :], in_=yqT[:]))

    pr.op("gpsimd", lambda e: e.nop(),
          waits=[(dgd, dgd.v), (dsp, dsp.v), (dsq, dsq.v), (dg2[0], dg2[0].v), (dg2[1], dg2[1].v),
                 (dg3[0], dg3[0].v), (dg3[1], dg3[1].v),
                 (dx2[0], dx2[0].v), (dx2[1], dx2[1].v),
                 (dx3[0], dx3[0].v), (dx3[1], dx3[1].v), (dw, dw.v),
                 (dw2[0], dw2[0].v), (dw2[1], dw2[1].v)])

    if os.environ.get("ATH_TAGDUMP"):
        with open("_tagdump.txt", "w") as f:
            for eng in ENGINES:
                for i, (fn, ws, inc, incv) in enumerate(pr.q[eng]):
                    wstr = ",".join(f"{s.name}>={v}" for s, v in ws)
                    istr = f"+{inc.name}" if inc is not None else ""
                    f.write(f"{eng} {i} [{wstr}] {istr}\n")

    block = stack.enter_context(nc.Block())

    namelog = open("_instmap.txt", "w") if os.environ.get("ATH_TAGDUMP") else None

    def runner(ops, ename=""):
        def run(eng):
            hwm = {}
            for i, (fn, waits, inc, incv) in enumerate(ops):
                for s, v in waits:
                    if hwm.get(id(s), 0) >= v:
                        continue
                    eng.wait_ge(s.h, v)
                    hwm[id(s)] = v
                instr = fn(eng)
                if inc is not None:
                    instr.then_inc(inc.h, incv)
                if namelog is not None:
                    namelog.write(f"{ename} {i} {getattr(instr, 'name', '?')}\n")
        return run

    block.sync(runner(pr.q["sync"], "sync"))
    block.vector(runner(pr.q["vector"], "vector"))
    block.scalar(runner(pr.q["scalar"], "scalar"))
    block.tensor(runner(pr.q["tensor"], "tensor"))
    block.gpsimd(runner(pr.q["gpsimd"], "gpsimd"))
    if namelog is not None:
        namelog.close()

    stack.close()
    return nc



def kernel(**inputs):
    x = np.asarray(inputs["L0_attn_hidden_states"])
    cos = np.asarray(inputs["L0_attn_cos"])
    sin = np.asarray(inputs["L0_attn_sin"])
    wq = np.asarray(inputs["L0_attn_wq"])
    bq = np.asarray(inputs["L0_attn_bq"])
    wk = np.asarray(inputs["L0_attn_wk"])
    wv = np.asarray(inputs["L0_attn_wv"])
    bv = np.asarray(inputs["L0_attn_bv"])
    wo = np.asarray(inputs["L0_attn_wo"])
    bo = np.asarray(inputs["L0_attn_bo"])
    w1 = np.asarray(inputs["L0_ffn_w1"])
    cwv = np.asarray(inputs["L0_ffn_conv_w"])
    cb = np.asarray(inputs["L0_ffn_conv_b"])
    w2 = np.asarray(inputs["L0_ffn_w2"])

    w1t = np.ascontiguousarray(w1.T)
    w2t = np.ascontiguousarray(w2.T)
    cwm = np.zeros((128, 128), dtype=np.float32)
    taps = cwv[:, 0, :]
    for tap in range(3):
        cwm[:, 32 * tap:32 * tap + 32] = taps[:, tap].reshape(32, 128).T
    cwm[:, 96:128] = cb.reshape(32, 128).T
    cost = np.ascontiguousarray(np.tile(cos.T, (2, 1)).astype(np.float32, copy=False))
    sint = np.ascontiguousarray(np.tile(sin.T, (2, 1)).astype(np.float32, copy=False))

    identf = np.eye(128, dtype=np.float32)
    onesf = np.ones((128, 128), dtype=np.float32)
    e16 = np.zeros((1, 256), dtype=np.float32)
    for h in range(16):
        e16[0, 16 * h + h] = 1
    selm = np.zeros((16, 1024), dtype=np.float32)
    for g in range(8):
        selm[2 * g, 128 * g:128 * g + 64] = 1
        selm[2 * g + 1, 128 * g + 64:128 * g + 128] = 1
    constsv = np.zeros((128, 8), dtype=np.float32)
    constsv[:, 0] = 1e-5
    constsv[:, 1] = 1.0 / 127.0
    constsv[:, 2] = -C_SHIFT
    constsv[:, 3] = EPS
    constsv[:, 4] = 0.5

    nc = build_nc()
    shared = dict(w1t=w1t, w2t=w2t, cwdrm=cwm, wq=wq, wk=wk, wv=wv, wo=wo,
                  bq=bq.reshape(1, -1), bv=bv.reshape(1, -1), bo=bo.reshape(1, -1),
                  cosd=cost, sind=sint, identfd=identf,
                  onesfd=onesf, e16d=e16, seld=selm, constsd=constsv)
    in_maps = []
    for c in range(NCORE):
        m = dict(shared)
        m["x"] = np.ascontiguousarray(x[2 * c:2 * c + 2].reshape(NTOK, D))
        in_maps.append(m)

    res = run_bass_kernel_spmd(nc, in_maps, core_ids=list(range(NCORE)))
    if os.environ.get("ATH_DEBUG"):
        kernel._dbg = res.results
    out = np.stack([np.asarray(r["out"]) for r in res.results])
    return np.ascontiguousarray(out.reshape(16, 2048, 1024).astype(np.float32, copy=False))
